# version 12
# baseline (speedup 1.0000x reference)
import contextlib
import numpy as np
import concourse.bass as bass
import concourse.mybir as mybir

F32 = mybir.dt.float32
BF16 = mybir.dt.bfloat16
I32 = mybir.dt.int32
AF = mybir.ActivationFunctionType
ALU = mybir.AluOpType
AX = mybir.AxisListType

ENGINES = {"t": "tensor", "v": "vector", "s": "scalar", "p": "gpsimd", "q": "sync"}


def _base(x):
    if hasattr(x, "tensor"):
        return x.tensor.name
    if hasattr(x, "name"):
        return x.name
    return str(x)


def _key(x):
    if isinstance(x, tuple):
        return (_base(x[0]), str(x[1]))
    return (_base(x), "")


class PB:
    def __init__(self, nc: bass.Bass):
        self.nc = nc
        self._stack = contextlib.ExitStack()
        self.sem = {}
        self.cnt = {}
        self.waited = {}
        self.reg = {}
        self.sb_bytes = 0
        self.n_ops = 0
        for k in ENGINES:
            self.sem[k] = self._stack.enter_context(nc.semaphore("sem_" + k))
            self.cnt[k] = 0

    def sb(self, name, shape, dtype):
        st = self._scopes[-1] if getattr(self, "_scopes", None) else self._stack
        t = st.enter_context(self.nc.sbuf_tensor(name, list(shape), dtype))
        self.sb_bytes += int(np.prod(shape[1:])) * np.dtype(mybir.dt.np(dtype)).itemsize
        return t

    def ps(self, name, shape, dtype=F32):
        st = self._scopes[-1] if getattr(self, "_scopes", None) else self._stack
        return st.enter_context(self.nc.psum_tensor(name, list(shape), dtype))

    @contextlib.contextmanager
    def scope(self):
        if not hasattr(self, "_scopes"):
            self._scopes = []
        st = contextlib.ExitStack()
        self._scopes.append(st)
        try:
            yield
        finally:
            self._scopes.pop()
            st.close()
            self.barrier()

    def barrier(self):
        need = {k: v for k, v in self.cnt.items() if v > 0}
        for eng in ENGINES:
            self._wait(eng, dict(need))
        self.reg.clear()

    def dram(self, name, shape, dtype, kind="Internal"):
        return self.nc.dram_tensor(name, list(shape), dtype, kind=kind).ap()

    def close(self):
        self._stack.close()

    def _group_sem(self, group):
        k = "g:" + group
        if k not in self.sem:
            self.sem[k] = self._stack.enter_context(self.nc.semaphore("sg_" + group))
            self.cnt[k] = 0
        return k

    def _entries(self, tensor, sub, for_write):
        d = self.reg.get(tensor)
        if d is None:
            return []
        if sub == "":
            return list(d.values())
        out = []
        if "" in d:
            out.append(d[""])
        if sub in d:
            out.append(d[sub])
        return out

    def _wait(self, eng, need):
        e = getattr(self.nc, ENGINES[eng])
        for semkey, val in sorted(need.items()):
            if val <= 0:
                continue
            if semkey.startswith("g:"):
                val = self.cnt[semkey]
            if self.waited.get((eng, semkey), -1) >= val:
                continue
            e.wait_ge(self.sem[semkey], val)
            self.waited[(eng, semkey)] = val

    def op(self, eng, fn, reads=(), writes=(), dma_group=None):
        reads = [_key(x) for x in reads]
        writes = [_key(x) for x in writes]
        need = {}

        def add(sv):
            if sv and need.get(sv[0], -1) < sv[1]:
                need[sv[0]] = sv[1]

        for t, s in reads:
            for ent in self._entries(t, s, False):
                add(ent.get("w"))
        for t, s in writes:
            for ent in self._entries(t, s, True):
                add(ent.get("w"))
                for sk, v in ent.get("r", {}).items():
                    add((sk, v))
        self._wait(eng, need)
        inst = fn(getattr(self.nc, ENGINES[eng]))
        if dma_group is not None:
            semkey = self._group_sem(dma_group)
            inc = 16
        else:
            semkey = eng
            inc = 1
        self.cnt[semkey] += inc
        inst.then_inc(self.sem[semkey], inc)
        v = self.cnt[semkey]
        self.n_ops += 1
        for t, s in writes:
            d = self.reg.setdefault(t, {})
            if s == "":
                d.clear()
            ent = d.setdefault(s, {})
            ent["w"] = (semkey, v)
            ent["r"] = {}
        for t, s in reads:
            ent = self.reg.setdefault(t, {}).setdefault(s, {})
            r = ent.setdefault("r", {})
            if r.get(semkey, -1) < v:
                r[semkey] = v
        return inst

    def dma(self, eng, out, in_, group=None, reads=(), writes=(), **kw):
        g = group or ("b_" + _base(out))
        return self.op(eng, lambda e: e.dma_start(out=out, in_=in_, **kw),
                       reads=list(reads) or [in_], writes=list(writes) or [out],
                       dma_group=g)

    def dmat(self, eng, out, in_, group=None, reads=(), writes=()):
        g = group or ("b_" + _base(out))
        return self.op(eng, lambda e: e.dma_start_transpose(out=out, in_=in_),
                       reads=list(reads) or [in_], writes=list(writes) or [out],
                       dma_group=g)

    def cpp(self, val, dtype=F32):
        if not hasattr(self, "_CPP"):
            self._CPP = {}
        key = (float(val), dtype)
        if key not in self._CPP:
            t = self._stack.enter_context(
                self.nc.sbuf_tensor(f"cpp_{len(self._CPP)}", [128, 1], dtype))
            self.op("v", lambda e: e.memset(t[:], float(val)), writes=[t])
            self._CPP[key] = t
        return self._CPP[key][:]

    def finish(self):
        need = {k: v for k, v in self.cnt.items() if v > 0}
        self._wait("q", need)


def simrun(nc, inputs, outs):
    from concourse.bass_interp import CoreSim
    nc.finalize()
    sim = CoreSim(nc)
    for k, v in inputs.items():
        sim.tensor(k)[:] = v
    sim.simulate(check_with_hw=False)
    return {k: np.array(sim.mem_tensor(k)) for k in outs}


"""Emitters for the mk_recurrent_d2 megakernel. Per-core program pieces.

Per core: 2 sequences x 2048 tokens x 1024 features.
Layouts: TM = [token part, feat free]; FM = [feat part, token free].
All matmuls bf16 operands / fp32 PSUM accumulation; elementwise fp32.
"""
import numpy as np
import ml_dtypes

D = 1024
T = 2048
NSEQ = 2
NTOK = 4096
HID = 4096
NH = 16
DH = 64
EPS = 1e-5
LN2 = float(np.log(2.0))
BIGNEG8 = -8.0e33
DC = D // 128
TC = NTOK // 128
HC = HID // 128
SCALE_ATT = 0.125

BF = ml_dtypes.bfloat16


def host_prep_shared(inp):
    c = lambda a: np.ascontiguousarray(a)
    d = {}
    d["wz"] = c(inp["L0_attn_Wz"])
    d["wh"] = c(inp["L0_attn_Wh"])
    d["bz_t"] = c(np.asarray(inp["L0_attn_bz"]).reshape(DC, 128).T)
    d["bh_t"] = c(np.asarray(inp["L0_attn_bh"]).reshape(DC, 128).T)
    d["gamma_r"] = c(np.asarray(inp["L0_attn_gamma"]).reshape(1, D))
    d["beta_r"] = c(np.asarray(inp["L0_attn_beta"]).reshape(1, D))
    d["wfc"] = c(inp["L0_ffn_w_fc"])
    d["bfc_r"] = c(np.asarray(inp["L0_ffn_b_fc"]).reshape(1, HID))
    d["wproj"] = c(inp["L0_ffn_w_proj"])
    d["bproj_r"] = c(np.asarray(inp["L0_ffn_b_proj"]).reshape(1, D))
    for nm, key in [("wqr", "L1_attn_Wq"), ("wkr", "L1_attn_Wk"), ("wvr", "L1_attn_Wv")]:
        d[nm] = c(np.asarray(inp[key]).transpose(1, 0, 2).reshape(D, NH * DH))
    d["wp"] = c(inp["L1_attn_Wp"])
    d["bp_r"] = c(np.asarray(inp["L1_attn_bp"]).reshape(1, D))
    rt = np.asarray(inp["L1_attn_rel_table"])
    d["rt_row"] = c(rt.T.reshape(1, NH * 33))
    i_ = np.arange(128)[:, None]
    j_ = np.arange(128)[None, :]
    bucket_diag = np.clip(i_ - j_ + 16, 0, 32)
    bucket_sub = np.maximum(i_ - j_ - 112, 0)
    bt = np.empty((128, NH * 2, 128), np.float32)
    for h in range(NH):
        t0 = rt[bucket_diag, h]
        bt[:, 2 * h, :] = np.where(i_ <= j_, t0, np.float32(-1e30))
        bt[:, 2 * h + 1, :] = rt[bucket_sub, h]
    d["btiles"] = c(bt)
    def tile_T(w, mchunks, kchunks):
        wt = np.ascontiguousarray(w.T)
        K, O = wt.shape
        return c(wt.reshape(kchunks, 128, mchunks, 128).transpose(2, 0, 1, 3))
    d["w1t"] = tile_T(np.asarray(inp["L1_ffn_w1_w"]), HC, DC)
    d["w2t"] = tile_T(np.asarray(inp["L1_ffn_w2_w"]), HC, DC)
    d["h1t"] = tile_T(np.asarray(inp["L1_ffn_h1_w"]), HC, HC)
    d["h2t"] = tile_T(np.asarray(inp["L1_ffn_h2_w"]), HC, HC)
    w3t = np.ascontiguousarray(np.asarray(inp["L1_ffn_w3_w"]).T)
    d["w3t"] = c(w3t.reshape(HC, 128, D))
    for nm, key in [("s1c", "L1_ffn_w1_s"), ("s2c", "L1_ffn_w2_s"),
                    ("sh1c", "L1_ffn_h1_s"), ("sh2c", "L1_ffn_h2_s")]:
        d[nm] = c(np.asarray(inp[key]).reshape(HC, 128).T)
    d["s3_r"] = c(np.asarray(inp["L1_ffn_w3_s"]).reshape(1, D))
    d["ident_f"] = np.eye(128, dtype=np.float32)
    d["ident_b"] = np.eye(128, dtype=BF)
    return d


def host_prep_core(inp, core):
    x = np.asarray(inp["L0_attn_x"])[2 * core:2 * core + 2]
    return {"x": np.ascontiguousarray(x.reshape(NTOK, D))}


SHARED_SPECS = {
    "wz": ([D, D], F32), "wh": ([D, D], F32),
    "bz_t": ([128, DC], F32), "bh_t": ([128, DC], F32),
    "gamma_r": ([1, D], F32), "beta_r": ([1, D], F32),
    "wfc": ([HID, D], F32), "bfc_r": ([1, HID], F32),
    "wproj": ([D, HID], F32), "bproj_r": ([1, D], F32),
    "wqr": ([D, D], F32), "wkr": ([D, D], F32), "wvr": ([D, D], F32),
    "wp": ([D, D], F32), "bp_r": ([1, D], F32),
    "rt_row": ([1, NH * 33], F32),
    "btiles": ([128, NH * 2, 128], F32),
    "w1t": ([HC, DC, 128, 128], F32), "w2t": ([HC, DC, 128, 128], F32),
    "h1t": ([HC, HC, 128, 128], F32), "h2t": ([HC, HC, 128, 128], F32),
    "w3t": ([HC, 128, D], F32),
    "s1c": ([128, HC], F32), "s2c": ([128, HC], F32),
    "sh1c": ([128, HC], F32), "sh2c": ([128, HC], F32),
    "s3_r": ([1, D], F32),
    "ident_f": ([128, 128], F32), "ident_b": ([128, 128], BF16),
}


def bcast_row(pb, c, name, row_ap, width, dtype=BF16, row_reads=()):
    ones = c["_ones1_f32"]
    if row_ap.tensor.__class__.__name__.startswith("DRam") or "DRAM" in str(getattr(row_ap, "space", "")):
        srow = pb.sb(name + "_row", [1, width], F32)
        pb.dma("q", srow[:], row_ap, group=name + "_row")
        row_ap = srow[:]
        row_reads = [srow]
    dst = pb.sb(name, [128, width], dtype)
    psb = c.get("_psb")
    if psb is None:
        psb = pb.ps("ps_bcast", [128, 512], F32)
        c["_psb"] = psb
    pkey = psb if not isinstance(psb, tuple) else psb
    for j in range(0, width, 512):
        w = min(512, width - j)
        pb.op("t", lambda e, j=j, w=w, row_ap=row_ap: e.matmul(
            psb[:, 0:w] if not isinstance(psb, tuple) else None,
            ones[:], row_ap[:, j:j + w], start=True, stop=True),
              reads=[ones] + list(row_reads), writes=[psb])
        pb.op("v", lambda e, j=j, w=w: e.tensor_copy(out=dst[:, j:j + w], in_=psb[:, 0:w]),
              reads=[psb], writes=[(dst, j)])
    return dst


def ln_stats(pb, c, sx_ap, w, pref, reads):
    if pref not in c:
        c[pref] = dict(
            sum=pb.sb(pref + "_sum", [128, 1], F32),
            ssq=pb.sb(pref + "_ssq", [128, 1], F32),
            sqs=pb.sb(pref + "_sqs", [128, w], F32),
            mean=pb.sb(pref + "_mean", [128, 1], F32),
            m2=pb.sb(pref + "_m2", [128, 1], F32),
            var=pb.sb(pref + "_var", [128, 1], F32),
            lv=pb.sb(pref + "_lv", [128, 1], F32),
            rstd=pb.sb(pref + "_rstd", [128, 1], F32),
            nmr=pb.sb(pref + "_nmr", [128, 1], F32),
        )
    t = c[pref]
    pb.op("v", lambda e: e.reduce_sum(out=t["sum"][:], in_=sx_ap, axis=AX.X),
          reads=reads, writes=[t["sum"]])
    pb.op("s", lambda e: e.activation(out=t["sqs"][:, 0:w], in_=sx_ap, func=AF.Square,
                                      accum_out=t["ssq"][:]),
          reads=reads, writes=[t["sqs"], t["ssq"]])
    pb.op("v", lambda e: e.tensor_scalar_mul(out=t["mean"][:], in0=t["sum"][:], scalar1=1.0 / w),
          reads=[t["sum"]], writes=[t["mean"]])
    pb.op("v", lambda e: e.tensor_tensor(out=t["m2"][:], in0=t["mean"][:], in1=t["mean"][:],
                                         op=ALU.mult),
          reads=[t["mean"]], writes=[t["m2"]])
    pb.op("v", lambda e: e.scalar_tensor_tensor(out=t["var"][:], in0=t["ssq"][:],
                                                scalar=1.0 / w, in1=t["m2"][:],
                                                op0=ALU.mult, op1=ALU.subtract),
          reads=[t["ssq"], t["m2"]], writes=[t["var"]])
    beps = pb.cpp(EPS)
    pb.op("s", lambda e: e.activation(out=t["lv"][:], in_=t["var"][:], func=AF.Ln, bias=beps),
          reads=[t["var"], beps], writes=[t["lv"]])
    pb.op("s", lambda e: e.activation(out=t["rstd"][:], in_=t["lv"][:], func=AF.Exp, scale=-0.5),
          reads=[t["lv"]], writes=[t["rstd"]])
    pb.op("v", lambda e: e.scalar_tensor_tensor(out=t["nmr"][:], in0=t["mean"][:], scalar=-1.0,
                                                in1=t["rstd"][:], op0=ALU.mult, op1=ALU.mult),
          reads=[t["mean"], t["rstd"]], writes=[t["nmr"]])
    return t


def quantize_tile(pb, c, sx_ap, w, out_bf_ap, pref, reads, out_writes):
    nb = w // 32
    if pref not in c:
        c[pref] = dict(
            am=pb.sb(pref + "_am", [128, nb, 1], F32),
            ef=pb.sb(pref + "_ef", [128, nb, 1], F32),
            ei=pb.sb(pref + "_ei", [128, nb, 1], I32),
            el=pb.sb(pref + "_el", [128, nb, 1], F32),
            inv=pb.sb(pref + "_inv", [128, nb, 1], F32),
            scl=pb.sb(pref + "_scl", [128, nb, 1], F32),
            y=pb.sb(pref + "_y", [128, w], F32),
            qi=pb.sb(pref + "_qi", [128, w], I32),
            r=pb.sb(pref + "_r", [128, w], F32),
        )
    t = c[pref]
    v3 = sx_ap.rearrange("p (b w) -> p b w", w=32)
    pb.op("v", lambda e: e.tensor_reduce(out=t["am"][:, 0:nb], in_=v3, axis=AX.X, op=ALU.max,
                                         apply_absolute_value=True),
          reads=reads, writes=[t["am"]])
    pb.op("v", lambda e: e.tensor_scalar_max(out=t["am"][:, 0:nb], in0=t["am"][:, 0:nb],
                                             scalar1=float(2.0 ** -63)),
          reads=[t["am"]], writes=[t["am"]])
    pb.op("s", lambda e: e.activation(out=t["el"][:, 0:nb], in_=t["am"][:, 0:nb], func=AF.Ln,
                                      scale=1.0),
          reads=[t["am"]], writes=[t["el"]])
    pb.op("v", lambda e: e.tensor_scalar_mul(out=t["el"][:, 0:nb], in0=t["el"][:, 0:nb],
                                             scalar1=float(1.0 / LN2)),
          reads=[t["el"]], writes=[t["el"]])
    pb.op("v", lambda e: e.tensor_copy(out=t["ei"][:, 0:nb], in_=t["el"][:, 0:nb]),
          reads=[t["el"]], writes=[t["ei"]])
    pb.op("v", lambda e: e.tensor_copy(out=t["ef"][:, 0:nb], in_=t["ei"][:, 0:nb]),
          reads=[t["ei"]], writes=[t["ef"]])
    pb.op("v", lambda e: e.tensor_tensor(out=t["el"][:, 0:nb], in0=t["el"][:, 0:nb],
                                         in1=t["ef"][:, 0:nb], op=ALU.is_lt),
          reads=[t["el"], t["ef"]], writes=[t["el"]])
    pb.op("v", lambda e: e.tensor_tensor(out=t["ef"][:, 0:nb], in0=t["ef"][:, 0:nb],
                                         in1=t["el"][:, 0:nb], op=ALU.subtract),
          reads=[t["ef"], t["el"]], writes=[t["ef"]])
    b3p = pb.cpp(3 * LN2)
    b3n = pb.cpp(-3 * LN2)
    pb.op("s", lambda e: e.activation(out=t["inv"][:, 0:nb], in_=t["ef"][:, 0:nb], func=AF.Exp,
                                      scale=-LN2, bias=b3p),
          reads=[t["ef"], b3p], writes=[t["inv"]])
    pb.op("s", lambda e: e.activation(out=t["scl"][:, 0:nb], in_=t["ef"][:, 0:nb], func=AF.Exp,
                                      scale=LN2, bias=b3n),
          reads=[t["ef"], b3n], writes=[t["scl"]])
    pb.op("v", lambda e: e.tensor_tensor(out=t["y"][:, 0:w].rearrange("p (b w) -> p b w", w=32),
                                         in0=v3, in1=t["inv"][:, 0:nb].broadcast_to([128, nb, 32]),
                                         op=ALU.mult),
          reads=reads + [t["inv"]], writes=[t["y"]])
    pb.op("v", lambda e: e.tensor_copy(out=t["qi"][:, 0:w], in_=t["y"][:, 0:w]),
          reads=[t["y"]], writes=[t["qi"]])
    pb.op("v", lambda e: e.tensor_copy(out=t["r"][:, 0:w], in_=t["qi"][:, 0:w]),
          reads=[t["qi"]], writes=[t["r"]])
    pb.op("v", lambda e: e.tensor_tensor(out=t["y"][:, 0:w], in0=t["y"][:, 0:w],
                                         in1=t["r"][:, 0:w], op=ALU.subtract),
          reads=[t["y"], t["r"]], writes=[t["y"]])
    pb.op("v", lambda e: e.scalar_tensor_tensor(out=t["r"][:, 0:w], in0=t["y"][:, 0:w],
                                                scalar=0.5, in1=t["r"][:, 0:w],
                                                op0=ALU.is_gt, op1=ALU.add),
          reads=[t["y"], t["r"]], writes=[t["r"]])
    pb.op("v", lambda e: e.scalar_tensor_tensor(out=t["r"][:, 0:w], in0=t["y"][:, 0:w],
                                                scalar=-0.5, in1=t["r"][:, 0:w],
                                                op0=ALU.is_ge, op1=ALU.add),
          reads=[t["y"], t["r"]], writes=[t["r"]])
    pb.op("v", lambda e: e.tensor_scalar_add(out=t["r"][:, 0:w], in0=t["r"][:, 0:w],
                                             scalar1=-1.0),
          reads=[t["r"]], writes=[t["r"]])
    pb.op("v", lambda e: e.tensor_tensor(out=out_bf_ap.rearrange("p (b w) -> p b w", w=32),
                                         in0=t["r"][:, 0:w].rearrange("p (b w) -> p b w", w=32),
                                         in1=t["scl"][:, 0:nb].broadcast_to([128, nb, 32]),
                                         op=ALU.mult),
          reads=[t["r"], t["scl"]], writes=out_writes)


def emit_premix(pb, c):
    io = c["io"]
    with pb.scope():
        sx = [pb.sb(f"pm_sx{i}", [128, 1024], F32) for i in range(2)]
        sxn = [pb.sb(f"pm_sxn{i}", [128, 1024], F32) for i in range(2)]
        sxnb = [pb.sb(f"pm_sxnb{i}", [128, 1024], BF16) for i in range(2)]
        sqb = [pb.sb(f"pm_sqb{i}", [128, 1024], BF16) for i in range(2)]

        def pre_tile(i):
            b = i % 2
            sq = i // 16
            pb.dma("q", sx[b][:], io["x"][i * 128:(i + 1) * 128, :], group=f"pm_x{b}")
            t = ln_stats(pb, c, sx[b][:], D, f"pm_ln{b}", [sx[b]])
            pb.op("s", lambda e, b=b, t=t: e.activation(out=sxn[b][:], in_=sx[b][:],
                                                        func=AF.Identity,
                                                        scale=t["rstd"][:], bias=t["nmr"][:]),
                  reads=[sx[b], t["rstd"], t["nmr"]], writes=[sxn[b]])
            pb.op("s", lambda e, b=b: e.activation(out=sxnb[b][:], in_=sxn[b][:], func=AF.Copy),
                  reads=[sxn[b]], writes=[sxnb[b]])
            pb.dma("p", io["xn_f32"][i * 128:(i + 1) * 128, :], sxn[b][:], group="xn_f32",
                   reads=[sxn[b]], writes=[(io["xn_f32"], sq)])
            pb.dma("p", io["xn_bf"][i * 128:(i + 1) * 128, :], sxnb[b][:], group="xn_bf",
                   reads=[sxnb[b]], writes=[(io["xn_bf"], sq)])
            quantize_tile(pb, c, sxn[b][:], D, sqb[b][:], f"pm_q{b}", [sxn[b]], [sqb[b]])
            pb.dma("p", io["xq_bf"][i * 128:(i + 1) * 128, :], sqb[b][:], group="xq_bf",
                   reads=[sqb[b]], writes=[(io["xq_bf"], sq)])

        def wfc_tile(j):
            b = j % 2
            pb.dma("q", sx[b][:], io["wfc"][j * 128:(j + 1) * 128, :], group=f"pm_x{b}")
            quantize_tile(pb, c, sx[b][:], D, sqb[b][:], f"pm_q{b}", [sx[b]], [sqb[b]])
            pb.dma("p", io["wqfc_bf"][j * 128:(j + 1) * 128, :], sqb[b][:], group="wqfc_bf")

        def wproj_tile(j):
            r, qq = j // 4, j % 4
            b = j % 2
            pb.dma("q", sx[b][:], io["wproj"][r * 128:(r + 1) * 128,
                                              qq * 1024:(qq + 1) * 1024], group=f"pm_x{b}")
            quantize_tile(pb, c, sx[b][:], D, sqb[b][:], f"pm_q{b}", [sx[b]], [sqb[b]])
            pb.dma("p", io["wqproj_bf"][r * 128:(r + 1) * 128, qq * 1024:(qq + 1) * 1024],
                   sqb[b][:], group="wqproj_bf")

        psA = pb.ps("mx_psA", [128, T], F32)
        psB = pb.ps("mx_psB", [128, T], F32)
        c["_psb"] = psA
        gamma_b = bcast_row(pb, c, "gamma_b", io["gamma_r"][:], D, F32)
        beta_b = bcast_row(pb, c, "beta_b", io["beta_r"][:], D, F32)
        sbz = pb.sb("mx_bz", [128, DC], F32)
        sbh = pb.sb("mx_bh", [128, DC], F32)
        snbz = pb.sb("mx_nbz", [128, DC], F32)
        sbh05 = pb.sb("mx_bh05", [128, DC], F32)
        pb.dma("q", sbz[:], io["bz_t"])
        pb.dma("q", sbh[:], io["bh_t"])
        pb.op("v", lambda e: e.tensor_scalar_mul(out=snbz[:], in0=sbz[:], scalar1=-1.0),
              reads=[sbz], writes=[snbz])
        pb.op("v", lambda e: e.tensor_scalar_add(out=sbh05[:], in0=sbh[:], scalar1=0.5),
              reads=[sbh], writes=[sbh05])
        sid = pb.sb("mx_id", [128, 128], F32)
        pb.dma("q", sid[:], io["ident_f"])
        xnfm = pb.sb("mx_xnfm", [128, DC, T], BF16)
        wzs = [pb.sb(f"mx_wz{i}", [128, DC, 128], BF16) for i in range(2)]
        whs = [pb.sb(f"mx_wh{i}", [128, DC, 128], BF16) for i in range(2)]
        sc = pb.sb("mx_c", [128, T], F32)
        sz = pb.sb("mx_z", [128, T], F32)
        sr = pb.sb("mx_r", [128, T], F32)
        ss = pb.sb("mx_s", [128, T], F32)
        sh = pb.sb("mx_h", [128, T], F32)
        hstg = [pb.sb(f"mx_hs{i}", [128, 128], F32) for i in range(4)]
        hld = [pb.sb(f"mx_hl{i}", [128, 1024], F32) for i in range(2)]
        sxn_r = [pb.sb(f"mx_xn{i}", [128, 1024], F32) for i in range(2)]
        n1 = [pb.sb(f"mx_n1{i}", [128, 1024], F32) for i in range(2)]
        wzr = io["wz"].rearrange("(kc p) h -> p kc h", p=128)
        whr = io["wh"].rearrange("(kc p) h -> p kc h", p=128)

        def hln_tile(s, j):
            jb = j % 2
            i = s * 16 + j
            pb.dma("q", hld[jb][:], io["h_dram"][i * 128:(i + 1) * 128, :],
                   group=f"mx_hld{jb}", reads=[(io["h_dram"], f"{s}.{j}")])
            pb.dma("q", sxn_r[jb][:], io["xn_f32"][i * 128:(i + 1) * 128, :],
                   group=f"mx_sxn{jb}", reads=[(io["xn_f32"], s)])
            t = ln_stats(pb, c, hld[jb][:], D, f"mx_ln{jb}", [hld[jb]])
            pb.op("s", lambda e, jb=jb, t=t: e.activation(out=n1[jb][:], in_=hld[jb][:],
                                                          func=AF.Identity,
                                                          scale=t["rstd"][:], bias=t["nmr"][:]),
                  reads=[hld[jb], t["rstd"], t["nmr"]], writes=[n1[jb]])
            pb.op("v", lambda e, jb=jb: e.tensor_tensor(out=n1[jb][:], in0=n1[jb][:],
                                                        in1=gamma_b[:], op=ALU.mult),
                  reads=[n1[jb], gamma_b], writes=[n1[jb]])
            pb.op("v", lambda e, jb=jb: e.tensor_tensor(out=n1[jb][:], in0=n1[jb][:],
                                                        in1=beta_b[:], op=ALU.add),
                  reads=[n1[jb], beta_b], writes=[n1[jb]])
            pb.op("v", lambda e, jb=jb: e.tensor_tensor(out=n1[jb][:], in0=n1[jb][:],
                                                        in1=sxn_r[jb][:], op=ALU.add),
                  reads=[n1[jb], sxn_r[jb]], writes=[n1[jb]])
            pb.dma("p", io["s0_f32"][i * 128:(i + 1) * 128, :], n1[jb][:],
                   group=f"s0_f32{jb}")

        def mixer_seq(s, filler, spill):
            for kc in range(DC):
                pb.dmat("q", xnfm[:, kc, :],
                        io["xn_bf"][s * T:(s + 1) * T, kc * 128:(kc + 1) * 128],
                        group="mx_xnfm", reads=[(io["xn_bf"], s)],
                        writes=[(xnfm, kc)])
            for m in range(DC):
                b = m % 2
                pb.dma("p", wzs[b][:], wzr[:, :, m * 128:(m + 1) * 128], group=f"mx_wz{b}")
                pb.dma("p", whs[b][:], whr[:, :, m * 128:(m + 1) * 128], group=f"mx_wh{b}")
                for kc in range(DC):
                    for n in range(4):
                        pb.op("t", lambda e, kc=kc, n=n, b=b: e.matmul(
                            psA[:, n * 512:(n + 1) * 512], wzs[b][:, kc, :],
                            xnfm[:, kc, n * 512:(n + 1) * 512],
                            start=(kc == 0), stop=(kc == DC - 1)),
                            reads=[wzs[b], (xnfm, kc)], writes=[(psA, n)])
                emit_spill(spill)
                for kc in range(DC):
                    for n in range(4):
                        pb.op("t", lambda e, kc=kc, n=n, b=b: e.matmul(
                            psB[:, n * 512:(n + 1) * 512], whs[b][:, kc, :],
                            xnfm[:, kc, n * 512:(n + 1) * 512],
                            start=(kc == 0), stop=(kc == DC - 1)),
                            reads=[whs[b], (xnfm, kc)], writes=[(psB, n)])
                pA = [(psA, n) for n in range(4)]
                pB = [(psB, n) for n in range(4)]
                pb.op("s", lambda e, m=m: e.activation(out=sc[:], in_=psA[:], func=AF.Sigmoid,
                                                       scale=-1.0, bias=snbz[:, m:m + 1]),
                      reads=pA + [snbz], writes=[sc])
                pb.op("s", lambda e, m=m: e.activation(out=sz[:], in_=psA[:], func=AF.Sigmoid,
                                                       bias=sbz[:, m:m + 1]),
                      reads=pA + [sbz], writes=[sz])
                pb.op("s", lambda e, m=m: e.activation(out=sr[:], in_=psB[:], func=AF.Identity,
                                                       bias=sbh05[:, m:m + 1]),
                      reads=pB + [sbh05], writes=[sr])
                pb.op("s", lambda e, m=m: e.activation(out=ss[:], in_=psB[:], func=AF.Sigmoid,
                                                       bias=sbh[:, m:m + 1]),
                      reads=pB + [sbh], writes=[ss])
                pb.op("v", lambda e: e.tensor_max(out=ss[:], in0=ss[:], in1=sr[:]),
                      reads=[ss, sr], writes=[ss])
                pb.op("v", lambda e: e.tensor_tensor(out=sz[:], in0=sz[:], in1=ss[:],
                                                     op=ALU.mult),
                      reads=[sz, ss], writes=[sz])
                pb.op("v", lambda e: e.tensor_tensor_scan(out=sh[:], data0=sc[:], data1=sz[:],
                                                          initial=0.0, op0=ALU.mult,
                                                          op1=ALU.add),
                      reads=[sc, sz], writes=[sh])
                for it in filler[m::DC]:
                    it()
                spill[0] = (s, m)

        def emit_spill(spill):
            if spill[0] is None:
                return
            s, m = spill[0]
            spill[0] = None
            for j in range(16):
                jb = j % 4
                pb.op("t", lambda e, j=j: e.transpose(psB[:, j * 128:(j + 1) * 128],
                                                      sh[:, j * 128:(j + 1) * 128], sid[:]),
                      reads=[sh, sid], writes=[(psB, j // 4)])
                pb.op("v", lambda e, j=j, jb=jb: e.tensor_copy(
                    out=hstg[jb][:], in_=psB[:, j * 128:(j + 1) * 128]),
                    reads=[(psB, j // 4)], writes=[hstg[jb]])
                pb.dma("p", io["h_dram"][(s * 16 + j) * 128:(s * 16 + j + 1) * 128,
                                         m * 128:(m + 1) * 128],
                       hstg[jb][:], group=f"h_dram{jb}",
                       reads=[hstg[jb]], writes=[(io["h_dram"], f"{s}.{j}")])

        spill = [None]
        pre0 = [(lambda i=i: pre_tile(i)) for i in range(16)]
        for it in pre0:
            it()
        fill0 = [(lambda i=i: pre_tile(i)) for i in range(16, 32)] \
            + [(lambda j=j: wfc_tile(j)) for j in range(8)]
        mixer_seq(0, fill0, spill)
        fill1 = [(lambda j=j: wfc_tile(j)) for j in range(8, 32)] \
            + [(lambda s=0, j=j: hln_tile(s, j)) for j in range(16)]
        mixer_seq(1, fill1, spill)
        emit_spill(spill)
        for j in range(32):
            wproj_tile(j)
        for j in range(16):
            hln_tile(1, j)


SCRATCH_SPECS = {
    "x": ([NTOK, D], F32),
    "xn_f32": ([NTOK, D], F32), "xn_bf": ([NTOK, D], BF16),
    "xq_bf": ([NTOK, D], BF16),
    "wqfc_bf": ([HID, D], BF16), "wqproj_bf": ([D, HID], BF16),
    "s0_f32": ([NTOK, D], F32),
    "x1_f32": ([NTOK, D], F32), "xn2_bf": ([NTOK, D], BF16),
    "s2_f32": ([NTOK, D], F32),
    "out": ([NTOK, D], F32),
    "h_dram": ([NTOK, D], F32),
    "hq_bf": ([NTOK, HID], BF16),
    "gx_dram": ([HC, 128, T], BF16),
}

LAUNCH_DEFS = {}


def build_launch(name, extra_core_id=None):
    ins, outs, internals, emit_fn = LAUNCH_DEFS[name]
    nc = bass.Bass("TRN2", target_bir_lowering=False)
    pb = PB(nc)
    io = {}
    for nm in ins:
        shape, dt = SHARED_SPECS.get(nm) or SCRATCH_SPECS[nm]
        io[nm] = pb.dram(nm, shape, dt, kind="ExternalInput")
    for nm in outs:
        shape, dt = SCRATCH_SPECS[nm]
        io[nm] = pb.dram(nm, shape, dt, kind="ExternalOutput")
    for nm in internals:
        shape, dt = SCRATCH_SPECS[nm]
        io[nm] = pb.dram(nm, shape, dt, kind="Internal")
    c = {"io": io}
    for v in (EPS, 3 * LN2, -3 * LN2, 1.0, 0.0):
        pb.cpp(v)
    ones = pb._stack.enter_context(pb.nc.sbuf_tensor("ones1_f32", [1, 128], F32))
    pb.op("v", lambda e: e.memset(ones[:], 1.0), writes=[ones])
    c["_ones1_f32"] = ones
    emit_fn(pb, c)
    pb.finish()
    return nc, pb




def emit_ffn0(pb, c):
    io = c["io"]
    with pb.scope():
        psF = [pb.ps(f"f0_psF{i}", [128, T], F32) for i in range(2)]
        c["_psb"] = psF[0]
        bfc_b = bcast_row(pb, c, "bfc_b", io["bfc_r"][:], HID, BF16)
        wqfcT = pb.sb("f0_wfcT", [128, DC, HID], BF16)
        for kc in range(DC):
            pb.dmat("q", wqfcT[:, kc, :], io["wqfc_bf"][:, kc * 128:(kc + 1) * 128],
                    group="f0_wfcT")
        xqfm = pb.sb("f0_xqfm", [128, DC, 1024], BF16)
        sth = [pb.sb(f"f0_sth{i}", [128, T], F32) for i in range(2)]
        shq = [pb.sb(f"f0_shq{i}", [128, T], BF16) for i in range(2)]
        for grp in range(4):
            for kc in range(DC):
                pb.dmat("q", xqfm[:, kc, :],
                        io["xq_bf"][grp * 1024:(grp + 1) * 1024, kc * 128:(kc + 1) * 128],
                        group="f0_xqfm")
            for tcl in range(8):
                tc = grp * 8 + tcl
                for hh in range(2):
                    b = (tcl * 2 + hh) % 2
                    for kc in range(DC):
                        for n in range(4):
                            pb.op("t", lambda e, kc=kc, n=n, hh=hh, tcl=tcl, b=b: e.matmul(
                                psF[b][:, n * 512:(n + 1) * 512],
                                xqfm[:, kc, tcl * 128:(tcl + 1) * 128],
                                wqfcT[:, kc, hh * 2048 + n * 512:hh * 2048 + (n + 1) * 512],
                                start=(kc == 0), stop=(kc == DC - 1)),
                                reads=[(xqfm, kc), wqfcT], writes=[(psF[b], n)])
                    pp = [(psF[b], n) for n in range(4)]
                    pb.op("v", lambda e, b=b, hh=hh: e.tensor_tensor(
                        out=sth[b][:], in0=psF[b][:], in1=bfc_b[:, hh * 2048:(hh + 1) * 2048],
                        op=ALU.add),
                        reads=pp + [bfc_b], writes=[sth[b]])
                    pb.op("s", lambda e, b=b: e.activation(out=sth[b][:], in_=sth[b][:],
                                                           func=AF.Gelu),
                          reads=[sth[b]], writes=[sth[b]])
                    quantize_tile(pb, c, sth[b][:], T, shq[b][:], f"f0_q{b}",
                                  [sth[b]], [shq[b]])
                    pb.dma("p", io["hq_bf"][tc * 128:(tc + 1) * 128,
                                            hh * 2048:(hh + 1) * 2048],
                           shq[b][:], group=f"hq_bf{b}")
    with pb.scope():
        psP = [pb.ps(f"f0_psP{i}", [128, D], F32) for i in range(2)]
        c["_psb"] = psP[0]
        bproj_b = bcast_row(pb, c, "bproj_b", io["bproj_r"][:], D, BF16)
        wqpT = pb.sb("f0_wpT", [128, HC, D], BF16)
        for kc in range(HC):
            pb.dmat("q", wqpT[:, kc, :], io["wqproj_bf"][:, kc * 128:(kc + 1) * 128],
                    group="f0_wpT")
        hqfm = pb.sb("f0_hqfm", [128, HC, 1024], BF16)
        sgl = [pb.sb(f"f0_sgl{i}", [128, D], F32) for i in range(2)]
        sx = [pb.sb(f"f0_sx{i}", [128, D], F32) for i in range(2)]
        ss0 = [pb.sb(f"f0_ss0{i}", [128, D], F32) for i in range(2)]
        sxnb = [pb.sb(f"f0_sxnb{i}", [128, D], BF16) for i in range(2)]
        for grp in range(4):
            for kc in range(HC):
                pb.dmat("q", hqfm[:, kc, :],
                        io["hq_bf"][grp * 1024:(grp + 1) * 1024, kc * 128:(kc + 1) * 128],
                        group="f0_hqfm")
            for tcl in range(8):
                tc = grp * 8 + tcl
                b = tcl % 2
                for kc in range(HC):
                    for n in range(2):
                        pb.op("t", lambda e, kc=kc, n=n, tcl=tcl, b=b: e.matmul(
                            psP[b][:, n * 512:(n + 1) * 512],
                            hqfm[:, kc, tcl * 128:(tcl + 1) * 128],
                            wqpT[:, kc, n * 512:(n + 1) * 512],
                            start=(kc == 0), stop=(kc == HC - 1)),
                            reads=[(hqfm, kc), wqpT], writes=[(psP[b], n)])
                pp = [(psP[b], n) for n in range(2)]
                pb.op("v", lambda e, b=b: e.tensor_tensor(out=sgl[b][:], in0=psP[b][:],
                                                          in1=bproj_b[:], op=ALU.add),
                      reads=pp + [bproj_b], writes=[sgl[b]])
                pb.op("s", lambda e, b=b: e.activation(out=sgl[b][:], in_=sgl[b][:],
                                                       func=AF.Silu),
                      reads=[sgl[b]], writes=[sgl[b]])
                pb.dma("q", sx[b][:], io["x"][tc * 128:(tc + 1) * 128, :], group=f"f0_x{b}")
                pb.dma("q", ss0[b][:], io["s0_f32"][tc * 128:(tc + 1) * 128, :],
                       group=f"f0_s0{b}")
                pb.op("v", lambda e, b=b: e.tensor_tensor(out=sgl[b][:], in0=sgl[b][:],
                                                          in1=ss0[b][:], op=ALU.mult),
                      reads=[sgl[b], ss0[b]], writes=[sgl[b]])
                pb.op("v", lambda e, b=b: e.tensor_tensor(out=sx[b][:], in0=sx[b][:],
                                                          in1=sgl[b][:], op=ALU.add),
                      reads=[sx[b], sgl[b]], writes=[sx[b]])
                pb.dma("p", io["x1_f32"][tc * 128:(tc + 1) * 128, :], sx[b][:],
                       group=f"x1_f32{b}")
                t = ln_stats(pb, c, sx[b][:], D, f"f0_ln{b}", [sx[b]])
                pb.op("s", lambda e, b=b, t=t: e.activation(out=sxnb[b][:], in_=sx[b][:],
                                                            func=AF.Identity,
                                                            scale=t["rstd"][:], bias=t["nmr"][:]),
                      reads=[sx[b], t["rstd"], t["nmr"]], writes=[sxnb[b]])
                pb.dma("p", io["xn2_bf"][tc * 128:(tc + 1) * 128, :], sxnb[b][:],
                       group=f"xn2_bf{b}")


LAUNCH_DEFS["premix"] = (
    ["x", "wz", "wh", "bz_t", "bh_t", "gamma_r", "beta_r", "ident_f", "wfc", "wproj"],
    ["xn_f32", "xn_bf", "xq_bf", "wqfc_bf", "wqproj_bf", "s0_f32"],
    ["h_dram"], emit_premix)


LAUNCH_DEFS["ffn0"] = (
    ["xq_bf", "wqfc_bf", "wqproj_bf", "bfc_r", "bproj_r", "s0_f32", "x"],
    ["x1_f32", "xn2_bf"], ["hq_bf"], emit_ffn0)


def emit_attn(pb, c):
    io = c["io"]
    STRIP_OFF = [0]
    for kc in range(16):
        STRIP_OFF.append(STRIP_OFF[-1] + (T - 128 * kc))
    NPT = STRIP_OFF[-1]

    with pb.scope():
        psB4 = pb.ps("at_psB4", [128, 2048], F32)
        psQ = pb.ps("at_psQ", [128, 1024], F32)
        psX = pb.ps("at_psX", [128, 1024], F32)
        c["_psb"] = psX
        btiles = pb.sb("at_bt", [128, NH * 2, 128], BF16)
        pb.dma("p", btiles[:], io["btiles"])
        srow = pb.sb("at_row", [1, NH * 33], F32)
        pb.dma("q", srow[:], io["rt_row"])
        srow3 = srow[:].rearrange("p (h u) -> p h u", u=33)
        bmaxh = pb.sb("at_bmaxh", [1, NH, 1], F32)
        pb.op("v", lambda e: e.tensor_reduce(out=bmaxh[:], in_=srow3[:, :, 0:17],
                                             axis=AX.X, op=ALU.max),
              reads=[srow], writes=[bmaxh])
        gbm = pb.sb("at_gbm", [1, 1], F32)
        pb.op("v", lambda e: e.tensor_reduce(
            out=gbm[:], in_=bmaxh[:].rearrange("p h u -> p (h u)"),
            axis=AX.X, op=ALU.max),
            reads=[bmaxh], writes=[gbm])
        onescol = pb.sb("at_onec", [128, 1], F32)
        pb.op("v", lambda e: e.memset(onescol[:], 1.0), writes=[onescol])
        bp_b = bcast_row(pb, c, "bp_b", io["bp_r"][:], D, F32)
        ones = c["_ones1_f32"]
        rel0bc = pb.sb("at_r0bc", [128, NH], F32)
        pb.op("t", lambda e: e.matmul(psX[:, 0:NH], ones[:], srow3[0:1, :, 0],
                                      start=True, stop=True),
              reads=[ones, srow], writes=[(psX, 0)])
        pb.op("v", lambda e: e.tensor_copy(out=rel0bc[:], in_=psX[:, 0:NH]),
              reads=[(psX, 0)], writes=[rel0bc])
        for h in range(NH):
            pb.op("v", lambda e, h=h: e.tensor_scalar(
                out=btiles[:, 2 * h:2 * h + 2, :], in0=btiles[:, 2 * h:2 * h + 2, :],
                scalar1=rel0bc[:, h:h + 1], scalar2=8.0,
                op0=ALU.subtract, op1=ALU.mult),
                reads=[(btiles, h), rel0bc], writes=[(btiles, h)])
        nrm = pb.sb("at_nrm", [1, 4], F32)
        u3 = pb.sb("at_u3", [1, 2], F32)
        rel0p = pb.sb("at_rel0", [1, 2], F32)
        upair = pb.sb("at_up", [1, 4], F32)
        mx = pb.sb("at_mx", [128, 1], F32)
        mx2 = pb.sb("at_mx2", [128, 1], F32)
        bU = [pb.sb(f"at_bU{i}", [128, 2], F32) for i in range(2)]
        linv = pb.sb("at_linv", [1, 2048], F32)
        linv2 = pb.sb("at_linv2", [1, 2048], F32)
        rb2 = pb.sb("at_rb2", [1, 2048], BF16)
        onesb = pb.sb("at_onesb", [1, 128], BF16)
        pb.op("v", lambda e: e.memset(onesb[:], 1.0), writes=[onesb])
        ss2 = [pb.sb(f"at_s2_{i}", [128, D], F32) for i in range(2)]
        vaug = pb.sb("at_v", [128, 16, NH * 65 + 63], BF16)
        o_fm = pb.sb("at_o", [128, DC, T], BF16)
        wqr3 = io["wqr"].rearrange("(kc p) o -> p kc o", p=128)
        wkr3 = io["wkr"].rearrange("(kc p) o -> p kc o", p=128)

        for s in range(NSEQ):
            with pb.scope():
                xn2fm = pb.sb(f"at_xn2{s}", [128, DC, T], BF16)
                for kc in range(DC):
                    pb.dmat("q", xn2fm[:, kc, :],
                            io["xn2_bf"][s * T:(s + 1) * T, kc * 128:(kc + 1) * 128],
                            group="at_xn2")
                with pb.scope():
                    wv_b = pb.sb(f"at_wv{s}", [128, DC, D], BF16)
                    pb.dma("p", wv_b[:],
                           io["wvr"].rearrange("(kc p) o -> p kc o", p=128),
                           group="at_wv")
                    pb.op("v", lambda e: e.memset(
                        vaug[:, :, 0:NH * 65].rearrange("p t (h w) -> p t h w", w=65)[:, :, :, 64:65], 1.0),
                        writes=[(vaug, "ones")])
                    pb.op("v", lambda e: e.memset(vaug[:, :, NH * 65:], 0.0),
                          writes=[(vaug, "pad")])
                    for tc in range(16):
                        for kc in range(DC):
                            for n in range(2):
                                pb.op("t", lambda e, tc=tc, kc=kc, n=n: e.matmul(
                                    psX[:, n * 512:(n + 1) * 512],
                                    xn2fm[:, kc, tc * 128:(tc + 1) * 128],
                                    wv_b[:, kc, n * 512:(n + 1) * 512],
                                    start=(kc == 0), stop=(kc == DC - 1)),
                                    reads=[(xn2fm, kc), wv_b], writes=[(psX, n)])
                        pb.op("s", lambda e, tc=tc: e.activation(
                            out=vaug[:, tc, 0:NH * 65].rearrange("p (h w) -> p h w", w=65)[:, :, 0:64],
                            in_=psX[:, 0:1024], func=AF.Copy),
                            reads=[(psX, 0), (psX, 1)], writes=[(vaug, tc)])
                pT = pb.sb(f"at_pT{s}", [128, NPT], BF16)
                qh2 = [pb.sb(f"at_q{s}{i}", [128, T], BF16) for i in range(2)]
                kh2 = [pb.sb(f"at_k{s}{i}", [128, T], BF16) for i in range(2)]
                wq2 = [pb.sb(f"at_wq{s}{i}", [128, DC, 128], BF16) for i in range(2)]
                wk2 = [pb.sb(f"at_wk{s}{i}", [128, DC, 128], BF16) for i in range(2)]
                for hc2 in range(8):
                    b = hc2 % 2
                    pb.dma("p", wq2[b][:], wqr3[:, :, hc2 * 128:(hc2 + 1) * 128],
                           group=f"at_wq{b}")
                    pb.dma("p", wk2[b][:], wkr3[:, :, hc2 * 128:(hc2 + 1) * 128],
                           group=f"at_wk{b}")
                    qbank = lambda n: (psQ[:, (n % 2) * 512:(n % 2 + 1) * 512], (psQ, n % 2)) \
                        if n < 2 else (psX[:, (n % 2) * 512:(n % 2 + 1) * 512], (psX, n % 2))
                    for w_b, dst in ((wq2[b], qh2[b]), (wk2[b], kh2[b])):
                        for kc in range(DC):
                            for n in range(4):
                                ap, tag = qbank(n)
                                pb.op("t", lambda e, w_b=w_b, kc=kc, n=n, ap=ap: e.matmul(
                                    ap, w_b[:, kc, :],
                                    xn2fm[:, kc, n * 512:(n + 1) * 512],
                                    start=(kc == 0), stop=(kc == DC - 1)),
                                    reads=[w_b, (xn2fm, kc)], writes=[tag])
                        for n in range(4):
                            ap, tag = qbank(n)
                            pb.op("s", lambda e, dst=dst, n=n, ap=ap: e.activation(
                                out=dst[:, n * 512:(n + 1) * 512], in_=ap, func=AF.Copy),
                                reads=[tag], writes=[(dst, n)])
                    for qi, src in enumerate((qh2[b], kh2[b])):
                        pb.op("v", lambda e, src=src: e.tensor_reduce(
                            out=mx[:], in_=src[:], axis=AX.X, op=ALU.max,
                            apply_absolute_value=True),
                            reads=[(src, 0), (src, 1), (src, 2), (src, 3)], writes=[mx])
                        pb.op("v", lambda e: e.tensor_tensor(out=mx2[:], in0=mx[:], in1=mx[:],
                                                             op=ALU.mult),
                              reads=[mx], writes=[mx2])
                        for hh in range(2):
                            pb.op("t", lambda e, hh=hh: e.matmul(
                                psX[0:1, 0:1], onescol[hh * 64:(hh + 1) * 64, :],
                                mx2[hh * 64:(hh + 1) * 64, :], start=True, stop=True),
                                reads=[onescol, mx2], writes=[(psX, 0)])
                            pb.op("v", lambda e, qi=qi, hh=hh: e.tensor_copy(
                                out=nrm[0:1, qi * 2 + hh:qi * 2 + hh + 1], in_=psX[0:1, 0:1]),
                                reads=[(psX, 0)], writes=[(nrm, qi * 2 + hh)])
                    nr = [(nrm, i) for i in range(4)]
                    pb.op("v", lambda e: e.tensor_tensor(out=u3[:], in0=nrm[0:1, 0:2],
                                                         in1=nrm[0:1, 2:4], op=ALU.mult),
                          reads=nr, writes=[u3])
                    pb.op("v", lambda e: e.tensor_scalar_max(out=u3[:], in0=u3[:],
                                                             scalar1=1e-30),
                          reads=[u3], writes=[u3])
                    pb.op("s", lambda e: e.activation(out=u3[:], in_=u3[:], func=AF.Ln),
                          reads=[u3], writes=[u3])
                    pb.op("s", lambda e: e.activation(out=u3[:], in_=u3[:], func=AF.Exp,
                                                      scale=0.5),
                          reads=[u3], writes=[u3])
                    pb.op("v", lambda e: e.scalar_tensor_tensor(
                        out=u3[:], in0=u3[:], scalar=0.1275,
                        in1=gbm[0:1, 0:1].broadcast_to([1, 2]), op0=ALU.mult, op1=ALU.add),
                        reads=[u3, gbm], writes=[u3])
                    pb.op("v", lambda e, hc2=hc2: e.tensor_copy(
                        out=rel0p[:], in_=srow3[0:1, 2 * hc2:2 * hc2 + 2, 0]),
                        reads=[srow], writes=[rel0p])
                    pb.op("v", lambda e: e.tensor_tensor(
                        out=upair[0:1, 0:2], in0=rel0p[:], in1=u3[:], op=ALU.subtract),
                        reads=[rel0p, u3], writes=[(upair, 0)])
                    pb.op("t", lambda e: e.matmul(psX[:, 0:2], ones[:], upair[0:1, 0:2],
                                                  start=True, stop=True),
                          reads=[ones, (upair, 0)], writes=[(psX, 0)])
                    pb.op("v", lambda e, b=b: e.tensor_copy(out=bU[b][:], in_=psX[:, 0:2]),
                          reads=[(psX, 0)], writes=[bU[b]])
                    for hh in range(2):
                        h = hc2 * 2 + hh
                        qh_ap = qh2[b][hh * 64:(hh + 1) * 64, :]
                        kh_ap = kh2[b][hh * 64:(hh + 1) * 64, :]
                        qreads = [(qh2[b], n) for n in range(4)]
                        kreads = [(kh2[b], n) for n in range(4)]
                        ci = 0
                        for kc in range(16):
                            N = T - 128 * kc
                            offp = STRIP_OFF[kc]
                            for r0 in range(0, N, 1024):
                                reg = ci % 2
                                ci += 1
                                w = min(1024, N - r0)
                                for n0 in range(0, w, 512):
                                    wn = min(512, w - n0)
                                    pb.op("t", lambda e, kc=kc, r0=r0, n0=n0, wn=wn, reg=reg,
                                          qh_ap=qh_ap, kh_ap=kh_ap: e.matmul(
                                        psB4[:, reg * 1024 + n0:reg * 1024 + n0 + wn],
                                        kh_ap[:, kc * 128:(kc + 1) * 128],
                                        qh_ap[:, kc * 128 + r0 + n0:kc * 128 + r0 + n0 + wn],
                                        start=True, stop=True),
                                        reads=qreads + kreads,
                                        writes=[(psB4, f"b{reg * 2 + n0 // 512}")])
                                sreads = [(psB4, f"b{reg * 2 + n0 // 512}") for n0 in range(0, w, 512)]
                                if r0 == 0:
                                    nw = 128 if kc == 15 else 256
                                    pb.op("v", lambda e, reg=reg, h=h, nw=nw: e.tensor_tensor(
                                        out=psB4[:, reg * 1024:reg * 1024 + nw],
                                        in0=psB4[:, reg * 1024:reg * 1024 + nw],
                                        in1=btiles[:, 2 * h:2 * h + 2, :].rearrange(
                                            "p b j -> p (b j)")[:, 0:nw], op=ALU.add),
                                        reads=sreads[0:1] + [(btiles, h)],
                                        writes=[(psB4, f"b{reg * 2}")])
                                pb.op("s", lambda e, offp=offp, r0=r0, w=w, reg=reg, b=b,
                                      hh=hh: e.activation(
                                    out=pT[:, offp + r0:offp + r0 + w],
                                    in_=psB4[:, reg * 1024:reg * 1024 + w], func=AF.Exp,
                                    scale=SCALE_ATT, bias=bU[b][:, hh:hh + 1]),
                                    reads=sreads + [bU[b]], writes=[(pT, kc)])
                        for kc in range(16):
                            for bk in range(max(kc // 4, 0), 4):
                                b0 = bk * 512
                                b1 = b0 + 512
                                s0 = max(b0, kc * 128)
                                last_kc = min(15, (b1 - 1) // 128)
                                pb.op("t", lambda e, kc=kc, s0=s0, b1=b1, h=h,
                                      last_kc=last_kc: e.matmul(
                                    psB4[:, s0:b1],
                                    vaug[:, kc, h * 65:h * 65 + 128],
                                    pT[:, STRIP_OFF[kc] + s0 - kc * 128:
                                        STRIP_OFF[kc] + b1 - kc * 128],
                                    start=(kc == 0), stop=(kc == last_kc)),
                                    reads=[(vaug, kc), (vaug, "ones"), (vaug, "pad"), (pT, kc)],
                                    writes=[(psB4, f"b{bk}")])
                        avr = [(psB4, f"b{bk}") for bk in range(4)]
                        pb.op("s", lambda e, hh=hh, hc2=hc2: e.activation(
                            out=o_fm[hh * 64:(hh + 1) * 64, hc2, :],
                            in_=psB4[0:64, :], func=AF.Copy),
                            reads=avr, writes=[(o_fm, f"{hh}.{hc2}")])
                        pb.op("v", lambda e: e.tensor_scalar_max(
                            out=linv[:], in0=psB4[64:65, :], scalar1=1e-30),
                            reads=avr, writes=[(linv, 0), (linv, 1)])
                        pb.op("v", lambda e: e.reciprocal(out=linv2[:], in_=linv[:]),
                              reads=[(linv, 0), (linv, 1)], writes=[linv2])
                        pb.op("v", lambda e: e.tensor_copy(out=rb2[:], in_=linv2[:]),
                              reads=[linv2], writes=[rb2])
                        for n4 in range(4):
                            pb.op("t", lambda e, n4=n4: e.matmul(
                                psX[0:64, (n4 % 2) * 512:(n4 % 2 + 1) * 512], onesb[0:1, 0:64],
                                rb2[0:1, n4 * 512:(n4 + 1) * 512],
                                start=True, stop=True),
                                reads=[onesb, rb2], writes=[(psX, n4 % 2)])
                            pb.op("v", lambda e, hh=hh, hc2=hc2, n4=n4: e.tensor_tensor(
                                out=o_fm[hh * 64:(hh + 1) * 64, hc2,
                                         n4 * 512:(n4 + 1) * 512],
                                in0=o_fm[hh * 64:(hh + 1) * 64, hc2,
                                         n4 * 512:(n4 + 1) * 512],
                                in1=psX[0:64, (n4 % 2) * 512:(n4 % 2 + 1) * 512], op=ALU.mult),
                                reads=[(psX, n4 % 2), (o_fm, f"{hh}.{hc2}")],
                                writes=[(o_fm, f"{hh}.{hc2}")])
            with pb.scope():
                wp_b = pb.sb(f"at_wp{s}", [128, DC, D], BF16)
                pb.dma("p", wp_b[:], io["wp"].rearrange("(kc p) o -> p kc o", p=128),
                       group="at_wp")
                for tc in range(16):
                    b = tc % 2
                    for kc in range(DC):
                        for n in range(2):
                            pb.op("t", lambda e, tc=tc, kc=kc, n=n: e.matmul(
                                psX[:, n * 512:(n + 1) * 512],
                                o_fm[:, kc, tc * 128:(tc + 1) * 128],
                                wp_b[:, kc, n * 512:(n + 1) * 512],
                                start=(kc == 0), stop=(kc == DC - 1)),
                                reads=[(o_fm, f"0.{kc}"), (o_fm, f"1.{kc}"), wp_b],
                                writes=[(psX, n)])
                    pb.op("v", lambda e, b=b: e.tensor_tensor(
                        out=ss2[b][:], in0=psX[:, 0:1024], in1=bp_b[:], op=ALU.add),
                        reads=[(psX, 0), (psX, 1), bp_b], writes=[ss2[b]])
                    pb.dma("p", io["s2_f32"][(s * 16 + tc) * 128:(s * 16 + tc + 1) * 128, :],
                           ss2[b][:], group=f"s2_f32{b}")


LAUNCH_DEFS["attn"] = (
    ["xn2_bf", "wqr", "wkr", "wvr", "wp", "bp_r", "rt_row", "btiles"],
    ["s2_f32"], [], emit_attn)


def emit_ffn1(pb, c):
    io = c["io"]
    with pb.scope():
        scl = {}
        for nm in ("s1c", "s2c", "sh1c", "sh2c"):
            scl[nm] = pb.sb("f1_" + nm, [128, HC], F32)
            pb.dma("q", scl[nm][:], io[nm], group="f1_" + nm)
        for s in range(NSEQ):
            with pb.scope():
                h_fm = pb.sb(f"f1_h{s}", [128, HC, T], BF16)
                with pb.scope():
                    psT1 = pb.ps(f"f1_psT1{s}", [128, T], F32)
                    psT2 = pb.ps(f"f1_psT2{s}", [128, T], F32)
                    xn2 = pb.sb(f"f1_xn2{s}", [128, DC, T], BF16)
                    w1s = [pb.sb(f"f1_w1s{s}{i}", [128, DC, 128], BF16) for i in range(2)]
                    w2s = [pb.sb(f"f1_w2s{s}{i}", [128, DC, 128], BF16) for i in range(2)]
                    sa = pb.sb(f"f1_sa{s}", [128, T], F32)
                    w1r = io["w1t"].rearrange("m kc k o -> m k kc o")
                    w2r = io["w2t"].rearrange("m kc k o -> m k kc o")
                    for kc in range(DC):
                        pb.dmat("q", xn2[:, kc, :],
                                io["xn2_bf"][s * T:(s + 1) * T,
                                             kc * 128:(kc + 1) * 128],
                                group="f1_xn2")
                    for m in range(HC):
                        b = m % 2
                        pb.dma("p", w1s[b][:], w1r[m], group=f"f1_w1{b}")
                        pb.dma("p", w2s[b][:], w2r[m], group=f"f1_w2{b}")
                        for kc in range(DC):
                            for n in range(4):
                                pb.op("t", lambda e, kc=kc, n=n, b=b: e.matmul(
                                    psT1[:, n * 512:(n + 1) * 512], w1s[b][:, kc, :],
                                    xn2[:, kc, n * 512:(n + 1) * 512],
                                    start=(kc == 0), stop=(kc == DC - 1)),
                                    reads=[w1s[b], (xn2, kc)], writes=[(psT1, n)])
                        for kc in range(DC):
                            for n in range(4):
                                pb.op("t", lambda e, kc=kc, n=n, b=b: e.matmul(
                                    psT2[:, n * 512:(n + 1) * 512], w2s[b][:, kc, :],
                                    xn2[:, kc, n * 512:(n + 1) * 512],
                                    start=(kc == 0), stop=(kc == DC - 1)),
                                    reads=[w2s[b], (xn2, kc)], writes=[(psT2, n)])
                        p1 = [(psT1, n) for n in range(4)]
                        p2 = [(psT2, n) for n in range(4)]
                        pb.op("s", lambda e, m=m: e.activation(
                            out=sa[:], in_=psT1[:], func=AF.Silu,
                            scale=scl["s1c"][:, m:m + 1]),
                            reads=p1 + [scl["s1c"]], writes=[sa])
                        pb.op("v", lambda e, m=m: e.scalar_tensor_tensor(
                            out=h_fm[:, m, :], in0=psT2[:],
                            scalar=scl["s2c"][:, m:m + 1], in1=sa[:],
                            op0=ALU.mult, op1=ALU.mult),
                            reads=p2 + [scl["s2c"], sa], writes=[(h_fm, m)])
                psU = [pb.ps(f"f1_psU{s}{i}", [128, T], F32) for i in range(2)]
                for ui, (wt, sc) in enumerate((("h1t", "sh1c"), ("h2t", "sh2c"))):
                    stU = [pb.sb(f"f1_u{ui}s{s}{i}", [128, HC, 128], BF16) for i in range(2)]
                    gout = [pb.sb(f"f1_g{ui}o{s}{i}", [128, T], BF16) for i in range(2)]
                    wr = io[wt].rearrange("m kc k o -> m k kc o")
                    for m in range(HC):
                        b = m % 2
                        pb.dma("p", stU[b][:], wr[m], group=f"f1_u{ui}w{b}")
                        for kc in range(HC):
                            for n in range(4):
                                pb.op("t", lambda e, kc=kc, n=n, b=b: e.matmul(
                                    psU[b][:, n * 512:(n + 1) * 512], stU[b][:, kc, :],
                                    h_fm[:, kc, n * 512:(n + 1) * 512],
                                    start=(kc == 0), stop=(kc == HC - 1)),
                                    reads=[stU[b], h_fm], writes=[(psU[b], n)])
                        pu = [(psU[b], n) for n in range(4)]
                        pb.op("s", lambda e, m=m, b=b, sc=sc: e.activation(
                            out=gout[b][:], in_=psU[b][:], func=AF.Gelu,
                            scale=scl[sc][:, m:m + 1]),
                            reads=pu + [scl[sc]], writes=[gout[b]])
                        pb.dma("p", io["gx_dram"][m], gout[b][:], group=f"f1_gx{b}")
                    for m in range(HC):
                        pb.dma("q", h_fm[:, m, :], io["gx_dram"][m],
                               group="f1_hre", reads=[("gx_dram", m)],
                               writes=[(h_fm, m)])
                    if ui == 0:
                        continue
            with pb.scope():
                w3sb = pb.sb(f"f1_w3{s}", [128, HC, D], BF16)
                w3r = io["w3t"].rearrange("(kc) k o -> k kc o")
                for q in range(4):
                    pb.dma("p", w3sb[:, q * 8:(q + 1) * 8, :], w3r[:, q * 8:(q + 1) * 8, :],
                           group="f1_w3")
                psW = [pb.ps(f"f1_psW{s}{i}", [128, 1024], F32) for i in range(2)]
                c["_psb"] = psW[0]
                s3b = bcast_row(pb, c, f"s3b{s}", io["s3_r"][:], D, F32)
                for kc in range(HC):
                    pb.op("v", lambda e, kc=kc: e.tensor_tensor(
                        out=w3sb[:, kc, :], in0=w3sb[:, kc, :], in1=s3b[:], op=ALU.mult),
                        reads=[(w3sb, kc), s3b], writes=[(w3sb, kc)])
                gl = [pb.sb(f"f1_gl{s}{i}", [128, HC, 128], BF16) for i in range(2)]
                sgf = [pb.sb(f"f1_sgf{s}{i}", [128, D], F32) for i in range(2)]
                sx1 = [pb.sb(f"f1_sx1{s}{i}", [128, D], F32) for i in range(2)]
                ss2 = [pb.sb(f"f1_ss2{s}{i}", [128, D], F32) for i in range(2)]
                gxr = io["gx_dram"]
                for tc in range(16):
                    b = tc % 2
                    pb.dma("q", gl[b][:], gxr[:, :, tc * 128:(tc + 1) * 128].rearrange(
                        "kc k t -> k kc t"), group=f"f1_gl{b}", reads=[("gx_dram", "")])
                    for kc in range(HC):
                        for n in range(2):
                            pb.op("t", lambda e, kc=kc, n=n, b=b: e.matmul(
                                psW[b][:, n * 512:(n + 1) * 512], gl[b][:, kc, :],
                                w3sb[:, kc, n * 512:(n + 1) * 512],
                                start=(kc == 0), stop=(kc == HC - 1)),
                                reads=[gl[b], (w3sb, kc)], writes=[(psW[b], n)])
                    pw = [(psW[b], n) for n in range(2)]
                    pb.op("s", lambda e, b=b: e.activation(out=sgf[b][:], in_=psW[b][:],
                                                           func=AF.Silu),
                          reads=pw, writes=[sgf[b]])
                    i = s * 16 + tc
                    pb.dma("q", sx1[b][:], io["x1_f32"][i * 128:(i + 1) * 128, :],
                           group=f"f1_x1{b}")
                    pb.dma("q", ss2[b][:], io["s2_f32"][i * 128:(i + 1) * 128, :],
                           group=f"f1_s2{b}")
                    pb.op("v", lambda e, b=b: e.tensor_tensor(
                        out=sgf[b][:], in0=sgf[b][:], in1=ss2[b][:], op=ALU.mult),
                        reads=[sgf[b], ss2[b]], writes=[sgf[b]])
                    pb.op("v", lambda e, b=b: e.tensor_tensor(
                        out=sx1[b][:], in0=sx1[b][:], in1=sgf[b][:], op=ALU.add),
                        reads=[sx1[b], sgf[b]], writes=[sx1[b]])
                    pb.dma("p", io["out"][i * 128:(i + 1) * 128, :], sx1[b][:],
                           group=f"out{b}")


LAUNCH_DEFS["ffn1"] = (
    ["xn2_bf", "w1t", "w2t", "h1t", "h2t", "w3t", "s1c", "s2c", "sh1c", "sh2c",
     "s3_r", "x1_f32", "s2_f32"],
    ["out"], ["gx_dram"], emit_ffn1)


from concourse.bass_utils import run_bass_kernel_spmd

N_CORES = 8
LAUNCH_ORDER = ["premix", "ffn0", "attn", "ffn1"]


def build_fused():
    nc = bass.Bass("TRN2", target_bir_lowering=False)
    pb = PB(nc)
    io = {}
    for nm, (shape, dt) in SHARED_SPECS.items():
        io[nm] = pb.dram(nm, shape, dt, kind="ExternalInput")
    io["x"] = pb.dram("x", list(SCRATCH_SPECS["x"][0]), F32, kind="ExternalInput")
    io["out"] = pb.dram("out", list(SCRATCH_SPECS["out"][0]), F32,
                        kind="ExternalOutput")
    for nm, (shape, dt) in SCRATCH_SPECS.items():
        if nm in ("x", "out"):
            continue
        io[nm] = pb.dram(nm, shape, dt, kind="Internal")
    c = {"io": io}
    for v in (EPS, 3 * LN2, -3 * LN2, 1.0, 0.0):
        pb.cpp(v)
    ones = pb._stack.enter_context(pb.nc.sbuf_tensor("ones1_f32", [1, 128], F32))
    pb.op("v", lambda e: e.memset(ones[:], 1.0), writes=[ones])
    c["_ones1_f32"] = ones
    for name in LAUNCH_ORDER:
        LAUNCH_DEFS[name][3](pb, c)
    pb.finish()
    return nc, pb


def kernel(**inputs):
    shared = host_prep_shared(inputs)
    cores = [host_prep_core(inputs, ci) for ci in range(N_CORES)]
    nc, pb = build_fused()
    in_maps = [dict(shared, **cores[ci]) for ci in range(N_CORES)]
    res = run_bass_kernel_spmd(nc, in_maps, core_ids=list(range(N_CORES)))
    out = np.concatenate(
        [np.asarray(res.results[ci]["out"], np.float32).reshape(2, T, D)
         for ci in range(N_CORES)], axis=0)
    pb.close()
    return out


# version 13
# speedup vs baseline: 1.1708x; 1.1708x over previous
import contextlib
import numpy as np
import concourse.bass as bass
import concourse.mybir as mybir

F32 = mybir.dt.float32
BF16 = mybir.dt.bfloat16
I32 = mybir.dt.int32
AF = mybir.ActivationFunctionType
ALU = mybir.AluOpType
AX = mybir.AxisListType

ENGINES = {"t": "tensor", "v": "vector", "s": "scalar", "p": "gpsimd", "q": "sync"}


def _base(x):
    if hasattr(x, "tensor"):
        return x.tensor.name
    if hasattr(x, "name"):
        return x.name
    return str(x)


def _key(x):
    if isinstance(x, tuple):
        return (_base(x[0]), str(x[1]))
    return (_base(x), "")


class PB:
    def __init__(self, nc: bass.Bass):
        self.nc = nc
        self._stack = contextlib.ExitStack()
        self.sem = {}
        self.cnt = {}
        self.waited = {}
        self.reg = {}
        self.sb_bytes = 0
        self.n_ops = 0
        for k in ENGINES:
            self.sem[k] = self._stack.enter_context(nc.semaphore("sem_" + k))
            self.cnt[k] = 0

    def sb(self, name, shape, dtype):
        st = self._scopes[-1] if getattr(self, "_scopes", None) else self._stack
        t = st.enter_context(self.nc.sbuf_tensor(name, list(shape), dtype))
        self.sb_bytes += int(np.prod(shape[1:])) * np.dtype(mybir.dt.np(dtype)).itemsize
        return t

    def ps(self, name, shape, dtype=F32):
        st = self._scopes[-1] if getattr(self, "_scopes", None) else self._stack
        return st.enter_context(self.nc.psum_tensor(name, list(shape), dtype))

    @contextlib.contextmanager
    def scope(self):
        if not hasattr(self, "_scopes"):
            self._scopes = []
        st = contextlib.ExitStack()
        self._scopes.append(st)
        try:
            yield
        finally:
            self._scopes.pop()
            st.close()
            self.barrier()

    def barrier(self):
        need = {k: v for k, v in self.cnt.items() if v > 0}
        for eng in ENGINES:
            self._wait(eng, dict(need))
        self.reg.clear()

    def dram(self, name, shape, dtype, kind="Internal"):
        return self.nc.dram_tensor(name, list(shape), dtype, kind=kind).ap()

    def close(self):
        self._stack.close()

    def _group_sem(self, group):
        k = "g:" + group
        if k not in self.sem:
            self.sem[k] = self._stack.enter_context(self.nc.semaphore("sg_" + group))
            self.cnt[k] = 0
        return k

    def _entries(self, tensor, sub, for_write):
        d = self.reg.get(tensor)
        if d is None:
            return []
        if sub == "":
            return list(d.values())
        out = []
        if "" in d:
            out.append(d[""])
        if sub in d:
            out.append(d[sub])
        return out

    def _wait(self, eng, need):
        e = getattr(self.nc, ENGINES[eng])
        for semkey, val in sorted(need.items()):
            if val <= 0:
                continue
            if semkey.startswith("g:"):
                val = self.cnt[semkey]
            if self.waited.get((eng, semkey), -1) >= val:
                continue
            e.wait_ge(self.sem[semkey], val)
            self.waited[(eng, semkey)] = val

    def op(self, eng, fn, reads=(), writes=(), dma_group=None):
        reads = [_key(x) for x in reads]
        writes = [_key(x) for x in writes]
        need = {}

        def add(sv):
            if sv and need.get(sv[0], -1) < sv[1]:
                need[sv[0]] = sv[1]

        for t, s in reads:
            for ent in self._entries(t, s, False):
                add(ent.get("w"))
        for t, s in writes:
            for ent in self._entries(t, s, True):
                add(ent.get("w"))
                for sk, v in ent.get("r", {}).items():
                    add((sk, v))
        self._wait(eng, need)
        inst = fn(getattr(self.nc, ENGINES[eng]))
        if dma_group is not None:
            semkey = self._group_sem(dma_group)
            inc = 16
        else:
            semkey = eng
            inc = 1
        self.cnt[semkey] += inc
        inst.then_inc(self.sem[semkey], inc)
        v = self.cnt[semkey]
        self.n_ops += 1
        for t, s in writes:
            d = self.reg.setdefault(t, {})
            if s == "":
                d.clear()
            ent = d.setdefault(s, {})
            ent["w"] = (semkey, v)
            ent["r"] = {}
        for t, s in reads:
            ent = self.reg.setdefault(t, {}).setdefault(s, {})
            r = ent.setdefault("r", {})
            if r.get(semkey, -1) < v:
                r[semkey] = v
        return inst

    def dma(self, eng, out, in_, group=None, reads=(), writes=(), **kw):
        g = group or ("b_" + _base(out))
        return self.op(eng, lambda e: e.dma_start(out=out, in_=in_, **kw),
                       reads=list(reads) or [in_], writes=list(writes) or [out],
                       dma_group=g)

    def dmat(self, eng, out, in_, group=None, reads=(), writes=()):
        g = group or ("b_" + _base(out))
        return self.op(eng, lambda e: e.dma_start_transpose(out=out, in_=in_),
                       reads=list(reads) or [in_], writes=list(writes) or [out],
                       dma_group=g)

    def cpp(self, val, dtype=F32):
        if not hasattr(self, "_CPP"):
            self._CPP = {}
        key = (float(val), dtype)
        if key not in self._CPP:
            t = self._stack.enter_context(
                self.nc.sbuf_tensor(f"cpp_{len(self._CPP)}", [128, 1], dtype))
            self.op("v", lambda e: e.memset(t[:], float(val)), writes=[t])
            self._CPP[key] = t
        return self._CPP[key][:]

    def finish(self):
        need = {k: v for k, v in self.cnt.items() if v > 0}
        self._wait("q", need)


def simrun(nc, inputs, outs):
    from concourse.bass_interp import CoreSim
    nc.finalize()
    sim = CoreSim(nc)
    for k, v in inputs.items():
        sim.tensor(k)[:] = v
    sim.simulate(check_with_hw=False)
    return {k: np.array(sim.mem_tensor(k)) for k in outs}


"""Emitters for the mk_recurrent_d2 megakernel. Per-core program pieces.

Per core: 2 sequences x 2048 tokens x 1024 features.
Layouts: TM = [token part, feat free]; FM = [feat part, token free].
All matmuls bf16 operands / fp32 PSUM accumulation; elementwise fp32.
"""
import numpy as np
import ml_dtypes

D = 1024
T = 2048
NSEQ = 2
NTOK = 4096
HID = 4096
NH = 16
DH = 64
EPS = 1e-5
LN2 = float(np.log(2.0))
BIGNEG8 = -8.0e33
DC = D // 128
TC = NTOK // 128
HC = HID // 128
SCALE_ATT = 0.125

BF = ml_dtypes.bfloat16


def host_prep_shared(inp):
    c = lambda a: np.ascontiguousarray(a)
    d = {}
    d["wz"] = c(inp["L0_attn_Wz"])
    d["wh"] = c(inp["L0_attn_Wh"])
    d["bz_t"] = c(np.asarray(inp["L0_attn_bz"]).reshape(DC, 128).T)
    d["bh_t"] = c(np.asarray(inp["L0_attn_bh"]).reshape(DC, 128).T)
    d["gamma_r"] = c(np.asarray(inp["L0_attn_gamma"]).reshape(1, D))
    d["beta_r"] = c(np.asarray(inp["L0_attn_beta"]).reshape(1, D))
    d["wfc"] = c(inp["L0_ffn_w_fc"])
    d["bfc_r"] = c(np.asarray(inp["L0_ffn_b_fc"]).reshape(1, HID))
    d["wproj"] = c(inp["L0_ffn_w_proj"])
    d["bproj_r"] = c(np.asarray(inp["L0_ffn_b_proj"]).reshape(1, D))
    for nm, key in [("wqr", "L1_attn_Wq"), ("wkr", "L1_attn_Wk"), ("wvr", "L1_attn_Wv")]:
        d[nm] = c(np.asarray(inp[key]).transpose(1, 0, 2).reshape(D, NH * DH))
    d["wp"] = c(inp["L1_attn_Wp"])
    d["bp_r"] = c(np.asarray(inp["L1_attn_bp"]).reshape(1, D))
    rt = np.asarray(inp["L1_attn_rel_table"])
    d["rt_row"] = c(rt.T.reshape(1, NH * 33))
    i_ = np.arange(128)[:, None]
    j_ = np.arange(128)[None, :]
    bucket_diag = np.clip(i_ - j_ + 16, 0, 32)
    bucket_sub = np.maximum(i_ - j_ - 112, 0)
    bt = np.empty((128, NH * 2, 128), np.float32)
    for h in range(NH):
        t0 = rt[bucket_diag, h]
        bt[:, 2 * h, :] = np.where(i_ <= j_, t0, np.float32(-1e30))
        bt[:, 2 * h + 1, :] = rt[bucket_sub, h]
    d["btiles"] = c(bt)
    def tile_T(w, mchunks, kchunks):
        wt = np.ascontiguousarray(w.T)
        K, O = wt.shape
        return c(wt.reshape(kchunks, 128, mchunks, 128).transpose(2, 0, 1, 3))
    d["w1t"] = tile_T(np.asarray(inp["L1_ffn_w1_w"]), HC, DC)
    d["w2t"] = tile_T(np.asarray(inp["L1_ffn_w2_w"]), HC, DC)
    d["h1t"] = tile_T(np.asarray(inp["L1_ffn_h1_w"]), HC, HC)
    d["h2t"] = tile_T(np.asarray(inp["L1_ffn_h2_w"]), HC, HC)
    w3t = np.ascontiguousarray(np.asarray(inp["L1_ffn_w3_w"]).T)
    d["w3t"] = c(w3t.reshape(HC, 128, D))
    for nm, key in [("s1c", "L1_ffn_w1_s"), ("s2c", "L1_ffn_w2_s"),
                    ("sh1c", "L1_ffn_h1_s"), ("sh2c", "L1_ffn_h2_s")]:
        d[nm] = c(np.asarray(inp[key]).reshape(HC, 128).T)
    d["s3_r"] = c(np.asarray(inp["L1_ffn_w3_s"]).reshape(1, D))
    d["ident_f"] = np.eye(128, dtype=np.float32)
    d["ident_b"] = np.eye(128, dtype=BF)
    return d


def host_prep_core(inp, core):
    x = np.asarray(inp["L0_attn_x"])[2 * core:2 * core + 2]
    return {"x": np.ascontiguousarray(x.reshape(NTOK, D))}


SHARED_SPECS = {
    "wz": ([D, D], F32), "wh": ([D, D], F32),
    "bz_t": ([128, DC], F32), "bh_t": ([128, DC], F32),
    "gamma_r": ([1, D], F32), "beta_r": ([1, D], F32),
    "wfc": ([HID, D], F32), "bfc_r": ([1, HID], F32),
    "wproj": ([D, HID], F32), "bproj_r": ([1, D], F32),
    "wqr": ([D, D], F32), "wkr": ([D, D], F32), "wvr": ([D, D], F32),
    "wp": ([D, D], F32), "bp_r": ([1, D], F32),
    "rt_row": ([1, NH * 33], F32),
    "btiles": ([128, NH * 2, 128], F32),
    "w1t": ([HC, DC, 128, 128], F32), "w2t": ([HC, DC, 128, 128], F32),
    "h1t": ([HC, HC, 128, 128], F32), "h2t": ([HC, HC, 128, 128], F32),
    "w3t": ([HC, 128, D], F32),
    "s1c": ([128, HC], F32), "s2c": ([128, HC], F32),
    "sh1c": ([128, HC], F32), "sh2c": ([128, HC], F32),
    "s3_r": ([1, D], F32),
    "ident_f": ([128, 128], F32), "ident_b": ([128, 128], BF16),
}


def bcast_row(pb, c, name, row_ap, width, dtype=BF16, row_reads=()):
    ones = c["_ones1_f32"]
    if row_ap.tensor.__class__.__name__.startswith("DRam") or "DRAM" in str(getattr(row_ap, "space", "")):
        srow = pb.sb(name + "_row", [1, width], F32)
        pb.dma("q", srow[:], row_ap, group=name + "_row")
        row_ap = srow[:]
        row_reads = [srow]
    dst = pb.sb(name, [128, width], dtype)
    psb = c.get("_psb")
    if psb is None:
        psb = pb.ps("ps_bcast", [128, 512], F32)
        c["_psb"] = psb
    pkey = psb if not isinstance(psb, tuple) else psb
    for j in range(0, width, 512):
        w = min(512, width - j)
        pb.op("t", lambda e, j=j, w=w, row_ap=row_ap: e.matmul(
            psb[:, 0:w] if not isinstance(psb, tuple) else None,
            ones[:], row_ap[:, j:j + w], start=True, stop=True),
              reads=[ones] + list(row_reads), writes=[psb])
        pb.op("v", lambda e, j=j, w=w: e.tensor_copy(out=dst[:, j:j + w], in_=psb[:, 0:w]),
              reads=[psb], writes=[(dst, j)])
    return dst


def ln_stats(pb, c, sx_ap, w, pref, reads):
    if pref not in c:
        c[pref] = dict(
            sum=pb.sb(pref + "_sum", [128, 1], F32),
            ssq=pb.sb(pref + "_ssq", [128, 1], F32),
            sqs=pb.sb(pref + "_sqs", [128, w], F32),
            mean=pb.sb(pref + "_mean", [128, 1], F32),
            m2=pb.sb(pref + "_m2", [128, 1], F32),
            var=pb.sb(pref + "_var", [128, 1], F32),
            lv=pb.sb(pref + "_lv", [128, 1], F32),
            rstd=pb.sb(pref + "_rstd", [128, 1], F32),
            nmr=pb.sb(pref + "_nmr", [128, 1], F32),
        )
    t = c[pref]
    pb.op("v", lambda e: e.reduce_sum(out=t["sum"][:], in_=sx_ap, axis=AX.X),
          reads=reads, writes=[t["sum"]])
    pb.op("s", lambda e: e.activation(out=t["sqs"][:, 0:w], in_=sx_ap, func=AF.Square,
                                      accum_out=t["ssq"][:]),
          reads=reads, writes=[t["sqs"], t["ssq"]])
    pb.op("v", lambda e: e.tensor_scalar_mul(out=t["mean"][:], in0=t["sum"][:], scalar1=1.0 / w),
          reads=[t["sum"]], writes=[t["mean"]])
    pb.op("v", lambda e: e.tensor_tensor(out=t["m2"][:], in0=t["mean"][:], in1=t["mean"][:],
                                         op=ALU.mult),
          reads=[t["mean"]], writes=[t["m2"]])
    pb.op("v", lambda e: e.scalar_tensor_tensor(out=t["var"][:], in0=t["ssq"][:],
                                                scalar=1.0 / w, in1=t["m2"][:],
                                                op0=ALU.mult, op1=ALU.subtract),
          reads=[t["ssq"], t["m2"]], writes=[t["var"]])
    beps = pb.cpp(EPS)
    pb.op("s", lambda e: e.activation(out=t["lv"][:], in_=t["var"][:], func=AF.Ln, bias=beps),
          reads=[t["var"], beps], writes=[t["lv"]])
    pb.op("s", lambda e: e.activation(out=t["rstd"][:], in_=t["lv"][:], func=AF.Exp, scale=-0.5),
          reads=[t["lv"]], writes=[t["rstd"]])
    pb.op("v", lambda e: e.scalar_tensor_tensor(out=t["nmr"][:], in0=t["mean"][:], scalar=-1.0,
                                                in1=t["rstd"][:], op0=ALU.mult, op1=ALU.mult),
          reads=[t["mean"], t["rstd"]], writes=[t["nmr"]])
    return t


def quantize_tile(pb, c, sx_ap, w, out_bf_ap, pref, reads, out_writes):
    nb = w // 32
    if pref not in c:
        c[pref] = dict(
            am=pb.sb(pref + "_am", [128, nb, 1], F32),
            ef=pb.sb(pref + "_ef", [128, nb, 1], F32),
            ei=pb.sb(pref + "_ei", [128, nb, 1], I32),
            el=pb.sb(pref + "_el", [128, nb, 1], F32),
            inv=pb.sb(pref + "_inv", [128, nb, 1], F32),
            scl=pb.sb(pref + "_scl", [128, nb, 1], F32),
            y=pb.sb(pref + "_y", [128, w], F32),
            qi=pb.sb(pref + "_qi", [128, w], I32),
            r=pb.sb(pref + "_r", [128, w], F32),
        )
    t = c[pref]
    v3 = sx_ap.rearrange("p (b w) -> p b w", w=32)
    pb.op("v", lambda e: e.tensor_reduce(out=t["am"][:, 0:nb], in_=v3, axis=AX.X, op=ALU.max,
                                         apply_absolute_value=True),
          reads=reads, writes=[t["am"]])
    pb.op("v", lambda e: e.tensor_scalar_max(out=t["am"][:, 0:nb], in0=t["am"][:, 0:nb],
                                             scalar1=float(2.0 ** -63)),
          reads=[t["am"]], writes=[t["am"]])
    pb.op("s", lambda e: e.activation(out=t["el"][:, 0:nb], in_=t["am"][:, 0:nb], func=AF.Ln,
                                      scale=1.0),
          reads=[t["am"]], writes=[t["el"]])
    pb.op("v", lambda e: e.tensor_scalar_mul(out=t["el"][:, 0:nb], in0=t["el"][:, 0:nb],
                                             scalar1=float(1.0 / LN2)),
          reads=[t["el"]], writes=[t["el"]])
    pb.op("v", lambda e: e.tensor_copy(out=t["ei"][:, 0:nb], in_=t["el"][:, 0:nb]),
          reads=[t["el"]], writes=[t["ei"]])
    pb.op("v", lambda e: e.tensor_copy(out=t["ef"][:, 0:nb], in_=t["ei"][:, 0:nb]),
          reads=[t["ei"]], writes=[t["ef"]])
    pb.op("v", lambda e: e.tensor_tensor(out=t["el"][:, 0:nb], in0=t["el"][:, 0:nb],
                                         in1=t["ef"][:, 0:nb], op=ALU.is_lt),
          reads=[t["el"], t["ef"]], writes=[t["el"]])
    pb.op("v", lambda e: e.tensor_tensor(out=t["ef"][:, 0:nb], in0=t["ef"][:, 0:nb],
                                         in1=t["el"][:, 0:nb], op=ALU.subtract),
          reads=[t["ef"], t["el"]], writes=[t["ef"]])
    b3p = pb.cpp(3 * LN2)
    b3n = pb.cpp(-3 * LN2)
    pb.op("s", lambda e: e.activation(out=t["inv"][:, 0:nb], in_=t["ef"][:, 0:nb], func=AF.Exp,
                                      scale=-LN2, bias=b3p),
          reads=[t["ef"], b3p], writes=[t["inv"]])
    pb.op("s", lambda e: e.activation(out=t["scl"][:, 0:nb], in_=t["ef"][:, 0:nb], func=AF.Exp,
                                      scale=LN2, bias=b3n),
          reads=[t["ef"], b3n], writes=[t["scl"]])
    pb.op("v", lambda e: e.tensor_tensor(out=t["y"][:, 0:w].rearrange("p (b w) -> p b w", w=32),
                                         in0=v3, in1=t["inv"][:, 0:nb].broadcast_to([128, nb, 32]),
                                         op=ALU.mult),
          reads=reads + [t["inv"]], writes=[t["y"]])
    pb.op("v", lambda e: e.tensor_copy(out=t["qi"][:, 0:w], in_=t["y"][:, 0:w]),
          reads=[t["y"]], writes=[t["qi"]])
    pb.op("v", lambda e: e.tensor_copy(out=t["r"][:, 0:w], in_=t["qi"][:, 0:w]),
          reads=[t["qi"]], writes=[t["r"]])
    pb.op("v", lambda e: e.tensor_tensor(out=t["y"][:, 0:w], in0=t["y"][:, 0:w],
                                         in1=t["r"][:, 0:w], op=ALU.subtract),
          reads=[t["y"], t["r"]], writes=[t["y"]])
    pb.op("v", lambda e: e.scalar_tensor_tensor(out=t["r"][:, 0:w], in0=t["y"][:, 0:w],
                                                scalar=0.5, in1=t["r"][:, 0:w],
                                                op0=ALU.is_gt, op1=ALU.add),
          reads=[t["y"], t["r"]], writes=[t["r"]])
    pb.op("v", lambda e: e.scalar_tensor_tensor(out=t["r"][:, 0:w], in0=t["y"][:, 0:w],
                                                scalar=-0.5, in1=t["r"][:, 0:w],
                                                op0=ALU.is_ge, op1=ALU.add),
          reads=[t["y"], t["r"]], writes=[t["r"]])
    pb.op("v", lambda e: e.tensor_scalar_add(out=t["r"][:, 0:w], in0=t["r"][:, 0:w],
                                             scalar1=-1.0),
          reads=[t["r"]], writes=[t["r"]])
    pb.op("v", lambda e: e.tensor_tensor(out=out_bf_ap.rearrange("p (b w) -> p b w", w=32),
                                         in0=t["r"][:, 0:w].rearrange("p (b w) -> p b w", w=32),
                                         in1=t["scl"][:, 0:nb].broadcast_to([128, nb, 32]),
                                         op=ALU.mult),
          reads=[t["r"], t["scl"]], writes=out_writes)


def emit_premix(pb, c):
    io = c["io"]
    with pb.scope():
        sx = [pb.sb(f"pm_sx{i}", [128, 1024], F32) for i in range(2)]
        sxn = [pb.sb(f"pm_sxn{i}", [128, 1024], F32) for i in range(2)]
        sxnb = [pb.sb(f"pm_sxnb{i}", [128, 1024], BF16) for i in range(2)]
        sqb = [pb.sb(f"pm_sqb{i}", [128, 1024], BF16) for i in range(2)]

        def pre_tile(i):
            b = i % 2
            sq = i // 16
            pb.dma("q", sx[b][:], io["x"][i * 128:(i + 1) * 128, :], group=f"pm_x{b}")
            t = ln_stats(pb, c, sx[b][:], D, f"pm_ln{b}", [sx[b]])
            pb.op("s", lambda e, b=b, t=t: e.activation(out=sxn[b][:], in_=sx[b][:],
                                                        func=AF.Identity,
                                                        scale=t["rstd"][:], bias=t["nmr"][:]),
                  reads=[sx[b], t["rstd"], t["nmr"]], writes=[sxn[b]])
            pb.op("s", lambda e, b=b: e.activation(out=sxnb[b][:], in_=sxn[b][:], func=AF.Copy),
                  reads=[sxn[b]], writes=[sxnb[b]])
            pb.dma("p", io["xn_f32"][i * 128:(i + 1) * 128, :], sxn[b][:], group="xn_f32",
                   reads=[sxn[b]], writes=[(io["xn_f32"], sq)])
            pb.dma("p", io["xn_bf"][i * 128:(i + 1) * 128, :], sxnb[b][:], group="xn_bf",
                   reads=[sxnb[b]], writes=[(io["xn_bf"], sq)])
            quantize_tile(pb, c, sxn[b][:], D, sqb[b][:], f"pm_q{b}", [sxn[b]], [sqb[b]])
            pb.dma("p", io["xq_bf"][i * 128:(i + 1) * 128, :], sqb[b][:], group="xq_bf",
                   reads=[sqb[b]], writes=[(io["xq_bf"], sq)])

        def wfc_tile(j):
            b = j % 2
            pb.dma("q", sx[b][:], io["wfc"][j * 128:(j + 1) * 128, :], group=f"pm_x{b}")
            quantize_tile(pb, c, sx[b][:], D, sqb[b][:], f"pm_q{b}", [sx[b]], [sqb[b]])
            pb.dma("p", io["wqfc_bf"][j * 128:(j + 1) * 128, :], sqb[b][:], group="wqfc_bf")

        def wproj_tile(j):
            r, qq = j // 4, j % 4
            b = j % 2
            pb.dma("q", sx[b][:], io["wproj"][r * 128:(r + 1) * 128,
                                              qq * 1024:(qq + 1) * 1024], group=f"pm_x{b}")
            quantize_tile(pb, c, sx[b][:], D, sqb[b][:], f"pm_q{b}", [sx[b]], [sqb[b]])
            pb.dma("p", io["wqproj_bf"][r * 128:(r + 1) * 128, qq * 1024:(qq + 1) * 1024],
                   sqb[b][:], group="wqproj_bf")

        psA = pb.ps("mx_psA", [128, T], F32)
        psB = pb.ps("mx_psB", [128, T], F32)
        c["_psb"] = psA
        gamma_b = bcast_row(pb, c, "gamma_b", io["gamma_r"][:], D, F32)
        beta_b = bcast_row(pb, c, "beta_b", io["beta_r"][:], D, F32)
        sbz = pb.sb("mx_bz", [128, DC], F32)
        sbh = pb.sb("mx_bh", [128, DC], F32)
        snbz = pb.sb("mx_nbz", [128, DC], F32)
        sbh05 = pb.sb("mx_bh05", [128, DC], F32)
        pb.dma("q", sbz[:], io["bz_t"])
        pb.dma("q", sbh[:], io["bh_t"])
        pb.op("v", lambda e: e.tensor_scalar_mul(out=snbz[:], in0=sbz[:], scalar1=-1.0),
              reads=[sbz], writes=[snbz])
        pb.op("v", lambda e: e.tensor_scalar_add(out=sbh05[:], in0=sbh[:], scalar1=0.5),
              reads=[sbh], writes=[sbh05])
        sid = pb.sb("mx_id", [128, 128], F32)
        pb.dma("q", sid[:], io["ident_f"])
        xnfm = pb.sb("mx_xnfm", [128, DC, T], BF16)
        wzs = [pb.sb(f"mx_wz{i}", [128, DC, 128], BF16) for i in range(2)]
        whs = [pb.sb(f"mx_wh{i}", [128, DC, 128], BF16) for i in range(2)]
        sc = pb.sb("mx_c", [128, T], F32)
        sz = pb.sb("mx_z", [128, T], F32)
        sr = pb.sb("mx_r", [128, T], F32)
        ss = pb.sb("mx_s", [128, T], F32)
        sh = pb.sb("mx_h", [128, T], F32)
        hstg = [pb.sb(f"mx_hs{i}", [128, 128], F32) for i in range(4)]
        hld = [pb.sb(f"mx_hl{i}", [128, 1024], F32) for i in range(2)]
        sxn_r = [pb.sb(f"mx_xn{i}", [128, 1024], F32) for i in range(2)]
        n1 = [pb.sb(f"mx_n1{i}", [128, 1024], F32) for i in range(2)]
        wzr = io["wz"].rearrange("(kc p) h -> p kc h", p=128)
        whr = io["wh"].rearrange("(kc p) h -> p kc h", p=128)

        def hln_tile(s, j):
            jb = j % 2
            i = s * 16 + j
            pb.dma("q", hld[jb][:], io["h_dram"][i * 128:(i + 1) * 128, :],
                   group=f"mx_hld{jb}", reads=[(io["h_dram"], f"{s}.{j}")])
            pb.dma("q", sxn_r[jb][:], io["xn_f32"][i * 128:(i + 1) * 128, :],
                   group=f"mx_sxn{jb}", reads=[(io["xn_f32"], s)])
            t = ln_stats(pb, c, hld[jb][:], D, f"mx_ln{jb}", [hld[jb]])
            pb.op("s", lambda e, jb=jb, t=t: e.activation(out=n1[jb][:], in_=hld[jb][:],
                                                          func=AF.Identity,
                                                          scale=t["rstd"][:], bias=t["nmr"][:]),
                  reads=[hld[jb], t["rstd"], t["nmr"]], writes=[n1[jb]])
            pb.op("v", lambda e, jb=jb: e.tensor_tensor(out=n1[jb][:], in0=n1[jb][:],
                                                        in1=gamma_b[:], op=ALU.mult),
                  reads=[n1[jb], gamma_b], writes=[n1[jb]])
            pb.op("v", lambda e, jb=jb: e.tensor_tensor(out=n1[jb][:], in0=n1[jb][:],
                                                        in1=beta_b[:], op=ALU.add),
                  reads=[n1[jb], beta_b], writes=[n1[jb]])
            pb.op("v", lambda e, jb=jb: e.tensor_tensor(out=n1[jb][:], in0=n1[jb][:],
                                                        in1=sxn_r[jb][:], op=ALU.add),
                  reads=[n1[jb], sxn_r[jb]], writes=[n1[jb]])
            pb.dma("p", io["s0_f32"][i * 128:(i + 1) * 128, :], n1[jb][:],
                   group=f"s0_f32{jb}")

        def mixer_seq(s, filler, spill):
            for kc in range(DC):
                pb.dmat("q", xnfm[:, kc, :],
                        io["xn_bf"][s * T:(s + 1) * T, kc * 128:(kc + 1) * 128],
                        group="mx_xnfm", reads=[(io["xn_bf"], s)],
                        writes=[(xnfm, kc)])
            for m in range(DC):
                b = m % 2
                pb.dma("p", wzs[b][:], wzr[:, :, m * 128:(m + 1) * 128], group=f"mx_wz{b}")
                pb.dma("p", whs[b][:], whr[:, :, m * 128:(m + 1) * 128], group=f"mx_wh{b}")
                for kc in range(DC):
                    for n in range(4):
                        pb.op("t", lambda e, kc=kc, n=n, b=b: e.matmul(
                            psA[:, n * 512:(n + 1) * 512], wzs[b][:, kc, :],
                            xnfm[:, kc, n * 512:(n + 1) * 512],
                            start=(kc == 0), stop=(kc == DC - 1)),
                            reads=[wzs[b], (xnfm, kc)], writes=[(psA, n)])
                emit_spill(spill)
                for kc in range(DC):
                    for n in range(4):
                        pb.op("t", lambda e, kc=kc, n=n, b=b: e.matmul(
                            psB[:, n * 512:(n + 1) * 512], whs[b][:, kc, :],
                            xnfm[:, kc, n * 512:(n + 1) * 512],
                            start=(kc == 0), stop=(kc == DC - 1)),
                            reads=[whs[b], (xnfm, kc)], writes=[(psB, n)])
                pA = [(psA, n) for n in range(4)]
                pB = [(psB, n) for n in range(4)]
                pb.op("s", lambda e, m=m: e.activation(out=sc[:], in_=psA[:], func=AF.Sigmoid,
                                                       scale=-1.0, bias=snbz[:, m:m + 1]),
                      reads=pA + [snbz], writes=[sc])
                pb.op("s", lambda e, m=m: e.activation(out=sz[:], in_=psA[:], func=AF.Sigmoid,
                                                       bias=sbz[:, m:m + 1]),
                      reads=pA + [sbz], writes=[sz])
                pb.op("s", lambda e, m=m: e.activation(out=sr[:], in_=psB[:], func=AF.Identity,
                                                       bias=sbh05[:, m:m + 1]),
                      reads=pB + [sbh05], writes=[sr])
                pb.op("s", lambda e, m=m: e.activation(out=ss[:], in_=psB[:], func=AF.Sigmoid,
                                                       bias=sbh[:, m:m + 1]),
                      reads=pB + [sbh], writes=[ss])
                pb.op("v", lambda e: e.tensor_max(out=ss[:], in0=ss[:], in1=sr[:]),
                      reads=[ss, sr], writes=[ss])
                pb.op("v", lambda e: e.tensor_tensor(out=sz[:], in0=sz[:], in1=ss[:],
                                                     op=ALU.mult),
                      reads=[sz, ss], writes=[sz])
                pb.op("v", lambda e: e.tensor_tensor_scan(out=sh[:], data0=sc[:], data1=sz[:],
                                                          initial=0.0, op0=ALU.mult,
                                                          op1=ALU.add),
                      reads=[sc, sz], writes=[sh])
                for it in filler[m::DC]:
                    it()
                spill[0] = (s, m)

        def emit_spill(spill):
            if spill[0] is None:
                return
            s, m = spill[0]
            spill[0] = None
            for j in range(16):
                jb = j % 4
                pb.op("t", lambda e, j=j: e.transpose(psB[:, j * 128:(j + 1) * 128],
                                                      sh[:, j * 128:(j + 1) * 128], sid[:]),
                      reads=[sh, sid], writes=[(psB, j // 4)])
                pb.op("v", lambda e, j=j, jb=jb: e.tensor_copy(
                    out=hstg[jb][:], in_=psB[:, j * 128:(j + 1) * 128]),
                    reads=[(psB, j // 4)], writes=[hstg[jb]])
                pb.dma("p", io["h_dram"][(s * 16 + j) * 128:(s * 16 + j + 1) * 128,
                                         m * 128:(m + 1) * 128],
                       hstg[jb][:], group=f"h_dram{jb}",
                       reads=[hstg[jb]], writes=[(io["h_dram"], f"{s}.{j}")])

        spill = [None]
        pre0 = [(lambda i=i: pre_tile(i)) for i in range(16)]
        for it in pre0:
            it()
        fill0 = [(lambda i=i: pre_tile(i)) for i in range(16, 32)] \
            + [(lambda j=j: wfc_tile(j)) for j in range(8)]
        mixer_seq(0, fill0, spill)
        fill1 = [(lambda j=j: wfc_tile(j)) for j in range(8, 32)] \
            + [(lambda s=0, j=j: hln_tile(s, j)) for j in range(16)]
        mixer_seq(1, fill1, spill)
        emit_spill(spill)
        for j in range(32):
            wproj_tile(j)
        for j in range(16):
            hln_tile(1, j)


SCRATCH_SPECS = {
    "x": ([NTOK, D], F32),
    "xn_f32": ([NTOK, D], F32), "xn_bf": ([NTOK, D], BF16),
    "xq_bf": ([NTOK, D], BF16),
    "wqfc_bf": ([HID, D], BF16), "wqproj_bf": ([D, HID], BF16),
    "s0_f32": ([NTOK, D], F32),
    "x1_f32": ([NTOK, D], F32), "xn2_bf": ([NTOK, D], BF16),
    "s2_f32": ([NTOK, D], F32),
    "out": ([NTOK, D], F32),
    "h_dram": ([NTOK, D], F32),
    "hq_bf": ([NTOK, HID], BF16),
    "gx_dram": ([HC, 128, T], BF16),
}

LAUNCH_DEFS = {}


def build_launch(name, extra_core_id=None):
    ins, outs, internals, emit_fn = LAUNCH_DEFS[name]
    nc = bass.Bass("TRN2", target_bir_lowering=False)
    pb = PB(nc)
    io = {}
    for nm in ins:
        shape, dt = SHARED_SPECS.get(nm) or SCRATCH_SPECS[nm]
        io[nm] = pb.dram(nm, shape, dt, kind="ExternalInput")
    for nm in outs:
        shape, dt = SCRATCH_SPECS[nm]
        io[nm] = pb.dram(nm, shape, dt, kind="ExternalOutput")
    for nm in internals:
        shape, dt = SCRATCH_SPECS[nm]
        io[nm] = pb.dram(nm, shape, dt, kind="Internal")
    c = {"io": io}
    for v in (EPS, 3 * LN2, -3 * LN2, 1.0, 0.0):
        pb.cpp(v)
    ones = pb._stack.enter_context(pb.nc.sbuf_tensor("ones1_f32", [1, 128], F32))
    pb.op("v", lambda e: e.memset(ones[:], 1.0), writes=[ones])
    c["_ones1_f32"] = ones
    emit_fn(pb, c)
    pb.finish()
    return nc, pb




def emit_ffn0(pb, c):
    io = c["io"]
    with pb.scope():
        psF = [pb.ps(f"f0_psF{i}", [128, T], F32) for i in range(2)]
        c["_psb"] = psF[0]
        bfc_b = bcast_row(pb, c, "bfc_b", io["bfc_r"][:], HID, BF16)
        wqfcT = pb.sb("f0_wfcT", [128, DC, HID], BF16)
        for kc in range(DC):
            pb.dmat("q", wqfcT[:, kc, :], io["wqfc_bf"][:, kc * 128:(kc + 1) * 128],
                    group="f0_wfcT")
        xqfm = pb.sb("f0_xqfm", [128, DC, 1024], BF16)
        sth = [pb.sb(f"f0_sth{i}", [128, T], F32) for i in range(2)]
        shq = [pb.sb(f"f0_shq{i}", [128, T], BF16) for i in range(2)]
        for grp in range(4):
            for kc in range(DC):
                pb.dmat("q", xqfm[:, kc, :],
                        io["xq_bf"][grp * 1024:(grp + 1) * 1024, kc * 128:(kc + 1) * 128],
                        group="f0_xqfm")
            for tcl in range(8):
                tc = grp * 8 + tcl
                for hh in range(2):
                    b = (tcl * 2 + hh) % 2
                    for kc in range(DC):
                        for n in range(4):
                            pb.op("t", lambda e, kc=kc, n=n, hh=hh, tcl=tcl, b=b: e.matmul(
                                psF[b][:, n * 512:(n + 1) * 512],
                                xqfm[:, kc, tcl * 128:(tcl + 1) * 128],
                                wqfcT[:, kc, hh * 2048 + n * 512:hh * 2048 + (n + 1) * 512],
                                start=(kc == 0), stop=(kc == DC - 1)),
                                reads=[(xqfm, kc), wqfcT], writes=[(psF[b], n)])
                    pp = [(psF[b], n) for n in range(4)]
                    pb.op("v", lambda e, b=b, hh=hh: e.tensor_tensor(
                        out=sth[b][:], in0=psF[b][:], in1=bfc_b[:, hh * 2048:(hh + 1) * 2048],
                        op=ALU.add),
                        reads=pp + [bfc_b], writes=[sth[b]])
                    pb.op("s", lambda e, b=b: e.activation(out=sth[b][:], in_=sth[b][:],
                                                           func=AF.Gelu),
                          reads=[sth[b]], writes=[sth[b]])
                    quantize_tile(pb, c, sth[b][:], T, shq[b][:], f"f0_q{b}",
                                  [sth[b]], [shq[b]])
                    pb.dma("p", io["hq_bf"][tc * 128:(tc + 1) * 128,
                                            hh * 2048:(hh + 1) * 2048],
                           shq[b][:], group=f"hq_bf{b}")
    with pb.scope():
        psP = [pb.ps(f"f0_psP{i}", [128, D], F32) for i in range(2)]
        c["_psb"] = psP[0]
        bproj_b = bcast_row(pb, c, "bproj_b", io["bproj_r"][:], D, BF16)
        wqpT = pb.sb("f0_wpT", [128, HC, D], BF16)
        for kc in range(HC):
            pb.dmat("q", wqpT[:, kc, :], io["wqproj_bf"][:, kc * 128:(kc + 1) * 128],
                    group="f0_wpT")
        hqfm = pb.sb("f0_hqfm", [128, HC, 1024], BF16)
        sgl = [pb.sb(f"f0_sgl{i}", [128, D], F32) for i in range(2)]
        sx = [pb.sb(f"f0_sx{i}", [128, D], F32) for i in range(2)]
        ss0 = [pb.sb(f"f0_ss0{i}", [128, D], F32) for i in range(2)]
        sxnb = [pb.sb(f"f0_sxnb{i}", [128, D], BF16) for i in range(2)]
        for grp in range(4):
            for kc in range(HC):
                pb.dmat("q", hqfm[:, kc, :],
                        io["hq_bf"][grp * 1024:(grp + 1) * 1024, kc * 128:(kc + 1) * 128],
                        group="f0_hqfm")
            for tcl in range(8):
                tc = grp * 8 + tcl
                b = tcl % 2
                for kc in range(HC):
                    for n in range(2):
                        pb.op("t", lambda e, kc=kc, n=n, tcl=tcl, b=b: e.matmul(
                            psP[b][:, n * 512:(n + 1) * 512],
                            hqfm[:, kc, tcl * 128:(tcl + 1) * 128],
                            wqpT[:, kc, n * 512:(n + 1) * 512],
                            start=(kc == 0), stop=(kc == HC - 1)),
                            reads=[(hqfm, kc), wqpT], writes=[(psP[b], n)])
                pp = [(psP[b], n) for n in range(2)]
                pb.op("v", lambda e, b=b: e.tensor_tensor(out=sgl[b][:], in0=psP[b][:],
                                                          in1=bproj_b[:], op=ALU.add),
                      reads=pp + [bproj_b], writes=[sgl[b]])
                pb.op("s", lambda e, b=b: e.activation(out=sgl[b][:], in_=sgl[b][:],
                                                       func=AF.Silu),
                      reads=[sgl[b]], writes=[sgl[b]])
                pb.dma("q", sx[b][:], io["x"][tc * 128:(tc + 1) * 128, :], group=f"f0_x{b}")
                pb.dma("q", ss0[b][:], io["s0_f32"][tc * 128:(tc + 1) * 128, :],
                       group=f"f0_s0{b}")
                pb.op("v", lambda e, b=b: e.tensor_tensor(out=sgl[b][:], in0=sgl[b][:],
                                                          in1=ss0[b][:], op=ALU.mult),
                      reads=[sgl[b], ss0[b]], writes=[sgl[b]])
                pb.op("v", lambda e, b=b: e.tensor_tensor(out=sx[b][:], in0=sx[b][:],
                                                          in1=sgl[b][:], op=ALU.add),
                      reads=[sx[b], sgl[b]], writes=[sx[b]])
                pb.dma("p", io["x1_f32"][tc * 128:(tc + 1) * 128, :], sx[b][:],
                       group=f"x1_f32{b}")
                t = ln_stats(pb, c, sx[b][:], D, f"f0_ln{b}", [sx[b]])
                pb.op("s", lambda e, b=b, t=t: e.activation(out=sxnb[b][:], in_=sx[b][:],
                                                            func=AF.Identity,
                                                            scale=t["rstd"][:], bias=t["nmr"][:]),
                      reads=[sx[b], t["rstd"], t["nmr"]], writes=[sxnb[b]])
                pb.dma("p", io["xn2_bf"][tc * 128:(tc + 1) * 128, :], sxnb[b][:],
                       group=f"xn2_bf{b}")


LAUNCH_DEFS["premix"] = (
    ["x", "wz", "wh", "bz_t", "bh_t", "gamma_r", "beta_r", "ident_f", "wfc", "wproj"],
    ["xn_f32", "xn_bf", "xq_bf", "wqfc_bf", "wqproj_bf", "s0_f32"],
    ["h_dram"], emit_premix)


LAUNCH_DEFS["ffn0"] = (
    ["xq_bf", "wqfc_bf", "wqproj_bf", "bfc_r", "bproj_r", "s0_f32", "x"],
    ["x1_f32", "xn2_bf"], ["hq_bf"], emit_ffn0)


def emit_attn(pb, c):
    io = c["io"]
    STRIP_OFF = [0]
    for kc in range(16):
        STRIP_OFF.append(STRIP_OFF[-1] + (T - 128 * kc))
    NPT = STRIP_OFF[-1]

    with pb.scope():
        psB4 = pb.ps("at_psB4", [128, 2048], F32)
        psQ = pb.ps("at_psQ", [128, 1024], F32)
        psX = pb.ps("at_psX", [128, 1024], F32)
        c["_psb"] = psX
        btiles = pb.sb("at_bt", [128, NH * 2, 128], BF16)
        pb.dma("p", btiles[:], io["btiles"])
        srow = pb.sb("at_row", [1, NH * 33], F32)
        pb.dma("q", srow[:], io["rt_row"])
        srow3 = srow[:].rearrange("p (h u) -> p h u", u=33)
        bmaxh = pb.sb("at_bmaxh", [1, NH, 1], F32)
        pb.op("v", lambda e: e.tensor_reduce(out=bmaxh[:], in_=srow3[:, :, 0:17],
                                             axis=AX.X, op=ALU.max),
              reads=[srow], writes=[bmaxh])
        gbm = pb.sb("at_gbm", [1, 1], F32)
        pb.op("v", lambda e: e.tensor_reduce(
            out=gbm[:], in_=bmaxh[:].rearrange("p h u -> p (h u)"),
            axis=AX.X, op=ALU.max),
            reads=[bmaxh], writes=[gbm])
        onescol = pb.sb("at_onec", [128, 1], F32)
        pb.op("v", lambda e: e.memset(onescol[:], 1.0), writes=[onescol])
        bp_b = bcast_row(pb, c, "bp_b", io["bp_r"][:], D, F32)
        ones = c["_ones1_f32"]
        rel0bc = pb.sb("at_r0bc", [128, NH], F32)
        pb.op("t", lambda e: e.matmul(psX[:, 0:NH], ones[:], srow3[0:1, :, 0],
                                      start=True, stop=True),
              reads=[ones, srow], writes=[(psX, 0)])
        pb.op("v", lambda e: e.tensor_copy(out=rel0bc[:], in_=psX[:, 0:NH]),
              reads=[(psX, 0)], writes=[rel0bc])
        for h in range(NH):
            pb.op("v", lambda e, h=h: e.tensor_scalar(
                out=btiles[:, 2 * h:2 * h + 2, :], in0=btiles[:, 2 * h:2 * h + 2, :],
                scalar1=rel0bc[:, h:h + 1], scalar2=8.0,
                op0=ALU.subtract, op1=ALU.mult),
                reads=[(btiles, h), rel0bc], writes=[(btiles, h)])
        nrm = pb.sb("at_nrm", [1, 4], F32)
        u3 = pb.sb("at_u3", [1, 2], F32)
        rel0p = pb.sb("at_rel0", [1, 2], F32)
        upair = pb.sb("at_up", [1, 4], F32)
        mxq = pb.sb("at_mxq", [128, 1], F32)
        mxk = pb.sb("at_mxk", [128, 1], F32)
        rmt = pb.sb("at_rmt", [128, 1], F32)
        mx2 = pb.sb("at_mx2", [128, 1], F32)
        bU = [pb.sb(f"at_bU{i}", [128, 2], F32) for i in range(2)]
        linv = pb.sb("at_linv", [1, 2048], F32)
        linv2 = pb.sb("at_linv2", [1, 2048], F32)
        rb2 = pb.sb("at_rb2", [1, 2048], BF16)
        onesb = pb.sb("at_onesb", [1, 128], BF16)
        pb.op("v", lambda e: e.memset(onesb[:], 1.0), writes=[onesb])
        ss2 = [pb.sb(f"at_s2_{i}", [128, D], F32) for i in range(2)]
        vaug = pb.sb("at_v", [128, 16, NH * 65 + 63], BF16)
        o_fm = pb.sb("at_o", [128, DC, T], BF16)
        wqr3 = io["wqr"].rearrange("(kc p) o -> p kc o", p=128)
        wkr3 = io["wkr"].rearrange("(kc p) o -> p kc o", p=128)

        for s in range(NSEQ):
            with pb.scope():
                xn2fm = pb.sb(f"at_xn2{s}", [128, DC, T], BF16)
                for kc in range(DC):
                    pb.dmat("q", xn2fm[:, kc, :],
                            io["xn2_bf"][s * T:(s + 1) * T, kc * 128:(kc + 1) * 128],
                            group="at_xn2")
                with pb.scope():
                    wv_b = pb.sb(f"at_wv{s}", [128, DC, D], BF16)
                    pb.dma("p", wv_b[:],
                           io["wvr"].rearrange("(kc p) o -> p kc o", p=128),
                           group="at_wv")
                    pb.op("v", lambda e: e.memset(
                        vaug[:, :, 0:NH * 65].rearrange("p t (h w) -> p t h w", w=65)[:, :, :, 64:65], 1.0),
                        writes=[(vaug, "ones")])
                    pb.op("v", lambda e: e.memset(vaug[:, :, NH * 65:], 0.0),
                          writes=[(vaug, "pad")])
                    for tc in range(16):
                        for kc in range(DC):
                            for n in range(2):
                                pb.op("t", lambda e, tc=tc, kc=kc, n=n: e.matmul(
                                    psX[:, n * 512:(n + 1) * 512],
                                    xn2fm[:, kc, tc * 128:(tc + 1) * 128],
                                    wv_b[:, kc, n * 512:(n + 1) * 512],
                                    start=(kc == 0), stop=(kc == DC - 1)),
                                    reads=[(xn2fm, kc), wv_b], writes=[(psX, n)])
                        pb.op("s", lambda e, tc=tc: e.activation(
                            out=vaug[:, tc, 0:NH * 65].rearrange("p (h w) -> p h w", w=65)[:, :, 0:64],
                            in_=psX[:, 0:1024], func=AF.Copy),
                            reads=[(psX, 0), (psX, 1)], writes=[(vaug, tc)])
                pT = pb.sb(f"at_pT{s}", [128, NPT], BF16)
                qh2 = [pb.sb(f"at_q{s}{i}", [128, T], BF16) for i in range(2)]
                kh2 = [pb.sb(f"at_k{s}{i}", [128, T], BF16) for i in range(2)]
                wq2 = [pb.sb(f"at_wq{s}{i}", [128, DC, 128], BF16) for i in range(2)]
                wk2 = [pb.sb(f"at_wk{s}{i}", [128, DC, 128], BF16) for i in range(2)]
                for hc2 in range(8):
                    b = hc2 % 2
                    pb.dma("p", wq2[b][:], wqr3[:, :, hc2 * 128:(hc2 + 1) * 128],
                           group=f"at_wq{b}")
                    pb.dma("p", wk2[b][:], wkr3[:, :, hc2 * 128:(hc2 + 1) * 128],
                           group=f"at_wk{b}")
                    qbank = lambda n: (psQ[:, (n % 2) * 512:(n % 2 + 1) * 512], (psQ, n % 2)) \
                        if n < 2 else (psX[:, (n % 2) * 512:(n % 2 + 1) * 512], (psX, n % 2))
                    for (w_b, dst), mx in (((wq2[b], qh2[b]), mxq), ((wk2[b], kh2[b]), mxk)):
                        for kc in range(DC):
                            for n in range(4):
                                ap, tag = qbank(n)
                                pb.op("t", lambda e, w_b=w_b, kc=kc, n=n, ap=ap: e.matmul(
                                    ap, w_b[:, kc, :],
                                    xn2fm[:, kc, n * 512:(n + 1) * 512],
                                    start=(kc == 0), stop=(kc == DC - 1)),
                                    reads=[w_b, (xn2fm, kc)], writes=[tag])
                        for n in range(4):
                            ap, tag = qbank(n)
                            pb.op("s", lambda e, dst=dst, n=n, ap=ap: e.activation(
                                out=dst[:, n * 512:(n + 1) * 512], in_=ap, func=AF.Copy),
                                reads=[tag], writes=[(dst, n)])
                            if n == 0:
                                pb.op("v", lambda e, dst=dst, n=n, mx=mx: e.tensor_reduce(
                                    out=mx[:], in_=dst[:, n * 512:(n + 1) * 512],
                                    axis=AX.X, op=ALU.max, apply_absolute_value=True),
                                    reads=[(dst, n)], writes=[mx])
                            else:
                                pb.op("v", lambda e, dst=dst, n=n: e.tensor_reduce(
                                    out=rmt[:], in_=dst[:, n * 512:(n + 1) * 512],
                                    axis=AX.X, op=ALU.max, apply_absolute_value=True),
                                    reads=[(dst, n)], writes=[rmt])
                                pb.op("v", lambda e, mx=mx: e.tensor_max(
                                    out=mx[:], in0=mx[:], in1=rmt[:]),
                                    reads=[mx, rmt], writes=[mx])
                    for qi, mx in enumerate((mxq, mxk)):
                        pb.op("v", lambda e, mx=mx: e.tensor_tensor(out=mx2[:], in0=mx[:],
                                                                    in1=mx[:], op=ALU.mult),
                              reads=[mx], writes=[mx2])
                        for hh in range(2):
                            pb.op("t", lambda e, hh=hh: e.matmul(
                                psX[0:1, 0:1], onescol[hh * 64:(hh + 1) * 64, :],
                                mx2[hh * 64:(hh + 1) * 64, :], start=True, stop=True),
                                reads=[onescol, mx2], writes=[(psX, 0)])
                            pb.op("v", lambda e, qi=qi, hh=hh: e.tensor_copy(
                                out=nrm[0:1, qi * 2 + hh:qi * 2 + hh + 1], in_=psX[0:1, 0:1]),
                                reads=[(psX, 0)], writes=[(nrm, qi * 2 + hh)])
                    nr = [(nrm, i) for i in range(4)]
                    pb.op("v", lambda e: e.tensor_tensor(out=u3[:], in0=nrm[0:1, 0:2],
                                                         in1=nrm[0:1, 2:4], op=ALU.mult),
                          reads=nr, writes=[u3])
                    pb.op("v", lambda e: e.tensor_scalar_max(out=u3[:], in0=u3[:],
                                                             scalar1=1e-30),
                          reads=[u3], writes=[u3])
                    pb.op("s", lambda e: e.activation(out=u3[:], in_=u3[:], func=AF.Ln),
                          reads=[u3], writes=[u3])
                    pb.op("s", lambda e: e.activation(out=u3[:], in_=u3[:], func=AF.Exp,
                                                      scale=0.5),
                          reads=[u3], writes=[u3])
                    pb.op("v", lambda e: e.scalar_tensor_tensor(
                        out=u3[:], in0=u3[:], scalar=0.1275,
                        in1=gbm[0:1, 0:1].broadcast_to([1, 2]), op0=ALU.mult, op1=ALU.add),
                        reads=[u3, gbm], writes=[u3])
                    pb.op("v", lambda e, hc2=hc2: e.tensor_copy(
                        out=rel0p[:], in_=srow3[0:1, 2 * hc2:2 * hc2 + 2, 0]),
                        reads=[srow], writes=[rel0p])
                    pb.op("v", lambda e: e.tensor_tensor(
                        out=upair[0:1, 0:2], in0=rel0p[:], in1=u3[:], op=ALU.subtract),
                        reads=[rel0p, u3], writes=[(upair, 0)])
                    pb.op("t", lambda e: e.matmul(psX[:, 0:2], ones[:], upair[0:1, 0:2],
                                                  start=True, stop=True),
                          reads=[ones, (upair, 0)], writes=[(psX, 0)])
                    pb.op("v", lambda e, b=b: e.tensor_copy(out=bU[b][:], in_=psX[:, 0:2]),
                          reads=[(psX, 0)], writes=[bU[b]])
                    for hh in range(2):
                        h = hc2 * 2 + hh
                        qh_ap = qh2[b][hh * 64:(hh + 1) * 64, :]
                        kh_ap = kh2[b][hh * 64:(hh + 1) * 64, :]
                        qreads = [(qh2[b], n) for n in range(4)]
                        kreads = [(kh2[b], n) for n in range(4)]
                        ci = 0
                        for kc in range(16):
                            N = T - 128 * kc
                            offp = STRIP_OFF[kc]
                            for r0 in range(0, N, 1024):
                                reg = ci % 2
                                ci += 1
                                w = min(1024, N - r0)
                                for n0 in range(0, w, 512):
                                    wn = min(512, w - n0)
                                    pb.op("t", lambda e, kc=kc, r0=r0, n0=n0, wn=wn, reg=reg,
                                          qh_ap=qh_ap, kh_ap=kh_ap: e.matmul(
                                        psB4[:, reg * 1024 + n0:reg * 1024 + n0 + wn],
                                        kh_ap[:, kc * 128:(kc + 1) * 128],
                                        qh_ap[:, kc * 128 + r0 + n0:kc * 128 + r0 + n0 + wn],
                                        start=True, stop=True),
                                        reads=qreads + kreads,
                                        writes=[(psB4, f"b{reg * 2 + n0 // 512}")])
                                sreads = [(psB4, f"b{reg * 2 + n0 // 512}") for n0 in range(0, w, 512)]
                                if r0 == 0:
                                    nw = 128 if kc == 15 else 256
                                    pb.op("v", lambda e, reg=reg, h=h, nw=nw: e.tensor_tensor(
                                        out=psB4[:, reg * 1024:reg * 1024 + nw],
                                        in0=psB4[:, reg * 1024:reg * 1024 + nw],
                                        in1=btiles[:, 2 * h:2 * h + 2, :].rearrange(
                                            "p b j -> p (b j)")[:, 0:nw], op=ALU.add),
                                        reads=sreads[0:1] + [(btiles, h)],
                                        writes=[(psB4, f"b{reg * 2}")])
                                pb.op("s", lambda e, offp=offp, r0=r0, w=w, reg=reg, b=b,
                                      hh=hh: e.activation(
                                    out=pT[:, offp + r0:offp + r0 + w],
                                    in_=psB4[:, reg * 1024:reg * 1024 + w], func=AF.Exp,
                                    scale=SCALE_ATT, bias=bU[b][:, hh:hh + 1]),
                                    reads=sreads + [bU[b]], writes=[(pT, kc)])
                        for kc in range(16):
                            for bk in range(max(kc // 4, 0), 4):
                                b0 = bk * 512
                                b1 = b0 + 512
                                s0 = max(b0, kc * 128)
                                last_kc = min(15, (b1 - 1) // 128)
                                pb.op("t", lambda e, kc=kc, s0=s0, b1=b1, h=h,
                                      last_kc=last_kc: e.matmul(
                                    psB4[:, s0:b1],
                                    vaug[:, kc, h * 65:h * 65 + 128],
                                    pT[:, STRIP_OFF[kc] + s0 - kc * 128:
                                        STRIP_OFF[kc] + b1 - kc * 128],
                                    start=(kc == 0), stop=(kc == last_kc)),
                                    reads=[(vaug, kc), (vaug, "ones"), (vaug, "pad"), (pT, kc)],
                                    writes=[(psB4, f"b{bk}")])
                        avr = [(psB4, f"b{bk}") for bk in range(4)]
                        pb.op("s", lambda e, hh=hh, hc2=hc2: e.activation(
                            out=o_fm[hh * 64:(hh + 1) * 64, hc2, :],
                            in_=psB4[0:64, :], func=AF.Copy),
                            reads=avr, writes=[(o_fm, f"{hh}.{hc2}")])
                        pb.op("v", lambda e: e.tensor_scalar_max(
                            out=linv[:], in0=psB4[64:65, :], scalar1=1e-30),
                            reads=avr, writes=[(linv, 0), (linv, 1)])
                        pb.op("v", lambda e: e.reciprocal(out=linv2[:], in_=linv[:]),
                              reads=[(linv, 0), (linv, 1)], writes=[linv2])
                        pb.op("v", lambda e: e.tensor_copy(out=rb2[:], in_=linv2[:]),
                              reads=[linv2], writes=[rb2])
                        for n4 in range(4):
                            pb.op("t", lambda e, n4=n4: e.matmul(
                                psX[0:64, (n4 % 2) * 512:(n4 % 2 + 1) * 512], onesb[0:1, 0:64],
                                rb2[0:1, n4 * 512:(n4 + 1) * 512],
                                start=True, stop=True),
                                reads=[onesb, rb2], writes=[(psX, n4 % 2)])
                            pb.op("v", lambda e, hh=hh, hc2=hc2, n4=n4: e.tensor_tensor(
                                out=o_fm[hh * 64:(hh + 1) * 64, hc2,
                                         n4 * 512:(n4 + 1) * 512],
                                in0=o_fm[hh * 64:(hh + 1) * 64, hc2,
                                         n4 * 512:(n4 + 1) * 512],
                                in1=psX[0:64, (n4 % 2) * 512:(n4 % 2 + 1) * 512], op=ALU.mult),
                                reads=[(psX, n4 % 2), (o_fm, f"{hh}.{hc2}")],
                                writes=[(o_fm, f"{hh}.{hc2}")])
            with pb.scope():
                wp_b = pb.sb(f"at_wp{s}", [128, DC, D], BF16)
                pb.dma("p", wp_b[:], io["wp"].rearrange("(kc p) o -> p kc o", p=128),
                       group="at_wp")
                for tc in range(16):
                    b = tc % 2
                    for kc in range(DC):
                        for n in range(2):
                            pb.op("t", lambda e, tc=tc, kc=kc, n=n: e.matmul(
                                psX[:, n * 512:(n + 1) * 512],
                                o_fm[:, kc, tc * 128:(tc + 1) * 128],
                                wp_b[:, kc, n * 512:(n + 1) * 512],
                                start=(kc == 0), stop=(kc == DC - 1)),
                                reads=[(o_fm, f"0.{kc}"), (o_fm, f"1.{kc}"), wp_b],
                                writes=[(psX, n)])
                    pb.op("v", lambda e, b=b: e.tensor_tensor(
                        out=ss2[b][:], in0=psX[:, 0:1024], in1=bp_b[:], op=ALU.add),
                        reads=[(psX, 0), (psX, 1), bp_b], writes=[ss2[b]])
                    pb.dma("p", io["s2_f32"][(s * 16 + tc) * 128:(s * 16 + tc + 1) * 128, :],
                           ss2[b][:], group=f"s2_f32{b}")


LAUNCH_DEFS["attn"] = (
    ["xn2_bf", "wqr", "wkr", "wvr", "wp", "bp_r", "rt_row", "btiles"],
    ["s2_f32"], [], emit_attn)


def emit_ffn1(pb, c):
    io = c["io"]
    with pb.scope():
        scl = {}
        for nm in ("s1c", "s2c", "sh1c", "sh2c"):
            scl[nm] = pb.sb("f1_" + nm, [128, HC], F32)
            pb.dma("q", scl[nm][:], io[nm], group="f1_" + nm)
        for s in range(NSEQ):
            with pb.scope():
                h_fm = pb.sb(f"f1_h{s}", [128, HC, T], BF16)
                with pb.scope():
                    psT1 = pb.ps(f"f1_psT1{s}", [128, T], F32)
                    psT2 = pb.ps(f"f1_psT2{s}", [128, T], F32)
                    xn2 = pb.sb(f"f1_xn2{s}", [128, DC, T], BF16)
                    w1s = [pb.sb(f"f1_w1s{s}{i}", [128, DC, 128], BF16) for i in range(2)]
                    w2s = [pb.sb(f"f1_w2s{s}{i}", [128, DC, 128], BF16) for i in range(2)]
                    sa = pb.sb(f"f1_sa{s}", [128, T], F32)
                    w1r = io["w1t"].rearrange("m kc k o -> m k kc o")
                    w2r = io["w2t"].rearrange("m kc k o -> m k kc o")
                    for kc in range(DC):
                        pb.dmat("q", xn2[:, kc, :],
                                io["xn2_bf"][s * T:(s + 1) * T,
                                             kc * 128:(kc + 1) * 128],
                                group="f1_xn2")
                    for m in range(HC):
                        b = m % 2
                        pb.dma("p", w1s[b][:], w1r[m], group=f"f1_w1{b}")
                        pb.dma("p", w2s[b][:], w2r[m], group=f"f1_w2{b}")
                        for kc in range(DC):
                            for n in range(4):
                                pb.op("t", lambda e, kc=kc, n=n, b=b: e.matmul(
                                    psT1[:, n * 512:(n + 1) * 512], w1s[b][:, kc, :],
                                    xn2[:, kc, n * 512:(n + 1) * 512],
                                    start=(kc == 0), stop=(kc == DC - 1)),
                                    reads=[w1s[b], (xn2, kc)], writes=[(psT1, n)])
                        for kc in range(DC):
                            for n in range(4):
                                pb.op("t", lambda e, kc=kc, n=n, b=b: e.matmul(
                                    psT2[:, n * 512:(n + 1) * 512], w2s[b][:, kc, :],
                                    xn2[:, kc, n * 512:(n + 1) * 512],
                                    start=(kc == 0), stop=(kc == DC - 1)),
                                    reads=[w2s[b], (xn2, kc)], writes=[(psT2, n)])
                        p1 = [(psT1, n) for n in range(4)]
                        p2 = [(psT2, n) for n in range(4)]
                        pb.op("s", lambda e, m=m: e.activation(
                            out=sa[:], in_=psT1[:], func=AF.Silu,
                            scale=scl["s1c"][:, m:m + 1]),
                            reads=p1 + [scl["s1c"]], writes=[sa])
                        pb.op("v", lambda e, m=m: e.scalar_tensor_tensor(
                            out=h_fm[:, m, :], in0=psT2[:],
                            scalar=scl["s2c"][:, m:m + 1], in1=sa[:],
                            op0=ALU.mult, op1=ALU.mult),
                            reads=p2 + [scl["s2c"], sa], writes=[(h_fm, m)])
                psU = [pb.ps(f"f1_psU{s}{i}", [128, T], F32) for i in range(2)]
                for ui, (wt, sc) in enumerate((("h1t", "sh1c"), ("h2t", "sh2c"))):
                    stU = [pb.sb(f"f1_u{ui}s{s}{i}", [128, HC, 128], BF16) for i in range(2)]
                    gout = [pb.sb(f"f1_g{ui}o{s}{i}", [128, T], BF16) for i in range(2)]
                    wr = io[wt].rearrange("m kc k o -> m k kc o")
                    for m in range(HC):
                        b = m % 2
                        pb.dma("p", stU[b][:], wr[m], group=f"f1_u{ui}w{b}")
                        for kc in range(HC):
                            for n in range(4):
                                pb.op("t", lambda e, kc=kc, n=n, b=b: e.matmul(
                                    psU[b][:, n * 512:(n + 1) * 512], stU[b][:, kc, :],
                                    h_fm[:, kc, n * 512:(n + 1) * 512],
                                    start=(kc == 0), stop=(kc == HC - 1)),
                                    reads=[stU[b], h_fm], writes=[(psU[b], n)])
                        pu = [(psU[b], n) for n in range(4)]
                        pb.op("s", lambda e, m=m, b=b, sc=sc: e.activation(
                            out=gout[b][:], in_=psU[b][:], func=AF.Gelu,
                            scale=scl[sc][:, m:m + 1]),
                            reads=pu + [scl[sc]], writes=[gout[b]])
                        pb.dma("p", io["gx_dram"][m], gout[b][:], group=f"f1_gx{b}")
                    for m in range(HC):
                        pb.dma("q", h_fm[:, m, :], io["gx_dram"][m],
                               group="f1_hre", reads=[("gx_dram", m)],
                               writes=[(h_fm, m)])
                    if ui == 0:
                        continue
            with pb.scope():
                w3sb = pb.sb(f"f1_w3{s}", [128, HC, D], BF16)
                w3r = io["w3t"].rearrange("(kc) k o -> k kc o")
                for q in range(4):
                    pb.dma("p", w3sb[:, q * 8:(q + 1) * 8, :], w3r[:, q * 8:(q + 1) * 8, :],
                           group="f1_w3")
                psW = [pb.ps(f"f1_psW{s}{i}", [128, 1024], F32) for i in range(2)]
                c["_psb"] = psW[0]
                s3b = bcast_row(pb, c, f"s3b{s}", io["s3_r"][:], D, F32)
                for kc in range(HC):
                    pb.op("v", lambda e, kc=kc: e.tensor_tensor(
                        out=w3sb[:, kc, :], in0=w3sb[:, kc, :], in1=s3b[:], op=ALU.mult),
                        reads=[(w3sb, kc), s3b], writes=[(w3sb, kc)])
                gl = [pb.sb(f"f1_gl{s}{i}", [128, HC, 128], BF16) for i in range(2)]
                sgf = [pb.sb(f"f1_sgf{s}{i}", [128, D], F32) for i in range(2)]
                sx1 = [pb.sb(f"f1_sx1{s}{i}", [128, D], F32) for i in range(2)]
                ss2 = [pb.sb(f"f1_ss2{s}{i}", [128, D], F32) for i in range(2)]
                gxr = io["gx_dram"]
                for tc in range(16):
                    b = tc % 2
                    pb.dma("q", gl[b][:], gxr[:, :, tc * 128:(tc + 1) * 128].rearrange(
                        "kc k t -> k kc t"), group=f"f1_gl{b}", reads=[("gx_dram", "")])
                    for kc in range(HC):
                        for n in range(2):
                            pb.op("t", lambda e, kc=kc, n=n, b=b: e.matmul(
                                psW[b][:, n * 512:(n + 1) * 512], gl[b][:, kc, :],
                                w3sb[:, kc, n * 512:(n + 1) * 512],
                                start=(kc == 0), stop=(kc == HC - 1)),
                                reads=[gl[b], (w3sb, kc)], writes=[(psW[b], n)])
                    pw = [(psW[b], n) for n in range(2)]
                    pb.op("s", lambda e, b=b: e.activation(out=sgf[b][:], in_=psW[b][:],
                                                           func=AF.Silu),
                          reads=pw, writes=[sgf[b]])
                    i = s * 16 + tc
                    pb.dma("q", sx1[b][:], io["x1_f32"][i * 128:(i + 1) * 128, :],
                           group=f"f1_x1{b}")
                    pb.dma("q", ss2[b][:], io["s2_f32"][i * 128:(i + 1) * 128, :],
                           group=f"f1_s2{b}")
                    pb.op("v", lambda e, b=b: e.tensor_tensor(
                        out=sgf[b][:], in0=sgf[b][:], in1=ss2[b][:], op=ALU.mult),
                        reads=[sgf[b], ss2[b]], writes=[sgf[b]])
                    pb.op("v", lambda e, b=b: e.tensor_tensor(
                        out=sx1[b][:], in0=sx1[b][:], in1=sgf[b][:], op=ALU.add),
                        reads=[sx1[b], sgf[b]], writes=[sx1[b]])
                    pb.dma("p", io["out"][i * 128:(i + 1) * 128, :], sx1[b][:],
                           group=f"out{b}")


LAUNCH_DEFS["ffn1"] = (
    ["xn2_bf", "w1t", "w2t", "h1t", "h2t", "w3t", "s1c", "s2c", "sh1c", "sh2c",
     "s3_r", "x1_f32", "s2_f32"],
    ["out"], ["gx_dram"], emit_ffn1)


from concourse.bass_utils import run_bass_kernel_spmd

N_CORES = 8
LAUNCH_ORDER = ["premix", "ffn0", "attn", "ffn1"]


def build_fused():
    nc = bass.Bass("TRN2", target_bir_lowering=False)
    pb = PB(nc)
    io = {}
    for nm, (shape, dt) in SHARED_SPECS.items():
        io[nm] = pb.dram(nm, shape, dt, kind="ExternalInput")
    io["x"] = pb.dram("x", list(SCRATCH_SPECS["x"][0]), F32, kind="ExternalInput")
    io["out"] = pb.dram("out", list(SCRATCH_SPECS["out"][0]), F32,
                        kind="ExternalOutput")
    for nm, (shape, dt) in SCRATCH_SPECS.items():
        if nm in ("x", "out"):
            continue
        io[nm] = pb.dram(nm, shape, dt, kind="Internal")
    c = {"io": io}
    for v in (EPS, 3 * LN2, -3 * LN2, 1.0, 0.0):
        pb.cpp(v)
    ones = pb._stack.enter_context(pb.nc.sbuf_tensor("ones1_f32", [1, 128], F32))
    pb.op("v", lambda e: e.memset(ones[:], 1.0), writes=[ones])
    c["_ones1_f32"] = ones
    for name in LAUNCH_ORDER:
        LAUNCH_DEFS[name][3](pb, c)
    pb.finish()
    return nc, pb


def kernel(**inputs):
    shared = host_prep_shared(inputs)
    cores = [host_prep_core(inputs, ci) for ci in range(N_CORES)]
    nc, pb = build_fused()
    in_maps = [dict(shared, **cores[ci]) for ci in range(N_CORES)]
    res = run_bass_kernel_spmd(nc, in_maps, core_ids=list(range(N_CORES)))
    out = np.concatenate(
        [np.asarray(res.results[ci]["out"], np.float32).reshape(2, T, D)
         for ci in range(N_CORES)], axis=0)
    pb.close()
    return out


# version 15
# speedup vs baseline: 1.2553x; 1.0722x over previous
import contextlib
import numpy as np
import concourse.bass as bass
import concourse.mybir as mybir

F32 = mybir.dt.float32
BF16 = mybir.dt.bfloat16
I32 = mybir.dt.int32
AF = mybir.ActivationFunctionType
ALU = mybir.AluOpType
AX = mybir.AxisListType

ENGINES = {"t": "tensor", "v": "vector", "s": "scalar", "p": "gpsimd", "q": "sync"}


def _base(x):
    if hasattr(x, "tensor"):
        return x.tensor.name
    if hasattr(x, "name"):
        return x.name
    return str(x)


def _key(x):
    if isinstance(x, tuple):
        return (_base(x[0]), str(x[1]))
    return (_base(x), "")


class PB:
    def __init__(self, nc: bass.Bass):
        self.nc = nc
        self._stack = contextlib.ExitStack()
        self.sem = {}
        self.cnt = {}
        self.waited = {}
        self.reg = {}
        self.sb_bytes = 0
        self.n_ops = 0
        for k in ENGINES:
            self.sem[k] = self._stack.enter_context(nc.semaphore("sem_" + k))
            self.cnt[k] = 0

    def sb(self, name, shape, dtype):
        st = self._scopes[-1] if getattr(self, "_scopes", None) else self._stack
        t = st.enter_context(self.nc.sbuf_tensor(name, list(shape), dtype))
        self.sb_bytes += int(np.prod(shape[1:])) * np.dtype(mybir.dt.np(dtype)).itemsize
        return t

    def ps(self, name, shape, dtype=F32):
        st = self._scopes[-1] if getattr(self, "_scopes", None) else self._stack
        return st.enter_context(self.nc.psum_tensor(name, list(shape), dtype))

    @contextlib.contextmanager
    def scope(self):
        if not hasattr(self, "_scopes"):
            self._scopes = []
        st = contextlib.ExitStack()
        self._scopes.append(st)
        try:
            yield
        finally:
            self._scopes.pop()
            st.close()
            self.barrier()

    def barrier(self):
        need = {k: v for k, v in self.cnt.items() if v > 0}
        for eng in ENGINES:
            self._wait(eng, dict(need))
        self.reg.clear()

    def dram(self, name, shape, dtype, kind="Internal"):
        return self.nc.dram_tensor(name, list(shape), dtype, kind=kind).ap()

    def close(self):
        self._stack.close()

    def _group_sem(self, group):
        k = "g:" + group
        if k not in self.sem:
            self.sem[k] = self._stack.enter_context(self.nc.semaphore("sg_" + group))
            self.cnt[k] = 0
        return k

    def _entries(self, tensor, sub, for_write):
        d = self.reg.get(tensor)
        if d is None:
            return []
        if sub == "":
            return list(d.values())
        out = []
        if "" in d:
            out.append(d[""])
        if sub in d:
            out.append(d[sub])
        return out

    def _wait(self, eng, need):
        e = getattr(self.nc, ENGINES[eng])
        for semkey, val in sorted(need.items()):
            if val <= 0:
                continue
            if semkey.startswith("g:"):
                val = self.cnt[semkey]
            if self.waited.get((eng, semkey), -1) >= val:
                continue
            e.wait_ge(self.sem[semkey], val)
            self.waited[(eng, semkey)] = val

    def op(self, eng, fn, reads=(), writes=(), dma_group=None):
        reads = [_key(x) for x in reads]
        writes = [_key(x) for x in writes]
        need = {}

        def add(sv):
            if sv and need.get(sv[0], -1) < sv[1]:
                need[sv[0]] = sv[1]

        for t, s in reads:
            for ent in self._entries(t, s, False):
                add(ent.get("w"))
        for t, s in writes:
            for ent in self._entries(t, s, True):
                add(ent.get("w"))
                for sk, v in ent.get("r", {}).items():
                    add((sk, v))
        self._wait(eng, need)
        inst = fn(getattr(self.nc, ENGINES[eng]))
        if dma_group is not None:
            semkey = self._group_sem(dma_group)
            inc = 16
        else:
            semkey = eng
            inc = 1
        self.cnt[semkey] += inc
        inst.then_inc(self.sem[semkey], inc)
        v = self.cnt[semkey]
        self.n_ops += 1
        for t, s in writes:
            d = self.reg.setdefault(t, {})
            if s == "":
                d.clear()
            ent = d.setdefault(s, {})
            ent["w"] = (semkey, v)
            ent["r"] = {}
        for t, s in reads:
            ent = self.reg.setdefault(t, {}).setdefault(s, {})
            r = ent.setdefault("r", {})
            if r.get(semkey, -1) < v:
                r[semkey] = v
        return inst

    def dma(self, eng, out, in_, group=None, reads=(), writes=(), **kw):
        g = group or ("b_" + _base(out))
        return self.op(eng, lambda e: e.dma_start(out=out, in_=in_, **kw),
                       reads=list(reads) or [in_], writes=list(writes) or [out],
                       dma_group=g)

    def dmat(self, eng, out, in_, group=None, reads=(), writes=()):
        g = group or ("b_" + _base(out))
        return self.op(eng, lambda e: e.dma_start_transpose(out=out, in_=in_),
                       reads=list(reads) or [in_], writes=list(writes) or [out],
                       dma_group=g)

    def cpp(self, val, dtype=F32):
        if not hasattr(self, "_CPP"):
            self._CPP = {}
        key = (float(val), dtype)
        if key not in self._CPP:
            t = self._stack.enter_context(
                self.nc.sbuf_tensor(f"cpp_{len(self._CPP)}", [128, 1], dtype))
            self.op("v", lambda e: e.memset(t[:], float(val)), writes=[t])
            self._CPP[key] = t
        return self._CPP[key][:]

    def finish(self):
        need = {k: v for k, v in self.cnt.items() if v > 0}
        self._wait("q", need)


def simrun(nc, inputs, outs):
    from concourse.bass_interp import CoreSim
    nc.finalize()
    sim = CoreSim(nc)
    for k, v in inputs.items():
        sim.tensor(k)[:] = v
    sim.simulate(check_with_hw=False)
    return {k: np.array(sim.mem_tensor(k)) for k in outs}


"""Emitters for the mk_recurrent_d2 megakernel. Per-core program pieces.

Per core: 2 sequences x 2048 tokens x 1024 features.
Layouts: TM = [token part, feat free]; FM = [feat part, token free].
All matmuls bf16 operands / fp32 PSUM accumulation; elementwise fp32.
"""
import numpy as np
import ml_dtypes

D = 1024
T = 2048
NSEQ = 2
NTOK = 4096
HID = 4096
NH = 16
DH = 64
EPS = 1e-5
LN2 = float(np.log(2.0))
BIGNEG8 = -8.0e33
DC = D // 128
TC = NTOK // 128
HC = HID // 128
SCALE_ATT = 0.125

BF = ml_dtypes.bfloat16


def host_prep_shared(inp):
    c = lambda a: np.ascontiguousarray(a)
    d = {}
    d["wz"] = c(inp["L0_attn_Wz"])
    d["wh"] = c(inp["L0_attn_Wh"])
    d["bz_t"] = c(np.asarray(inp["L0_attn_bz"]).reshape(DC, 128).T)
    d["bh_t"] = c(np.asarray(inp["L0_attn_bh"]).reshape(DC, 128).T)
    d["gamma_r"] = c(np.asarray(inp["L0_attn_gamma"]).reshape(1, D))
    d["beta_r"] = c(np.asarray(inp["L0_attn_beta"]).reshape(1, D))
    d["wfc"] = c(inp["L0_ffn_w_fc"])
    d["bfc_r"] = c(np.asarray(inp["L0_ffn_b_fc"]).reshape(1, HID))
    d["wproj"] = c(inp["L0_ffn_w_proj"])
    d["bproj_r"] = c(np.asarray(inp["L0_ffn_b_proj"]).reshape(1, D))
    for nm, key in [("wqr", "L1_attn_Wq"), ("wkr", "L1_attn_Wk"), ("wvr", "L1_attn_Wv")]:
        d[nm] = c(np.asarray(inp[key]).transpose(1, 0, 2).reshape(D, NH * DH))
    d["wp"] = c(inp["L1_attn_Wp"])
    d["bp_r"] = c(np.asarray(inp["L1_attn_bp"]).reshape(1, D))
    rt = np.asarray(inp["L1_attn_rel_table"])
    d["rt_row"] = c(rt.T.reshape(1, NH * 33))
    i_ = np.arange(128)[:, None]
    j_ = np.arange(128)[None, :]
    bucket_diag = np.clip(i_ - j_ + 16, 0, 32)
    bucket_sub = np.maximum(i_ - j_ - 112, 0)
    bt = np.empty((128, NH * 2, 128), np.float32)
    for h in range(NH):
        t0 = rt[bucket_diag, h]
        bt[:, 2 * h, :] = np.where(i_ <= j_, t0, np.float32(-1e30))
        bt[:, 2 * h + 1, :] = rt[bucket_sub, h]
    d["btiles"] = c(bt)
    def tile_T(w, mchunks, kchunks):
        wt = np.ascontiguousarray(w.T)
        K, O = wt.shape
        return c(wt.reshape(kchunks, 128, mchunks, 128).transpose(2, 0, 1, 3))
    d["w1t"] = tile_T(np.asarray(inp["L1_ffn_w1_w"]), HC, DC)
    d["w2t"] = tile_T(np.asarray(inp["L1_ffn_w2_w"]), HC, DC)
    d["h1t"] = tile_T(np.asarray(inp["L1_ffn_h1_w"]), HC, HC)
    d["h2t"] = tile_T(np.asarray(inp["L1_ffn_h2_w"]), HC, HC)
    w3t = np.ascontiguousarray(np.asarray(inp["L1_ffn_w3_w"]).T)
    d["w3t"] = c(w3t.reshape(HC, 128, D))
    for nm, key in [("s1c", "L1_ffn_w1_s"), ("s2c", "L1_ffn_w2_s"),
                    ("sh1c", "L1_ffn_h1_s"), ("sh2c", "L1_ffn_h2_s")]:
        d[nm] = c(np.asarray(inp[key]).reshape(HC, 128).T)
    d["s3_r"] = c(np.asarray(inp["L1_ffn_w3_s"]).reshape(1, D))
    d["ident_f"] = np.eye(128, dtype=np.float32)
    d["ident_b"] = np.eye(128, dtype=BF)
    return d


def host_prep_core(inp, core):
    x = np.asarray(inp["L0_attn_x"])[2 * core:2 * core + 2]
    return {"x": np.ascontiguousarray(x.reshape(NTOK, D))}


SHARED_SPECS = {
    "wz": ([D, D], F32), "wh": ([D, D], F32),
    "bz_t": ([128, DC], F32), "bh_t": ([128, DC], F32),
    "gamma_r": ([1, D], F32), "beta_r": ([1, D], F32),
    "wfc": ([HID, D], F32), "bfc_r": ([1, HID], F32),
    "wproj": ([D, HID], F32), "bproj_r": ([1, D], F32),
    "wqr": ([D, D], F32), "wkr": ([D, D], F32), "wvr": ([D, D], F32),
    "wp": ([D, D], F32), "bp_r": ([1, D], F32),
    "rt_row": ([1, NH * 33], F32),
    "btiles": ([128, NH * 2, 128], F32),
    "w1t": ([HC, DC, 128, 128], F32), "w2t": ([HC, DC, 128, 128], F32),
    "h1t": ([HC, HC, 128, 128], F32), "h2t": ([HC, HC, 128, 128], F32),
    "w3t": ([HC, 128, D], F32),
    "s1c": ([128, HC], F32), "s2c": ([128, HC], F32),
    "sh1c": ([128, HC], F32), "sh2c": ([128, HC], F32),
    "s3_r": ([1, D], F32),
    "ident_f": ([128, 128], F32), "ident_b": ([128, 128], BF16),
}


def bcast_row(pb, c, name, row_ap, width, dtype=BF16, row_reads=()):
    ones = c["_ones1_f32"]
    if row_ap.tensor.__class__.__name__.startswith("DRam") or "DRAM" in str(getattr(row_ap, "space", "")):
        srow = pb.sb(name + "_row", [1, width], F32)
        pb.dma("q", srow[:], row_ap, group=name + "_row")
        row_ap = srow[:]
        row_reads = [srow]
    dst = pb.sb(name, [128, width], dtype)
    psb = c.get("_psb")
    if psb is None:
        psb = pb.ps("ps_bcast", [128, 512], F32)
        c["_psb"] = psb
    pkey = psb if not isinstance(psb, tuple) else psb
    for j in range(0, width, 512):
        w = min(512, width - j)
        pb.op("t", lambda e, j=j, w=w, row_ap=row_ap: e.matmul(
            psb[:, 0:w] if not isinstance(psb, tuple) else None,
            ones[:], row_ap[:, j:j + w], start=True, stop=True),
              reads=[ones] + list(row_reads), writes=[psb])
        pb.op("v", lambda e, j=j, w=w: e.tensor_copy(out=dst[:, j:j + w], in_=psb[:, 0:w]),
              reads=[psb], writes=[(dst, j)])
    return dst


def ln_stats(pb, c, sx_ap, w, pref, reads):
    if pref not in c:
        c[pref] = dict(
            sum=pb.sb(pref + "_sum", [128, 1], F32),
            ssq=pb.sb(pref + "_ssq", [128, 1], F32),
            sqs=pb.sb(pref + "_sqs", [128, w], F32),
            mean=pb.sb(pref + "_mean", [128, 1], F32),
            m2=pb.sb(pref + "_m2", [128, 1], F32),
            var=pb.sb(pref + "_var", [128, 1], F32),
            lv=pb.sb(pref + "_lv", [128, 1], F32),
            rstd=pb.sb(pref + "_rstd", [128, 1], F32),
            nmr=pb.sb(pref + "_nmr", [128, 1], F32),
        )
    t = c[pref]
    pb.op("v", lambda e: e.reduce_sum(out=t["sum"][:], in_=sx_ap, axis=AX.X),
          reads=reads, writes=[t["sum"]])
    pb.op("s", lambda e: e.activation(out=t["sqs"][:, 0:w], in_=sx_ap, func=AF.Square,
                                      accum_out=t["ssq"][:]),
          reads=reads, writes=[t["sqs"], t["ssq"]])
    pb.op("v", lambda e: e.tensor_scalar_mul(out=t["mean"][:], in0=t["sum"][:], scalar1=1.0 / w),
          reads=[t["sum"]], writes=[t["mean"]])
    pb.op("v", lambda e: e.tensor_tensor(out=t["m2"][:], in0=t["mean"][:], in1=t["mean"][:],
                                         op=ALU.mult),
          reads=[t["mean"]], writes=[t["m2"]])
    pb.op("v", lambda e: e.scalar_tensor_tensor(out=t["var"][:], in0=t["ssq"][:],
                                                scalar=1.0 / w, in1=t["m2"][:],
                                                op0=ALU.mult, op1=ALU.subtract),
          reads=[t["ssq"], t["m2"]], writes=[t["var"]])
    beps = pb.cpp(EPS)
    pb.op("s", lambda e: e.activation(out=t["lv"][:], in_=t["var"][:], func=AF.Ln, bias=beps),
          reads=[t["var"], beps], writes=[t["lv"]])
    pb.op("s", lambda e: e.activation(out=t["rstd"][:], in_=t["lv"][:], func=AF.Exp, scale=-0.5),
          reads=[t["lv"]], writes=[t["rstd"]])
    pb.op("v", lambda e: e.scalar_tensor_tensor(out=t["nmr"][:], in0=t["mean"][:], scalar=-1.0,
                                                in1=t["rstd"][:], op0=ALU.mult, op1=ALU.mult),
          reads=[t["mean"], t["rstd"]], writes=[t["nmr"]])
    return t


def quantize_tile(pb, c, sx_ap, w, out_bf_ap, pref, reads, out_writes):
    nb = w // 32
    if pref not in c:
        c[pref] = dict(
            am=pb.sb(pref + "_am", [128, nb, 1], F32),
            ef=pb.sb(pref + "_ef", [128, nb, 1], F32),
            ei=pb.sb(pref + "_ei", [128, nb, 1], I32),
            el=pb.sb(pref + "_el", [128, nb, 1], F32),
            inv=pb.sb(pref + "_inv", [128, nb, 1], F32),
            scl=pb.sb(pref + "_scl", [128, nb, 1], F32),
            y=pb.sb(pref + "_y", [128, w], F32),
            qi=pb.sb(pref + "_qi", [128, w], I32),
            r=pb.sb(pref + "_r", [128, w], F32),
        )
    t = c[pref]
    v3 = sx_ap.rearrange("p (b w) -> p b w", w=32)
    pb.op("v", lambda e: e.tensor_reduce(out=t["am"][:, 0:nb], in_=v3, axis=AX.X, op=ALU.max,
                                         apply_absolute_value=True),
          reads=reads, writes=[t["am"]])
    pb.op("v", lambda e: e.tensor_scalar_max(out=t["am"][:, 0:nb], in0=t["am"][:, 0:nb],
                                             scalar1=float(2.0 ** -63)),
          reads=[t["am"]], writes=[t["am"]])
    pb.op("s", lambda e: e.activation(out=t["el"][:, 0:nb], in_=t["am"][:, 0:nb], func=AF.Ln,
                                      scale=1.0),
          reads=[t["am"]], writes=[t["el"]])
    pb.op("v", lambda e: e.tensor_scalar_mul(out=t["el"][:, 0:nb], in0=t["el"][:, 0:nb],
                                             scalar1=float(1.0 / LN2)),
          reads=[t["el"]], writes=[t["el"]])
    pb.op("v", lambda e: e.tensor_copy(out=t["ei"][:, 0:nb], in_=t["el"][:, 0:nb]),
          reads=[t["el"]], writes=[t["ei"]])
    pb.op("v", lambda e: e.tensor_copy(out=t["ef"][:, 0:nb], in_=t["ei"][:, 0:nb]),
          reads=[t["ei"]], writes=[t["ef"]])
    pb.op("v", lambda e: e.tensor_tensor(out=t["el"][:, 0:nb], in0=t["el"][:, 0:nb],
                                         in1=t["ef"][:, 0:nb], op=ALU.is_lt),
          reads=[t["el"], t["ef"]], writes=[t["el"]])
    pb.op("v", lambda e: e.tensor_tensor(out=t["ef"][:, 0:nb], in0=t["ef"][:, 0:nb],
                                         in1=t["el"][:, 0:nb], op=ALU.subtract),
          reads=[t["ef"], t["el"]], writes=[t["ef"]])
    b3p = pb.cpp(3 * LN2)
    b3n = pb.cpp(-3 * LN2)
    pb.op("s", lambda e: e.activation(out=t["inv"][:, 0:nb], in_=t["ef"][:, 0:nb], func=AF.Exp,
                                      scale=-LN2, bias=b3p),
          reads=[t["ef"], b3p], writes=[t["inv"]])
    pb.op("s", lambda e: e.activation(out=t["scl"][:, 0:nb], in_=t["ef"][:, 0:nb], func=AF.Exp,
                                      scale=LN2, bias=b3n),
          reads=[t["ef"], b3n], writes=[t["scl"]])
    pb.op("v", lambda e: e.tensor_tensor(out=t["y"][:, 0:w].rearrange("p (b w) -> p b w", w=32),
                                         in0=v3, in1=t["inv"][:, 0:nb].broadcast_to([128, nb, 32]),
                                         op=ALU.mult),
          reads=reads + [t["inv"]], writes=[t["y"]])
    pb.op("v", lambda e: e.tensor_copy(out=t["qi"][:, 0:w], in_=t["y"][:, 0:w]),
          reads=[t["y"]], writes=[t["qi"]])
    pb.op("v", lambda e: e.tensor_copy(out=t["r"][:, 0:w], in_=t["qi"][:, 0:w]),
          reads=[t["qi"]], writes=[t["r"]])
    pb.op("v", lambda e: e.tensor_tensor(out=t["y"][:, 0:w], in0=t["y"][:, 0:w],
                                         in1=t["r"][:, 0:w], op=ALU.subtract),
          reads=[t["y"], t["r"]], writes=[t["y"]])
    pb.op("v", lambda e: e.scalar_tensor_tensor(out=t["r"][:, 0:w], in0=t["y"][:, 0:w],
                                                scalar=0.5, in1=t["r"][:, 0:w],
                                                op0=ALU.is_gt, op1=ALU.add),
          reads=[t["y"], t["r"]], writes=[t["r"]])
    pb.op("v", lambda e: e.scalar_tensor_tensor(out=t["r"][:, 0:w], in0=t["y"][:, 0:w],
                                                scalar=-0.5, in1=t["r"][:, 0:w],
                                                op0=ALU.is_ge, op1=ALU.add),
          reads=[t["y"], t["r"]], writes=[t["r"]])
    pb.op("v", lambda e: e.tensor_scalar_add(out=t["r"][:, 0:w], in0=t["r"][:, 0:w],
                                             scalar1=-1.0),
          reads=[t["r"]], writes=[t["r"]])
    pb.op("v", lambda e: e.tensor_tensor(out=out_bf_ap.rearrange("p (b w) -> p b w", w=32),
                                         in0=t["r"][:, 0:w].rearrange("p (b w) -> p b w", w=32),
                                         in1=t["scl"][:, 0:nb].broadcast_to([128, nb, 32]),
                                         op=ALU.mult),
          reads=[t["r"], t["scl"]], writes=out_writes)


def emit_premix(pb, c):
    io = c["io"]
    with pb.scope():
        sx = [pb.sb(f"pm_sx{i}", [128, 1024], F32) for i in range(2)]
        sxn = [pb.sb(f"pm_sxn{i}", [128, 1024], F32) for i in range(2)]
        sxnb = [pb.sb(f"pm_sxnb{i}", [128, 1024], BF16) for i in range(2)]
        sqb = [pb.sb(f"pm_sqb{i}", [128, 1024], BF16) for i in range(2)]

        def pre_tile(i):
            b = i % 2
            sq = i // 16
            pb.dma("q", sx[b][:], io["x"][i * 128:(i + 1) * 128, :], group=f"pm_x{b}")
            t = ln_stats(pb, c, sx[b][:], D, f"pm_ln{b}", [sx[b]])
            pb.op("s", lambda e, b=b, t=t: e.activation(out=sxn[b][:], in_=sx[b][:],
                                                        func=AF.Identity,
                                                        scale=t["rstd"][:], bias=t["nmr"][:]),
                  reads=[sx[b], t["rstd"], t["nmr"]], writes=[sxn[b]])
            pb.op("s", lambda e, b=b: e.activation(out=sxnb[b][:], in_=sxn[b][:], func=AF.Copy),
                  reads=[sxn[b]], writes=[sxnb[b]])
            pb.dma("p", io["xn_f32"][i * 128:(i + 1) * 128, :], sxn[b][:], group="xn_f32",
                   reads=[sxn[b]], writes=[(io["xn_f32"], sq)])
            pb.dma("p", io["xn_bf"][i * 128:(i + 1) * 128, :], sxnb[b][:], group="xn_bf",
                   reads=[sxnb[b]], writes=[(io["xn_bf"], sq)])
            quantize_tile(pb, c, sxn[b][:], D, sqb[b][:], f"pm_q{b}", [sxn[b]], [sqb[b]])
            pb.dma("p", io["xq_bf"][i * 128:(i + 1) * 128, :], sqb[b][:], group="xq_bf",
                   reads=[sqb[b]], writes=[(io["xq_bf"], sq)])

        def wfc_tile(j):
            b = j % 2
            pb.dma("q", sx[b][:], io["wfc"][j * 128:(j + 1) * 128, :], group=f"pm_x{b}")
            quantize_tile(pb, c, sx[b][:], D, sqb[b][:], f"pm_q{b}", [sx[b]], [sqb[b]])
            pb.dma("p", io["wqfc_bf"][j * 128:(j + 1) * 128, :], sqb[b][:], group="wqfc_bf")

        def wproj_tile(j):
            r, qq = j // 4, j % 4
            b = j % 2
            pb.dma("q", sx[b][:], io["wproj"][r * 128:(r + 1) * 128,
                                              qq * 1024:(qq + 1) * 1024], group=f"pm_x{b}")
            quantize_tile(pb, c, sx[b][:], D, sqb[b][:], f"pm_q{b}", [sx[b]], [sqb[b]])
            pb.dma("p", io["wqproj_bf"][r * 128:(r + 1) * 128, qq * 1024:(qq + 1) * 1024],
                   sqb[b][:], group="wqproj_bf")

        psA = pb.ps("mx_psA", [128, T], F32)
        psB = pb.ps("mx_psB", [128, T], F32)
        c["_psb"] = psA
        gamma_b = bcast_row(pb, c, "gamma_b", io["gamma_r"][:], D, F32)
        beta_b = bcast_row(pb, c, "beta_b", io["beta_r"][:], D, F32)
        sbz = pb.sb("mx_bz", [128, DC], F32)
        sbh = pb.sb("mx_bh", [128, DC], F32)
        snbz = pb.sb("mx_nbz", [128, DC], F32)
        sbh05 = pb.sb("mx_bh05", [128, DC], F32)
        pb.dma("q", sbz[:], io["bz_t"])
        pb.dma("q", sbh[:], io["bh_t"])
        pb.op("v", lambda e: e.tensor_scalar_mul(out=snbz[:], in0=sbz[:], scalar1=-1.0),
              reads=[sbz], writes=[snbz])
        pb.op("v", lambda e: e.tensor_scalar_add(out=sbh05[:], in0=sbh[:], scalar1=0.5),
              reads=[sbh], writes=[sbh05])
        sid = pb.sb("mx_id", [128, 128], F32)
        pb.dma("q", sid[:], io["ident_f"])
        xnfm = pb.sb("mx_xnfm", [128, DC, T], BF16)
        wzs = [pb.sb(f"mx_wz{i}", [128, DC, 128], BF16) for i in range(2)]
        whs = [pb.sb(f"mx_wh{i}", [128, DC, 128], BF16) for i in range(2)]
        sc = pb.sb("mx_c", [128, T], F32)
        sz = pb.sb("mx_z", [128, T], F32)
        sr = pb.sb("mx_r", [128, T], F32)
        ss = pb.sb("mx_s", [128, T], F32)
        sh = pb.sb("mx_h", [128, T], F32)
        hstg = [pb.sb(f"mx_hs{i}", [128, 128], F32) for i in range(4)]
        hld = [pb.sb(f"mx_hl{i}", [128, 1024], F32) for i in range(2)]
        sxn_r = [pb.sb(f"mx_xn{i}", [128, 1024], F32) for i in range(2)]
        n1 = [pb.sb(f"mx_n1{i}", [128, 1024], F32) for i in range(2)]
        wzr = io["wz"].rearrange("(kc p) h -> p kc h", p=128)
        whr = io["wh"].rearrange("(kc p) h -> p kc h", p=128)

        def hln_tile(s, j):
            jb = j % 2
            i = s * 16 + j
            pb.dma("q", hld[jb][:], io["h_dram"][i * 128:(i + 1) * 128, :],
                   group=f"mx_hld{jb}", reads=[(io["h_dram"], f"{s}.{j}")])
            pb.dma("q", sxn_r[jb][:], io["xn_f32"][i * 128:(i + 1) * 128, :],
                   group=f"mx_sxn{jb}", reads=[(io["xn_f32"], s)])
            t = ln_stats(pb, c, hld[jb][:], D, f"mx_ln{jb}", [hld[jb]])
            pb.op("s", lambda e, jb=jb, t=t: e.activation(out=n1[jb][:], in_=hld[jb][:],
                                                          func=AF.Identity,
                                                          scale=t["rstd"][:], bias=t["nmr"][:]),
                  reads=[hld[jb], t["rstd"], t["nmr"]], writes=[n1[jb]])
            pb.op("v", lambda e, jb=jb: e.tensor_tensor(out=n1[jb][:], in0=n1[jb][:],
                                                        in1=gamma_b[:], op=ALU.mult),
                  reads=[n1[jb], gamma_b], writes=[n1[jb]])
            pb.op("v", lambda e, jb=jb: e.tensor_tensor(out=n1[jb][:], in0=n1[jb][:],
                                                        in1=beta_b[:], op=ALU.add),
                  reads=[n1[jb], beta_b], writes=[n1[jb]])
            pb.op("v", lambda e, jb=jb: e.tensor_tensor(out=n1[jb][:], in0=n1[jb][:],
                                                        in1=sxn_r[jb][:], op=ALU.add),
                  reads=[n1[jb], sxn_r[jb]], writes=[n1[jb]])
            pb.dma("p", io["s0_f32"][i * 128:(i + 1) * 128, :], n1[jb][:],
                   group=f"s0_f32{jb}")

        def mixer_seq(s, filler, spill):
            for kc in range(DC):
                pb.dmat("q", xnfm[:, kc, :],
                        io["xn_bf"][s * T:(s + 1) * T, kc * 128:(kc + 1) * 128],
                        group="mx_xnfm", reads=[(io["xn_bf"], s)],
                        writes=[(xnfm, kc)])
            for m in range(DC):
                b = m % 2
                pb.dma("p", wzs[b][:], wzr[:, :, m * 128:(m + 1) * 128], group=f"mx_wz{b}")
                pb.dma("p", whs[b][:], whr[:, :, m * 128:(m + 1) * 128], group=f"mx_wh{b}")
                for kc in range(DC):
                    for n in range(4):
                        pb.op("t", lambda e, kc=kc, n=n, b=b: e.matmul(
                            psA[:, n * 512:(n + 1) * 512], wzs[b][:, kc, :],
                            xnfm[:, kc, n * 512:(n + 1) * 512],
                            start=(kc == 0), stop=(kc == DC - 1)),
                            reads=[wzs[b], (xnfm, kc)], writes=[(psA, n)])
                emit_spill(spill)
                for kc in range(DC):
                    for n in range(4):
                        pb.op("t", lambda e, kc=kc, n=n, b=b: e.matmul(
                            psB[:, n * 512:(n + 1) * 512], whs[b][:, kc, :],
                            xnfm[:, kc, n * 512:(n + 1) * 512],
                            start=(kc == 0), stop=(kc == DC - 1)),
                            reads=[whs[b], (xnfm, kc)], writes=[(psB, n)])
                pA = [(psA, n) for n in range(4)]
                pB = [(psB, n) for n in range(4)]
                pb.op("s", lambda e, m=m: e.activation(out=sc[:], in_=psA[:], func=AF.Sigmoid,
                                                       scale=-1.0, bias=snbz[:, m:m + 1]),
                      reads=pA + [snbz], writes=[sc])
                pb.op("s", lambda e, m=m: e.activation(out=sz[:], in_=psA[:], func=AF.Sigmoid,
                                                       bias=sbz[:, m:m + 1]),
                      reads=pA + [sbz], writes=[sz])
                pb.op("s", lambda e, m=m: e.activation(out=sr[:], in_=psB[:], func=AF.Identity,
                                                       bias=sbh05[:, m:m + 1]),
                      reads=pB + [sbh05], writes=[sr])
                pb.op("s", lambda e, m=m: e.activation(out=ss[:], in_=psB[:], func=AF.Sigmoid,
                                                       bias=sbh[:, m:m + 1]),
                      reads=pB + [sbh], writes=[ss])
                pb.op("v", lambda e: e.tensor_max(out=ss[:], in0=ss[:], in1=sr[:]),
                      reads=[ss, sr], writes=[ss])
                pb.op("v", lambda e: e.tensor_tensor(out=sz[:], in0=sz[:], in1=ss[:],
                                                     op=ALU.mult),
                      reads=[sz, ss], writes=[sz])
                pb.op("v", lambda e: e.tensor_tensor_scan(out=sh[:], data0=sc[:], data1=sz[:],
                                                          initial=0.0, op0=ALU.mult,
                                                          op1=ALU.add),
                      reads=[sc, sz], writes=[sh])
                for it in filler[m::DC]:
                    it()
                spill[0] = (s, m)

        def emit_spill(spill):
            if spill[0] is None:
                return
            s, m = spill[0]
            spill[0] = None
            for j in range(16):
                jb = j % 4
                pb.op("t", lambda e, j=j: e.transpose(psB[:, j * 128:(j + 1) * 128],
                                                      sh[:, j * 128:(j + 1) * 128], sid[:]),
                      reads=[sh, sid], writes=[(psB, j // 4)])
                pb.op("v", lambda e, j=j, jb=jb: e.tensor_copy(
                    out=hstg[jb][:], in_=psB[:, j * 128:(j + 1) * 128]),
                    reads=[(psB, j // 4)], writes=[hstg[jb]])
                pb.dma("p", io["h_dram"][(s * 16 + j) * 128:(s * 16 + j + 1) * 128,
                                         m * 128:(m + 1) * 128],
                       hstg[jb][:], group=f"h_dram{jb}",
                       reads=[hstg[jb]], writes=[(io["h_dram"], f"{s}.{j}")])

        spill = [None]
        pre0 = [(lambda i=i: pre_tile(i)) for i in range(16)]
        for it in pre0:
            it()
        fill0 = [(lambda i=i: pre_tile(i)) for i in range(16, 32)] \
            + [(lambda j=j: wproj_tile(j)) for j in range(32)]
        mixer_seq(0, fill0, spill)
        fill1 = [(lambda j=j: wfc_tile(j)) for j in range(32)] \
            + [(lambda s=0, j=j: hln_tile(s, j)) for j in range(16)]
        mixer_seq(1, fill1, spill)
        emit_spill(spill)
        for j in range(16):
            hln_tile(1, j)


SCRATCH_SPECS = {
    "x": ([NTOK, D], F32),
    "xn_f32": ([NTOK, D], F32), "xn_bf": ([NTOK, D], BF16),
    "xq_bf": ([NTOK, D], BF16),
    "wqfc_bf": ([HID, D], BF16), "wqproj_bf": ([D, HID], BF16),
    "s0_f32": ([NTOK, D], F32),
    "x1_f32": ([NTOK, D], F32), "xn2_bf": ([NTOK, D], BF16),
    "s2_f32": ([NTOK, D], F32),
    "out": ([NTOK, D], F32),
    "h_dram": ([NTOK, D], F32),
    "w1_bf": ([HC, DC, 128, 128], BF16), "w2_bf": ([HC, DC, 128, 128], BF16),
    "h1_bf": ([HC, HC, 128, 128], BF16), "h2_bf": ([HC, HC, 128, 128], BF16),
    "w3_bf": ([HC, 128, D], BF16),
    "hq_bf": ([NTOK, HID], BF16),
    "gx_dram": ([HC, 128, T], BF16),
}

LAUNCH_DEFS = {}


def build_launch(name, extra_core_id=None):
    ins, outs, internals, emit_fn = LAUNCH_DEFS[name]
    nc = bass.Bass("TRN2", target_bir_lowering=False)
    pb = PB(nc)
    io = {}
    for nm in ins:
        shape, dt = SHARED_SPECS.get(nm) or SCRATCH_SPECS[nm]
        io[nm] = pb.dram(nm, shape, dt, kind="ExternalInput")
    for nm in outs:
        shape, dt = SCRATCH_SPECS[nm]
        io[nm] = pb.dram(nm, shape, dt, kind="ExternalOutput")
    for nm in internals:
        shape, dt = SCRATCH_SPECS[nm]
        io[nm] = pb.dram(nm, shape, dt, kind="Internal")
    c = {"io": io}
    for v in (EPS, 3 * LN2, -3 * LN2, 1.0, 0.0):
        pb.cpp(v)
    ones = pb._stack.enter_context(pb.nc.sbuf_tensor("ones1_f32", [1, 128], F32))
    pb.op("v", lambda e: e.memset(ones[:], 1.0), writes=[ones])
    c["_ones1_f32"] = ones
    emit_fn(pb, c)
    pb.finish()
    return nc, pb




def build_wconv_pairs(io):
    pairs = []
    for nm, rows, cols in (("w1t", 256, 16384), ("w2t", 256, 16384),
                           ("h1t", 1024, 16384), ("h2t", 1024, 16384),
                           ("w3t", 4096, 1024)):
        srcf = io[nm]
        dstf = io[{"w1t": "w1_bf", "w2t": "w2_bf", "h1t": "h1_bf",
                   "h2t": "h2_bf", "w3t": "w3_bf"}[nm]]
        if nm == "w3t":
            s2d = srcf.rearrange("kc k o -> (kc k) o")
            d2d = dstf.rearrange("kc k o -> (kc k) o")
            CT = 1024
        else:
            s2d = srcf.rearrange("m kc k o -> (m kc) (k o)")
            d2d = dstf.rearrange("m kc k o -> (m kc) (k o)")
            CT = 2048
        for r in range(rows // 128):
            for c0 in range(0, cols, CT):
                pairs.append((s2d[r * 128:(r + 1) * 128, c0:c0 + CT],
                              d2d[r * 128:(r + 1) * 128, c0:c0 + CT], CT))
    return pairs


def emit_ffn0(pb, c):
    io = c["io"]
    wconv_pairs = build_wconv_pairs(io)
    if c.get("fused"):
        c["wconv_rest_list"] = wconv_pairs[64:]
        wconv_pairs = wconv_pairs[:64]
    wci = [0]

    def emit_wconv(k):
        for _ in range(k):
            if wci[0] >= len(wconv_pairs):
                return
            sap, dap, ct = wconv_pairs[wci[0]]
            b = wci[0] % 2
            wci[0] += 1
            cv = c["_wcv"][b]
            cvb = c["_wcvb"][b]
            pb.dma("q", cv[:, 0:ct], sap, group=f"wcv{b}")
            pb.op("s", lambda e, cv=cv, cvb=cvb, ct=ct: e.activation(
                out=cvb[:, 0:ct], in_=cv[:, 0:ct], func=AF.Copy),
                reads=[cv], writes=[cvb])
            pb.dma("s", dap, cvb[:, 0:ct], group=f"wcvo{b}", reads=[cvb],
                   writes=[dap])
    with pb.scope():
        c["_wcv"] = [pb.sb(f"f0_wcv{i}", [128, 2048], F32) for i in range(2)]
        c["_wcvb"] = [pb.sb(f"f0_wcvb{i}", [128, 2048], BF16) for i in range(2)]
        psF = [pb.ps(f"f0_psF{i}", [128, T], F32) for i in range(2)]
        c["_psb"] = psF[0]
        bfc_b = bcast_row(pb, c, "bfc_b", io["bfc_r"][:], HID, BF16)
        wqfcT = pb.sb("f0_wfcT", [128, DC, HID], BF16)
        for kc in range(DC):
            pb.dmat("q", wqfcT[:, kc, :], io["wqfc_bf"][:, kc * 128:(kc + 1) * 128],
                    group="f0_wfcT")
        xqfm = pb.sb("f0_xqfm", [128, DC, 1024], BF16)
        sth = [pb.sb(f"f0_sth{i}", [128, T], F32) for i in range(2)]
        shq = [pb.sb(f"f0_shq{i}", [128, T], BF16) for i in range(2)]
        for grp in range(4):
            for kc in range(DC):
                pb.dmat("q", xqfm[:, kc, :],
                        io["xq_bf"][grp * 1024:(grp + 1) * 1024, kc * 128:(kc + 1) * 128],
                        group="f0_xqfm")
            for tcl in range(8):
                tc = grp * 8 + tcl
                for hh in range(2):
                    b = (tcl * 2 + hh) % 2
                    for kc in range(DC):
                        for n in range(4):
                            pb.op("t", lambda e, kc=kc, n=n, hh=hh, tcl=tcl, b=b: e.matmul(
                                psF[b][:, n * 512:(n + 1) * 512],
                                xqfm[:, kc, tcl * 128:(tcl + 1) * 128],
                                wqfcT[:, kc, hh * 2048 + n * 512:hh * 2048 + (n + 1) * 512],
                                start=(kc == 0), stop=(kc == DC - 1)),
                                reads=[(xqfm, kc), wqfcT], writes=[(psF[b], n)])
                    pp = [(psF[b], n) for n in range(4)]
                    pb.op("v", lambda e, b=b, hh=hh: e.tensor_tensor(
                        out=sth[b][:], in0=psF[b][:], in1=bfc_b[:, hh * 2048:(hh + 1) * 2048],
                        op=ALU.add),
                        reads=pp + [bfc_b], writes=[sth[b]])
                    pb.op("s", lambda e, b=b: e.activation(out=sth[b][:], in_=sth[b][:],
                                                           func=AF.Gelu),
                          reads=[sth[b]], writes=[sth[b]])
                    quantize_tile(pb, c, sth[b][:], T, shq[b][:], f"f0_q{b}",
                                  [sth[b]], [shq[b]])
                    pb.dma("p", io["hq_bf"][tc * 128:(tc + 1) * 128,
                                            hh * 2048:(hh + 1) * 2048],
                           shq[b][:], group=f"hq_bf{b}")
                    emit_wconv(2)
    with pb.scope():
        c["_wcv"] = [pb.sb(f"f0p_wcv{i}", [128, 2048], F32) for i in range(2)]
        c["_wcvb"] = [pb.sb(f"f0p_wcvb{i}", [128, 2048], BF16) for i in range(2)]
        psP = [pb.ps(f"f0_psP{i}", [128, D], F32) for i in range(2)]
        c["_psb"] = psP[0]
        bproj_b = bcast_row(pb, c, "bproj_b", io["bproj_r"][:], D, BF16)
        wqpT = pb.sb("f0_wpT", [128, HC, D], BF16)
        for kc in range(HC):
            pb.dmat("q", wqpT[:, kc, :], io["wqproj_bf"][:, kc * 128:(kc + 1) * 128],
                    group="f0_wpT")
        hqfm = pb.sb("f0_hqfm", [128, HC, 1024], BF16)
        sgl = [pb.sb(f"f0_sgl{i}", [128, D], F32) for i in range(2)]
        sx = [pb.sb(f"f0_sx{i}", [128, D], F32) for i in range(2)]
        ss0 = [pb.sb(f"f0_ss0{i}", [128, D], F32) for i in range(2)]
        sxnb = [pb.sb(f"f0_sxnb{i}", [128, D], BF16) for i in range(2)]
        for grp in range(4):
            for kc in range(HC):
                pb.dmat("q", hqfm[:, kc, :],
                        io["hq_bf"][grp * 1024:(grp + 1) * 1024, kc * 128:(kc + 1) * 128],
                        group="f0_hqfm")
            for tcl in range(8):
                tc = grp * 8 + tcl
                b = tcl % 2
                for kc in range(HC):
                    for n in range(2):
                        pb.op("t", lambda e, kc=kc, n=n, tcl=tcl, b=b: e.matmul(
                            psP[b][:, n * 512:(n + 1) * 512],
                            hqfm[:, kc, tcl * 128:(tcl + 1) * 128],
                            wqpT[:, kc, n * 512:(n + 1) * 512],
                            start=(kc == 0), stop=(kc == HC - 1)),
                            reads=[(hqfm, kc), wqpT], writes=[(psP[b], n)])
                pp = [(psP[b], n) for n in range(2)]
                pb.op("v", lambda e, b=b: e.tensor_tensor(out=sgl[b][:], in0=psP[b][:],
                                                          in1=bproj_b[:], op=ALU.add),
                      reads=pp + [bproj_b], writes=[sgl[b]])
                pb.op("s", lambda e, b=b: e.activation(out=sgl[b][:], in_=sgl[b][:],
                                                       func=AF.Silu),
                      reads=[sgl[b]], writes=[sgl[b]])
                pb.dma("q", sx[b][:], io["x"][tc * 128:(tc + 1) * 128, :], group=f"f0_x{b}")
                pb.dma("q", ss0[b][:], io["s0_f32"][tc * 128:(tc + 1) * 128, :],
                       group=f"f0_s0{b}")
                pb.op("v", lambda e, b=b: e.tensor_tensor(out=sgl[b][:], in0=sgl[b][:],
                                                          in1=ss0[b][:], op=ALU.mult),
                      reads=[sgl[b], ss0[b]], writes=[sgl[b]])
                pb.op("v", lambda e, b=b: e.tensor_tensor(out=sx[b][:], in0=sx[b][:],
                                                          in1=sgl[b][:], op=ALU.add),
                      reads=[sx[b], sgl[b]], writes=[sx[b]])
                pb.dma("p", io["x1_f32"][tc * 128:(tc + 1) * 128, :], sx[b][:],
                       group=f"x1_f32{b}")
                t = ln_stats(pb, c, sx[b][:], D, f"f0_ln{b}", [sx[b]])
                pb.op("s", lambda e, b=b, t=t: e.activation(out=sxnb[b][:], in_=sx[b][:],
                                                            func=AF.Identity,
                                                            scale=t["rstd"][:], bias=t["nmr"][:]),
                      reads=[sx[b], t["rstd"], t["nmr"]], writes=[sxnb[b]])
                pb.dma("p", io["xn2_bf"][tc * 128:(tc + 1) * 128, :], sxnb[b][:],
                       group=f"xn2_bf{b}")
                emit_wconv(2)
        emit_wconv(len(wconv_pairs))


LAUNCH_DEFS["premix"] = (
    ["x", "wz", "wh", "bz_t", "bh_t", "gamma_r", "beta_r", "ident_f", "wfc", "wproj"],
    ["xn_f32", "xn_bf", "xq_bf", "wqfc_bf", "wqproj_bf", "s0_f32"],
    ["h_dram"], emit_premix)


LAUNCH_DEFS["ffn0"] = (
    ["xq_bf", "wqfc_bf", "wqproj_bf", "bfc_r", "bproj_r", "s0_f32", "x",
     "w1t", "w2t", "h1t", "h2t", "w3t"],
    ["x1_f32", "xn2_bf", "w1_bf", "w2_bf", "h1_bf", "h2_bf", "w3_bf"],
    ["hq_bf"], emit_ffn0)


def emit_attn(pb, c):
    io = c["io"]
    STRIP_OFF = [0]
    for kc in range(16):
        STRIP_OFF.append(STRIP_OFF[-1] + (T - 128 * kc))
    NPT = STRIP_OFF[-1]

    with pb.scope():
        psB4 = pb.ps("at_psB4", [128, 2048], F32)
        psQ = pb.ps("at_psQ", [128, 1024], F32)
        psX = pb.ps("at_psX", [128, 1024], F32)
        c["_psb"] = psX
        btiles = pb.sb("at_bt", [128, NH * 2, 128], BF16)
        pb.dma("p", btiles[:], io["btiles"])
        srow = pb.sb("at_row", [1, NH * 33], F32)
        pb.dma("q", srow[:], io["rt_row"])
        srow3 = srow[:].rearrange("p (h u) -> p h u", u=33)
        bmaxh = pb.sb("at_bmaxh", [1, NH, 1], F32)
        pb.op("v", lambda e: e.tensor_reduce(out=bmaxh[:], in_=srow3[:, :, 0:17],
                                             axis=AX.X, op=ALU.max),
              reads=[srow], writes=[bmaxh])
        gbm = pb.sb("at_gbm", [1, 1], F32)
        pb.op("v", lambda e: e.tensor_reduce(
            out=gbm[:], in_=bmaxh[:].rearrange("p h u -> p (h u)"),
            axis=AX.X, op=ALU.max),
            reads=[bmaxh], writes=[gbm])
        onescol = pb.sb("at_onec", [128, 1], F32)
        pb.op("v", lambda e: e.memset(onescol[:], 1.0), writes=[onescol])
        bp_b = bcast_row(pb, c, "bp_b", io["bp_r"][:], D, F32)
        ones = c["_ones1_f32"]
        rel0bc = pb.sb("at_r0bc", [128, NH], F32)
        pb.op("t", lambda e: e.matmul(psX[:, 0:NH], ones[:], srow3[0:1, :, 0],
                                      start=True, stop=True),
              reads=[ones, srow], writes=[(psX, 0)])
        pb.op("v", lambda e: e.tensor_copy(out=rel0bc[:], in_=psX[:, 0:NH]),
              reads=[(psX, 0)], writes=[rel0bc])
        for h in range(NH):
            pb.op("v", lambda e, h=h: e.tensor_scalar(
                out=btiles[:, 2 * h:2 * h + 2, :], in0=btiles[:, 2 * h:2 * h + 2, :],
                scalar1=rel0bc[:, h:h + 1], scalar2=8.0,
                op0=ALU.subtract, op1=ALU.mult),
                reads=[(btiles, h), rel0bc], writes=[(btiles, h)])
        nrm = pb.sb("at_nrm", [1, 4], F32)
        u3 = pb.sb("at_u3", [1, 2], F32)
        rel0p = pb.sb("at_rel0", [1, 2], F32)
        upair = pb.sb("at_up", [1, 4], F32)
        mxq = pb.sb("at_mxq", [128, 1], F32)
        mxk = pb.sb("at_mxk", [128, 1], F32)
        rmt = pb.sb("at_rmt", [128, 1], F32)
        mx2 = pb.sb("at_mx2", [128, 1], F32)
        bU = [pb.sb(f"at_bU{i}", [128, 2], F32) for i in range(2)]
        linv = pb.sb("at_linv", [1, 2048], F32)
        linv2 = pb.sb("at_linv2", [1, 2048], F32)
        ss2 = [pb.sb("at_s2_0", [128, D], F32)] * 2
        vaug = pb.sb("at_v", [128, 16, NH * 65 + 63], BF16)
        o_fm = pb.sb("at_o", [128, DC, T], BF16)
        wqr3 = io["wqr"].rearrange("(kc p) o -> p kc o", p=128)
        wkr3 = io["wkr"].rearrange("(kc p) o -> p kc o", p=128)

        for s in range(NSEQ):
            with pb.scope():
                xn2fm = pb.sb(f"at_xn2{s}", [128, DC, T], BF16)
                for kc in range(DC):
                    pb.dmat("q", xn2fm[:, kc, :],
                            io["xn2_bf"][s * T:(s + 1) * T, kc * 128:(kc + 1) * 128],
                            group="at_xn2")
                with pb.scope():
                    wv_b = pb.sb(f"at_wv{s}", [128, DC, D], BF16)
                    pb.dma("p", wv_b[:],
                           io["wvr"].rearrange("(kc p) o -> p kc o", p=128),
                           group="at_wv")
                    pb.op("v", lambda e: e.memset(
                        vaug[:, :, 0:NH * 65].rearrange("p t (h w) -> p t h w", w=65)[:, :, :, 64:65], 1.0),
                        writes=[(vaug, "ones")])
                    pb.op("v", lambda e: e.memset(vaug[:, :, NH * 65:], 0.0),
                          writes=[(vaug, "pad")])
                    for tc in range(16):
                        for kc in range(DC):
                            for n in range(2):
                                pb.op("t", lambda e, tc=tc, kc=kc, n=n: e.matmul(
                                    psX[:, n * 512:(n + 1) * 512],
                                    xn2fm[:, kc, tc * 128:(tc + 1) * 128],
                                    wv_b[:, kc, n * 512:(n + 1) * 512],
                                    start=(kc == 0), stop=(kc == DC - 1)),
                                    reads=[(xn2fm, kc), wv_b], writes=[(psX, n)])
                        pb.op("s", lambda e, tc=tc: e.activation(
                            out=vaug[:, tc, 0:NH * 65].rearrange("p (h w) -> p h w", w=65)[:, :, 0:64],
                            in_=psX[:, 0:1024], func=AF.Copy),
                            reads=[(psX, 0), (psX, 1)], writes=[(vaug, tc)])
                pT = pb.sb(f"at_pT{s}", [128, NPT], BF16)
                qh2 = [pb.sb(f"at_q{s}{i}", [128, T], BF16) for i in range(2)]
                kh2 = [pb.sb(f"at_k{s}{i}", [128, T], BF16) for i in range(2)]
                wq2 = [pb.sb(f"at_wq{s}{i}", [128, DC, 128], BF16) for i in range(2)]
                wk2 = [pb.sb(f"at_wk{s}{i}", [128, DC, 128], BF16) for i in range(2)]
                wcv2 = pb.sb(f"at_wcv{s}", [128, 2048], F32)
                wcvb2 = pb.sb(f"at_wcvb{s}", [128, 2048], BF16)
                rest = c.get("wconv_rest_list", [])

                def emit_wconv2(k):
                    for _ in range(k):
                        if not rest:
                            return
                        sap, dap, ct = rest.pop(0)
                        pb.dma("s", wcv2[:, 0:ct], sap, group="at_wcv")
                        pb.op("v", lambda e, ct=ct: e.tensor_copy(
                            out=wcvb2[:, 0:ct], in_=wcv2[:, 0:ct]),
                            reads=[wcv2], writes=[wcvb2])
                        pb.dma("p", dap, wcvb2[:, 0:ct], group="at_wcvo",
                               reads=[wcvb2], writes=[dap])
                def pair_setup(hc2):
                    b = hc2 % 2
                    pb.dma("p", wq2[b][:], wqr3[:, :, hc2 * 128:(hc2 + 1) * 128],
                           group=f"at_wq{b}")
                    pb.dma("p", wk2[b][:], wkr3[:, :, hc2 * 128:(hc2 + 1) * 128],
                           group=f"at_wk{b}")
                    qbank = lambda n: (psQ[:, (n % 2) * 512:(n % 2 + 1) * 512], (psQ, n % 2)) \
                        if n < 2 else (psX[:, (n % 2) * 512:(n % 2 + 1) * 512], (psX, n % 2))
                    for (w_b, dst), mx in (((wq2[b], qh2[b]), mxq), ((wk2[b], kh2[b]), mxk)):
                        for kc in range(DC):
                            for n in range(4):
                                ap, tag = qbank(n)
                                pb.op("t", lambda e, w_b=w_b, kc=kc, n=n, ap=ap: e.matmul(
                                    ap, w_b[:, kc, :],
                                    xn2fm[:, kc, n * 512:(n + 1) * 512],
                                    start=(kc == 0), stop=(kc == DC - 1)),
                                    reads=[w_b, (xn2fm, kc)], writes=[tag])
                        for n in range(4):
                            ap, tag = qbank(n)
                            pb.op("s", lambda e, dst=dst, n=n, ap=ap: e.activation(
                                out=dst[:, n * 512:(n + 1) * 512], in_=ap, func=AF.Copy),
                                reads=[tag], writes=[(dst, n)])
                            if n == 0:
                                pb.op("v", lambda e, dst=dst, n=n, mx=mx: e.tensor_reduce(
                                    out=mx[:], in_=dst[:, n * 512:(n + 1) * 512],
                                    axis=AX.X, op=ALU.max, apply_absolute_value=True),
                                    reads=[(dst, n)], writes=[mx])
                            else:
                                pb.op("v", lambda e, dst=dst, n=n: e.tensor_reduce(
                                    out=rmt[:], in_=dst[:, n * 512:(n + 1) * 512],
                                    axis=AX.X, op=ALU.max, apply_absolute_value=True),
                                    reads=[(dst, n)], writes=[rmt])
                                pb.op("v", lambda e, mx=mx: e.tensor_max(
                                    out=mx[:], in0=mx[:], in1=rmt[:]),
                                    reads=[mx, rmt], writes=[mx])
                    for qi, mx in enumerate((mxq, mxk)):
                        pb.op("v", lambda e, mx=mx: e.tensor_tensor(out=mx2[:], in0=mx[:],
                                                                    in1=mx[:], op=ALU.mult),
                              reads=[mx], writes=[mx2])
                        for hh in range(2):
                            pb.op("t", lambda e, hh=hh: e.matmul(
                                psX[0:1, 0:1], onescol[hh * 64:(hh + 1) * 64, :],
                                mx2[hh * 64:(hh + 1) * 64, :], start=True, stop=True),
                                reads=[onescol, mx2], writes=[(psX, 0)])
                            pb.op("v", lambda e, qi=qi, hh=hh: e.tensor_copy(
                                out=nrm[0:1, qi * 2 + hh:qi * 2 + hh + 1], in_=psX[0:1, 0:1]),
                                reads=[(psX, 0)], writes=[(nrm, qi * 2 + hh)])
                    nr = [(nrm, i) for i in range(4)]
                    pb.op("v", lambda e: e.tensor_tensor(out=u3[:], in0=nrm[0:1, 0:2],
                                                         in1=nrm[0:1, 2:4], op=ALU.mult),
                          reads=nr, writes=[u3])
                    pb.op("v", lambda e: e.tensor_scalar_max(out=u3[:], in0=u3[:],
                                                             scalar1=1e-30),
                          reads=[u3], writes=[u3])
                    pb.op("s", lambda e: e.activation(out=u3[:], in_=u3[:], func=AF.Ln),
                          reads=[u3], writes=[u3])
                    pb.op("s", lambda e: e.activation(out=u3[:], in_=u3[:], func=AF.Exp,
                                                      scale=0.5),
                          reads=[u3], writes=[u3])
                    pb.op("v", lambda e: e.scalar_tensor_tensor(
                        out=u3[:], in0=u3[:], scalar=0.1275,
                        in1=gbm[0:1, 0:1].broadcast_to([1, 2]), op0=ALU.mult, op1=ALU.add),
                        reads=[u3, gbm], writes=[u3])
                    pb.op("v", lambda e, hc2=hc2: e.tensor_copy(
                        out=rel0p[:], in_=srow3[0:1, 2 * hc2:2 * hc2 + 2, 0]),
                        reads=[srow], writes=[rel0p])
                    pb.op("v", lambda e: e.tensor_tensor(
                        out=upair[0:1, 0:2], in0=rel0p[:], in1=u3[:], op=ALU.subtract),
                        reads=[rel0p, u3], writes=[(upair, 0)])
                    pb.op("t", lambda e: e.matmul(psX[:, 0:2], ones[:], upair[0:1, 0:2],
                                                  start=True, stop=True),
                          reads=[ones, (upair, 0)], writes=[(psX, 0)])
                    pb.op("v", lambda e, b=b: e.tensor_copy(out=bU[b][:], in_=psX[:, 0:2]),
                          reads=[(psX, 0)], writes=[bU[b]])

                def do_head(hc2, hh):
                    b = hc2 % 2
                    h = hc2 * 2 + hh
                    qh_ap = qh2[b][hh * 64:(hh + 1) * 64, :]
                    kh_ap = kh2[b][hh * 64:(hh + 1) * 64, :]
                    qreads = [(qh2[b], n) for n in range(4)]
                    kreads = [(kh2[b], n) for n in range(4)]
                    ci = 0
                    for kc in range(16):
                        N = T - 128 * kc
                        offp = STRIP_OFF[kc]
                        for r0 in range(0, N, 1024):
                            reg = ci % 2
                            ci += 1
                            w = min(1024, N - r0)
                            for n0 in range(0, w, 512):
                                wn = min(512, w - n0)
                                pb.op("t", lambda e, kc=kc, r0=r0, n0=n0, wn=wn, reg=reg,
                                      qh_ap=qh_ap, kh_ap=kh_ap: e.matmul(
                                    psB4[:, reg * 1024 + n0:reg * 1024 + n0 + wn],
                                    kh_ap[:, kc * 128:(kc + 1) * 128],
                                    qh_ap[:, kc * 128 + r0 + n0:kc * 128 + r0 + n0 + wn],
                                    start=True, stop=True),
                                    reads=qreads + kreads,
                                    writes=[(psB4, f"b{reg * 2 + n0 // 512}")])
                            sreads = [(psB4, f"b{reg * 2 + n0 // 512}") for n0 in range(0, w, 512)]
                            if r0 == 0:
                                nw = 128 if kc == 15 else 256
                                pb.op("v", lambda e, reg=reg, h=h, nw=nw: e.tensor_tensor(
                                    out=psB4[:, reg * 1024:reg * 1024 + nw],
                                    in0=psB4[:, reg * 1024:reg * 1024 + nw],
                                    in1=btiles[:, 2 * h:2 * h + 2, :].rearrange(
                                        "p b j -> p (b j)")[:, 0:nw], op=ALU.add),
                                    reads=sreads[0:1] + [(btiles, h)],
                                    writes=[(psB4, f"b{reg * 2}")])
                            pb.op("s", lambda e, offp=offp, r0=r0, w=w, reg=reg, b=b,
                                  hh=hh: e.activation(
                                out=pT[:, offp + r0:offp + r0 + w],
                                in_=psB4[:, reg * 1024:reg * 1024 + w], func=AF.Exp,
                                scale=SCALE_ATT, bias=bU[b][:, hh:hh + 1]),
                                reads=sreads + [bU[b]], writes=[(pT, kc)])
                    for kc in range(16):
                        for bk in range(max(kc // 4, 0), 4):
                            b0 = bk * 512
                            b1 = b0 + 512
                            s0 = max(b0, kc * 128)
                            last_kc = min(15, (b1 - 1) // 128)
                            pb.op("t", lambda e, kc=kc, s0=s0, b1=b1, h=h,
                                  last_kc=last_kc: e.matmul(
                                psB4[:, s0:b1],
                                vaug[:, kc, h * 65:h * 65 + 128],
                                pT[:, STRIP_OFF[kc] + s0 - kc * 128:
                                    STRIP_OFF[kc] + b1 - kc * 128],
                                start=(kc == 0), stop=(kc == last_kc)),
                                reads=[(vaug, kc), (vaug, "ones"), (vaug, "pad"), (pT, kc)],
                                writes=[(psB4, f"b{bk}")])
                    avr = [(psB4, f"b{bk}") for bk in range(4)]
                    pb.op("s", lambda e, hh=hh, hc2=hc2: e.activation(
                        out=o_fm[hh * 64:(hh + 1) * 64, hc2, :],
                        in_=psB4[0:64, :], func=AF.Copy),
                        reads=avr, writes=[(o_fm, f"{hh}.{hc2}")])
                    pb.op("v", lambda e: e.tensor_scalar_max(
                        out=linv[:], in0=psB4[64:65, :], scalar1=1e-30),
                        reads=avr, writes=[(linv, 0), (linv, 1)])
                    pb.op("v", lambda e: e.reciprocal(out=linv2[:], in_=linv[:]),
                          reads=[(linv, 0), (linv, 1)], writes=[linv2])
                    for n4 in range(4):
                        pb.op("t", lambda e, n4=n4: e.matmul(
                            psX[0:64, (n4 % 2) * 512:(n4 % 2 + 1) * 512], ones[0:1, 0:64],
                            linv2[0:1, n4 * 512:(n4 + 1) * 512],
                            start=True, stop=True),
                            reads=[ones, linv2], writes=[(psX, n4 % 2)])
                        pb.op("v", lambda e, hh=hh, hc2=hc2, n4=n4: e.tensor_tensor(
                            out=o_fm[hh * 64:(hh + 1) * 64, hc2,
                                     n4 * 512:(n4 + 1) * 512],
                            in0=o_fm[hh * 64:(hh + 1) * 64, hc2,
                                     n4 * 512:(n4 + 1) * 512],
                            in1=psX[0:64, (n4 % 2) * 512:(n4 % 2 + 1) * 512], op=ALU.mult),
                            reads=[(psX, n4 % 2), (o_fm, f"{hh}.{hc2}")],
                            writes=[(o_fm, f"{hh}.{hc2}")])

                pair_setup(0)
                for hc2 in range(8):
                    do_head(hc2, 0)
                    emit_wconv2(2)
                    if hc2 + 1 < 8:
                        pair_setup(hc2 + 1)
                    do_head(hc2, 1)
                    emit_wconv2(2)
                if s == NSEQ - 1:
                    emit_wconv2(len(rest))
            with pb.scope():
                wp_b = pb.sb(f"at_wp{s}", [128, DC, D], BF16)
                pb.dma("p", wp_b[:], io["wp"].rearrange("(kc p) o -> p kc o", p=128),
                       group="at_wp")
                for tc in range(16):
                    b = tc % 2
                    for kc in range(DC):
                        for n in range(2):
                            pb.op("t", lambda e, tc=tc, kc=kc, n=n: e.matmul(
                                psX[:, n * 512:(n + 1) * 512],
                                o_fm[:, kc, tc * 128:(tc + 1) * 128],
                                wp_b[:, kc, n * 512:(n + 1) * 512],
                                start=(kc == 0), stop=(kc == DC - 1)),
                                reads=[(o_fm, f"0.{kc}"), (o_fm, f"1.{kc}"), wp_b],
                                writes=[(psX, n)])
                    pb.op("v", lambda e, b=b: e.tensor_tensor(
                        out=ss2[b][:], in0=psX[:, 0:1024], in1=bp_b[:], op=ALU.add),
                        reads=[(psX, 0), (psX, 1), bp_b], writes=[ss2[b]])
                    pb.dma("p", io["s2_f32"][(s * 16 + tc) * 128:(s * 16 + tc + 1) * 128, :],
                           ss2[b][:], group=f"s2_f32{b}")


LAUNCH_DEFS["attn"] = (
    ["xn2_bf", "wqr", "wkr", "wvr", "wp", "bp_r", "rt_row", "btiles"],
    ["s2_f32"], [], emit_attn)


def emit_ffn1(pb, c):
    io = c["io"]
    with pb.scope():
        scl = {}
        for nm in ("s1c", "s2c", "sh1c", "sh2c"):
            scl[nm] = pb.sb("f1_" + nm, [128, HC], F32)
            pb.dma("q", scl[nm][:], io[nm], group="f1_" + nm)
        for s in range(NSEQ):
            with pb.scope():
                h_fm = pb.sb(f"f1_h{s}", [128, HC, T], BF16)
                with pb.scope():
                    psT1 = pb.ps(f"f1_psT1{s}", [128, T], F32)
                    psT2 = pb.ps(f"f1_psT2{s}", [128, T], F32)
                    xn2 = pb.sb(f"f1_xn2{s}", [128, DC, T], BF16)
                    w1s = [pb.sb(f"f1_w1s{s}{i}", [128, DC, 128], BF16) for i in range(2)]
                    w2s = [pb.sb(f"f1_w2s{s}{i}", [128, DC, 128], BF16) for i in range(2)]
                    sa = pb.sb(f"f1_sa{s}", [128, T], F32)
                    w1r = io["w1_bf"].rearrange("m kc k o -> m k kc o")
                    w2r = io["w2_bf"].rearrange("m kc k o -> m k kc o")
                    for kc in range(DC):
                        pb.dmat("q", xn2[:, kc, :],
                                io["xn2_bf"][s * T:(s + 1) * T,
                                             kc * 128:(kc + 1) * 128],
                                group="f1_xn2")
                    for m in range(HC):
                        b = m % 2
                        pb.dma("q", w1s[b][:], w1r[m], group=f"f1_w1{b}")
                        pb.dma("q", w2s[b][:], w2r[m], group=f"f1_w2{b}")
                        for kc in range(DC):
                            for n in range(4):
                                pb.op("t", lambda e, kc=kc, n=n, b=b: e.matmul(
                                    psT1[:, n * 512:(n + 1) * 512], w1s[b][:, kc, :],
                                    xn2[:, kc, n * 512:(n + 1) * 512],
                                    start=(kc == 0), stop=(kc == DC - 1)),
                                    reads=[w1s[b], (xn2, kc)], writes=[(psT1, n)])
                        for kc in range(DC):
                            for n in range(4):
                                pb.op("t", lambda e, kc=kc, n=n, b=b: e.matmul(
                                    psT2[:, n * 512:(n + 1) * 512], w2s[b][:, kc, :],
                                    xn2[:, kc, n * 512:(n + 1) * 512],
                                    start=(kc == 0), stop=(kc == DC - 1)),
                                    reads=[w2s[b], (xn2, kc)], writes=[(psT2, n)])
                        p1 = [(psT1, n) for n in range(4)]
                        p2 = [(psT2, n) for n in range(4)]
                        pb.op("s", lambda e, m=m: e.activation(
                            out=sa[:], in_=psT1[:], func=AF.Silu,
                            scale=scl["s1c"][:, m:m + 1]),
                            reads=p1 + [scl["s1c"]], writes=[sa])
                        pb.op("v", lambda e, m=m: e.scalar_tensor_tensor(
                            out=h_fm[:, m, :], in0=psT2[:],
                            scalar=scl["s2c"][:, m:m + 1], in1=sa[:],
                            op0=ALU.mult, op1=ALU.mult),
                            reads=p2 + [scl["s2c"], sa], writes=[(h_fm, m)])
                psU = [pb.ps(f"f1_psU{s}{i}", [128, T], F32) for i in range(2)]
                for ui, (wt, sc) in enumerate((("h1_bf", "sh1c"), ("h2_bf", "sh2c"))):
                    stU = [pb.sb(f"f1_u{ui}s{s}{i}", [128, HC, 128], BF16) for i in range(3)]
                    gout = [pb.sb(f"f1_g{ui}o{s}{i}", [128, T], BF16) for i in range(2)]
                    wr = io[wt].rearrange("m kc k o -> m k kc o")
                    for m in range(HC):
                        b = m % 3
                        bp2 = m % 2
                        pb.dma("q", stU[b][:], wr[m], group=f"f1_u{ui}w{b}")
                        for kc in range(HC):
                            for n in range(4):
                                pb.op("t", lambda e, kc=kc, n=n, b=b, bp2=bp2: e.matmul(
                                    psU[bp2][:, n * 512:(n + 1) * 512], stU[b][:, kc, :],
                                    h_fm[:, kc, n * 512:(n + 1) * 512],
                                    start=(kc == 0), stop=(kc == HC - 1)),
                                    reads=[stU[b], h_fm], writes=[(psU[bp2], n)])
                        pu = [(psU[bp2], n) for n in range(4)]
                        pb.op("s", lambda e, m=m, bp2=bp2, sc=sc: e.activation(
                            out=gout[bp2][:], in_=psU[bp2][:], func=AF.Gelu,
                            scale=scl[sc][:, m:m + 1]),
                            reads=pu + [scl[sc]], writes=[gout[bp2]])
                        pb.dma("p", io["gx_dram"][m], gout[bp2][:], group=f"f1_gx{bp2}")
                    for m in range(HC):
                        pb.dma("q", h_fm[:, m, :], io["gx_dram"][m],
                               group="f1_hre", reads=[("gx_dram", m)],
                               writes=[(h_fm, m)])
                    if ui == 0:
                        continue
            with pb.scope():
                w3sb = pb.sb(f"f1_w3{s}", [128, HC, D], BF16)
                w3r = io["w3_bf"].rearrange("(kc) k o -> k kc o")
                for q in range(4):
                    pb.dma("q", w3sb[:, q * 8:(q + 1) * 8, :], w3r[:, q * 8:(q + 1) * 8, :],
                           group="f1_w3")
                psW = [pb.ps(f"f1_psW{s}{i}", [128, 1024], F32) for i in range(2)]
                c["_psb"] = psW[0]
                s3b = bcast_row(pb, c, f"s3b{s}", io["s3_r"][:], D, F32)
                for kc in range(HC):
                    pb.op("v", lambda e, kc=kc: e.tensor_tensor(
                        out=w3sb[:, kc, :], in0=w3sb[:, kc, :], in1=s3b[:], op=ALU.mult),
                        reads=[(w3sb, kc), s3b], writes=[(w3sb, kc)])
                gl = [pb.sb(f"f1_gl{s}{i}", [128, HC, 128], BF16) for i in range(2)]
                sgf = [pb.sb(f"f1_sgf{s}{i}", [128, D], F32) for i in range(2)]
                sx1 = [pb.sb(f"f1_sx1{s}{i}", [128, D], F32) for i in range(2)]
                ss2 = [pb.sb(f"f1_ss2{s}{i}", [128, D], F32) for i in range(2)]
                gxr = io["gx_dram"]
                for tc in range(16):
                    b = tc % 2
                    pb.dma("q", gl[b][:], gxr[:, :, tc * 128:(tc + 1) * 128].rearrange(
                        "kc k t -> k kc t"), group=f"f1_gl{b}", reads=[("gx_dram", "")])
                    for kc in range(HC):
                        for n in range(2):
                            pb.op("t", lambda e, kc=kc, n=n, b=b: e.matmul(
                                psW[b][:, n * 512:(n + 1) * 512], gl[b][:, kc, :],
                                w3sb[:, kc, n * 512:(n + 1) * 512],
                                start=(kc == 0), stop=(kc == HC - 1)),
                                reads=[gl[b], (w3sb, kc)], writes=[(psW[b], n)])
                    pw = [(psW[b], n) for n in range(2)]
                    pb.op("s", lambda e, b=b: e.activation(out=sgf[b][:], in_=psW[b][:],
                                                           func=AF.Silu),
                          reads=pw, writes=[sgf[b]])
                    i = s * 16 + tc
                    pb.dma("q", sx1[b][:], io["x1_f32"][i * 128:(i + 1) * 128, :],
                           group=f"f1_x1{b}")
                    pb.dma("q", ss2[b][:], io["s2_f32"][i * 128:(i + 1) * 128, :],
                           group=f"f1_s2{b}")
                    pb.op("v", lambda e, b=b: e.tensor_tensor(
                        out=sgf[b][:], in0=sgf[b][:], in1=ss2[b][:], op=ALU.mult),
                        reads=[sgf[b], ss2[b]], writes=[sgf[b]])
                    pb.op("v", lambda e, b=b: e.tensor_tensor(
                        out=sx1[b][:], in0=sx1[b][:], in1=sgf[b][:], op=ALU.add),
                        reads=[sx1[b], sgf[b]], writes=[sx1[b]])
                    pb.dma("p", io["out"][i * 128:(i + 1) * 128, :], sx1[b][:],
                           group=f"out{b}")


LAUNCH_DEFS["ffn1"] = (
    ["xn2_bf", "w1_bf", "w2_bf", "h1_bf", "h2_bf", "w3_bf", "s1c", "s2c",
     "sh1c", "sh2c", "s3_r", "x1_f32", "s2_f32"],
    ["out"], ["gx_dram"], emit_ffn1)


from concourse.bass_utils import run_bass_kernel_spmd

N_CORES = 8
LAUNCH_ORDER = ["premix", "ffn0", "attn", "ffn1"]


def build_fused():
    nc = bass.Bass("TRN2", target_bir_lowering=False)
    pb = PB(nc)
    io = {}
    for nm, (shape, dt) in SHARED_SPECS.items():
        io[nm] = pb.dram(nm, shape, dt, kind="ExternalInput")
    io["x"] = pb.dram("x", list(SCRATCH_SPECS["x"][0]), F32, kind="ExternalInput")
    io["out"] = pb.dram("out", list(SCRATCH_SPECS["out"][0]), F32,
                        kind="ExternalOutput")
    for nm, (shape, dt) in SCRATCH_SPECS.items():
        if nm in ("x", "out"):
            continue
        io[nm] = pb.dram(nm, shape, dt, kind="Internal")
    c = {"io": io, "fused": True}
    for v in (EPS, 3 * LN2, -3 * LN2, 1.0, 0.0):
        pb.cpp(v)
    ones = pb._stack.enter_context(pb.nc.sbuf_tensor("ones1_f32", [1, 128], F32))
    pb.op("v", lambda e: e.memset(ones[:], 1.0), writes=[ones])
    c["_ones1_f32"] = ones
    for name in LAUNCH_ORDER:
        LAUNCH_DEFS[name][3](pb, c)
    pb.finish()
    return nc, pb


def kernel(**inputs):
    shared = host_prep_shared(inputs)
    cores = [host_prep_core(inputs, ci) for ci in range(N_CORES)]
    nc, pb = build_fused()
    in_maps = [dict(shared, **cores[ci]) for ci in range(N_CORES)]
    res = run_bass_kernel_spmd(nc, in_maps, core_ids=list(range(N_CORES)))
    out = np.concatenate(
        [np.asarray(res.results[ci]["out"], np.float32).reshape(2, T, D)
         for ci in range(N_CORES)], axis=0)
    pb.close()
    return out


# version 16
# speedup vs baseline: 1.3379x; 1.0658x over previous
import contextlib
import numpy as np
import concourse.bass as bass
import concourse.mybir as mybir

F32 = mybir.dt.float32
BF16 = mybir.dt.bfloat16
I32 = mybir.dt.int32
AF = mybir.ActivationFunctionType
ALU = mybir.AluOpType
AX = mybir.AxisListType

ENGINES = {"t": "tensor", "v": "vector", "s": "scalar", "p": "gpsimd", "q": "sync"}


def _base(x):
    if hasattr(x, "tensor"):
        return x.tensor.name
    if hasattr(x, "name"):
        return x.name
    return str(x)


def _key(x):
    if isinstance(x, tuple):
        return (_base(x[0]), str(x[1]))
    return (_base(x), "")


class PB:
    def __init__(self, nc: bass.Bass):
        self.nc = nc
        self._stack = contextlib.ExitStack()
        self.sem = {}
        self.cnt = {}
        self.waited = {}
        self.reg = {}
        self.sb_bytes = 0
        self.n_ops = 0
        for k in ENGINES:
            self.sem[k] = self._stack.enter_context(nc.semaphore("sem_" + k))
            self.cnt[k] = 0

    def sb(self, name, shape, dtype):
        st = self._scopes[-1] if getattr(self, "_scopes", None) else self._stack
        t = st.enter_context(self.nc.sbuf_tensor(name, list(shape), dtype))
        self.sb_bytes += int(np.prod(shape[1:])) * np.dtype(mybir.dt.np(dtype)).itemsize
        return t

    def ps(self, name, shape, dtype=F32):
        st = self._scopes[-1] if getattr(self, "_scopes", None) else self._stack
        return st.enter_context(self.nc.psum_tensor(name, list(shape), dtype))

    @contextlib.contextmanager
    def scope(self):
        if not hasattr(self, "_scopes"):
            self._scopes = []
        st = contextlib.ExitStack()
        self._scopes.append(st)
        try:
            yield
        finally:
            self._scopes.pop()
            st.close()
            self.barrier()

    def barrier(self):
        need = {k: v for k, v in self.cnt.items() if v > 0}
        for eng in ENGINES:
            self._wait(eng, dict(need))
        self.reg.clear()

    def dram(self, name, shape, dtype, kind="Internal"):
        return self.nc.dram_tensor(name, list(shape), dtype, kind=kind).ap()

    def close(self):
        self._stack.close()

    def _group_sem(self, group):
        k = "g:" + group
        if k not in self.sem:
            self.sem[k] = self._stack.enter_context(self.nc.semaphore("sg_" + group))
            self.cnt[k] = 0
        return k

    def _entries(self, tensor, sub, for_write):
        d = self.reg.get(tensor)
        if d is None:
            return []
        if sub == "":
            return list(d.values())
        out = []
        if "" in d:
            out.append(d[""])
        if sub in d:
            out.append(d[sub])
        return out

    def _wait(self, eng, need):
        e = getattr(self.nc, ENGINES[eng])
        for semkey, val in sorted(need.items()):
            if val <= 0:
                continue
            if semkey.startswith("g:"):
                val = self.cnt[semkey]
            if self.waited.get((eng, semkey), -1) >= val:
                continue
            e.wait_ge(self.sem[semkey], val)
            self.waited[(eng, semkey)] = val

    def op(self, eng, fn, reads=(), writes=(), dma_group=None):
        reads = [_key(x) for x in reads]
        writes = [_key(x) for x in writes]
        need = {}

        def add(sv):
            if sv and need.get(sv[0], -1) < sv[1]:
                need[sv[0]] = sv[1]

        for t, s in reads:
            for ent in self._entries(t, s, False):
                add(ent.get("w"))
        for t, s in writes:
            for ent in self._entries(t, s, True):
                add(ent.get("w"))
                for sk, v in ent.get("r", {}).items():
                    add((sk, v))
        self._wait(eng, need)
        inst = fn(getattr(self.nc, ENGINES[eng]))
        if dma_group is not None:
            semkey = self._group_sem(dma_group)
            inc = 16
        else:
            semkey = eng
            inc = 1
        self.cnt[semkey] += inc
        inst.then_inc(self.sem[semkey], inc)
        v = self.cnt[semkey]
        self.n_ops += 1
        for t, s in writes:
            d = self.reg.setdefault(t, {})
            if s == "":
                d.clear()
            ent = d.setdefault(s, {})
            ent["w"] = (semkey, v)
            ent["r"] = {}
        for t, s in reads:
            ent = self.reg.setdefault(t, {}).setdefault(s, {})
            r = ent.setdefault("r", {})
            if r.get(semkey, -1) < v:
                r[semkey] = v
        return inst

    def dma(self, eng, out, in_, group=None, reads=(), writes=(), **kw):
        g = group or ("b_" + _base(out))
        return self.op(eng, lambda e: e.dma_start(out=out, in_=in_, **kw),
                       reads=list(reads) or [in_], writes=list(writes) or [out],
                       dma_group=g)

    def dmat(self, eng, out, in_, group=None, reads=(), writes=()):
        g = group or ("b_" + _base(out))
        return self.op(eng, lambda e: e.dma_start_transpose(out=out, in_=in_),
                       reads=list(reads) or [in_], writes=list(writes) or [out],
                       dma_group=g)

    def cpp(self, val, dtype=F32):
        if not hasattr(self, "_CPP"):
            self._CPP = {}
        key = (float(val), dtype)
        if key not in self._CPP:
            t = self._stack.enter_context(
                self.nc.sbuf_tensor(f"cpp_{len(self._CPP)}", [128, 1], dtype))
            self.op("v", lambda e: e.memset(t[:], float(val)), writes=[t])
            self._CPP[key] = t
        return self._CPP[key][:]

    def finish(self):
        need = {k: v for k, v in self.cnt.items() if v > 0}
        self._wait("q", need)


def simrun(nc, inputs, outs):
    from concourse.bass_interp import CoreSim
    nc.finalize()
    sim = CoreSim(nc)
    for k, v in inputs.items():
        sim.tensor(k)[:] = v
    sim.simulate(check_with_hw=False)
    return {k: np.array(sim.mem_tensor(k)) for k in outs}


"""Emitters for the mk_recurrent_d2 megakernel. Per-core program pieces.

Per core: 2 sequences x 2048 tokens x 1024 features.
Layouts: TM = [token part, feat free]; FM = [feat part, token free].
All matmuls bf16 operands / fp32 PSUM accumulation; elementwise fp32.
"""
import numpy as np
import ml_dtypes

D = 1024
T = 2048
NSEQ = 2
NTOK = 4096
HID = 4096
NH = 16
DH = 64
EPS = 1e-5
LN2 = float(np.log(2.0))
BIGNEG8 = -8.0e33
DC = D // 128
TC = NTOK // 128
HC = HID // 128
SCALE_ATT = 0.125

BF = ml_dtypes.bfloat16


def host_prep_shared(inp):
    c = lambda a: np.ascontiguousarray(a)
    d = {}
    d["wz"] = c(inp["L0_attn_Wz"])
    d["wh"] = c(inp["L0_attn_Wh"])
    d["bz_t"] = c(np.asarray(inp["L0_attn_bz"]).reshape(DC, 128).T)
    d["bh_t"] = c(np.asarray(inp["L0_attn_bh"]).reshape(DC, 128).T)
    d["gamma_r"] = c(np.asarray(inp["L0_attn_gamma"]).reshape(1, D))
    d["beta_r"] = c(np.asarray(inp["L0_attn_beta"]).reshape(1, D))
    d["wfc"] = c(inp["L0_ffn_w_fc"])
    d["bfc_r"] = c(np.asarray(inp["L0_ffn_b_fc"]).reshape(1, HID))
    d["wproj"] = c(inp["L0_ffn_w_proj"])
    d["bproj_r"] = c(np.asarray(inp["L0_ffn_b_proj"]).reshape(1, D))
    for nm, key in [("wqr", "L1_attn_Wq"), ("wkr", "L1_attn_Wk"), ("wvr", "L1_attn_Wv")]:
        d[nm] = c(np.asarray(inp[key]).transpose(1, 0, 2).reshape(D, NH * DH))
    d["wp"] = c(inp["L1_attn_Wp"])
    d["bp_r"] = c(np.asarray(inp["L1_attn_bp"]).reshape(1, D))
    rt = np.asarray(inp["L1_attn_rel_table"])
    d["rt_row"] = c(rt.T.reshape(1, NH * 33))
    i_ = np.arange(128)[:, None]
    j_ = np.arange(128)[None, :]
    bucket_diag = np.clip(i_ - j_ + 16, 0, 32)
    bucket_sub = np.maximum(i_ - j_ - 112, 0)
    bt = np.empty((128, NH * 2, 128), np.float32)
    for h in range(NH):
        t0 = rt[bucket_diag, h]
        bt[:, 2 * h, :] = np.where(i_ <= j_, t0, np.float32(-1e30))
        bt[:, 2 * h + 1, :] = rt[bucket_sub, h]
    d["btiles"] = c(bt)
    def tile_T(w, mchunks, kchunks):
        wt = np.ascontiguousarray(w.T)
        K, O = wt.shape
        return c(wt.reshape(kchunks, 128, mchunks, 128).transpose(2, 0, 1, 3))
    d["w1t"] = tile_T(np.asarray(inp["L1_ffn_w1_w"]), HC, DC)
    d["w2t"] = tile_T(np.asarray(inp["L1_ffn_w2_w"]), HC, DC)
    d["h1t"] = tile_T(np.asarray(inp["L1_ffn_h1_w"]), HC, HC)
    d["h2t"] = tile_T(np.asarray(inp["L1_ffn_h2_w"]), HC, HC)
    w3t = np.ascontiguousarray(np.asarray(inp["L1_ffn_w3_w"]).T)
    d["w3t"] = c(w3t.reshape(HC, 128, D))
    for nm, key in [("s1c", "L1_ffn_w1_s"), ("s2c", "L1_ffn_w2_s"),
                    ("sh1c", "L1_ffn_h1_s"), ("sh2c", "L1_ffn_h2_s")]:
        d[nm] = c(np.asarray(inp[key]).reshape(HC, 128).T)
    d["s3_r"] = c(np.asarray(inp["L1_ffn_w3_s"]).reshape(1, D))
    d["ident_f"] = np.eye(128, dtype=np.float32)
    d["ident_b"] = np.eye(128, dtype=BF)
    return d


def host_prep_core(inp, core):
    x = np.asarray(inp["L0_attn_x"])[2 * core:2 * core + 2]
    return {"x": np.ascontiguousarray(x.reshape(NTOK, D))}


SHARED_SPECS = {
    "wz": ([D, D], F32), "wh": ([D, D], F32),
    "bz_t": ([128, DC], F32), "bh_t": ([128, DC], F32),
    "gamma_r": ([1, D], F32), "beta_r": ([1, D], F32),
    "wfc": ([HID, D], F32), "bfc_r": ([1, HID], F32),
    "wproj": ([D, HID], F32), "bproj_r": ([1, D], F32),
    "wqr": ([D, D], F32), "wkr": ([D, D], F32), "wvr": ([D, D], F32),
    "wp": ([D, D], F32), "bp_r": ([1, D], F32),
    "rt_row": ([1, NH * 33], F32),
    "btiles": ([128, NH * 2, 128], F32),
    "w1t": ([HC, DC, 128, 128], F32), "w2t": ([HC, DC, 128, 128], F32),
    "h1t": ([HC, HC, 128, 128], F32), "h2t": ([HC, HC, 128, 128], F32),
    "w3t": ([HC, 128, D], F32),
    "s1c": ([128, HC], F32), "s2c": ([128, HC], F32),
    "sh1c": ([128, HC], F32), "sh2c": ([128, HC], F32),
    "s3_r": ([1, D], F32),
    "ident_f": ([128, 128], F32), "ident_b": ([128, 128], BF16),
}


def bcast_row(pb, c, name, row_ap, width, dtype=BF16, row_reads=()):
    ones = c["_ones1_f32"]
    if row_ap.tensor.__class__.__name__.startswith("DRam") or "DRAM" in str(getattr(row_ap, "space", "")):
        srow = pb.sb(name + "_row", [1, width], F32)
        pb.dma("q", srow[:], row_ap, group=name + "_row")
        row_ap = srow[:]
        row_reads = [srow]
    dst = pb.sb(name, [128, width], dtype)
    psb = c.get("_psb")
    if psb is None:
        psb = pb.ps("ps_bcast", [128, 512], F32)
        c["_psb"] = psb
    pkey = psb if not isinstance(psb, tuple) else psb
    for j in range(0, width, 512):
        w = min(512, width - j)
        pb.op("t", lambda e, j=j, w=w, row_ap=row_ap: e.matmul(
            psb[:, 0:w] if not isinstance(psb, tuple) else None,
            ones[:], row_ap[:, j:j + w], start=True, stop=True),
              reads=[ones] + list(row_reads), writes=[psb])
        pb.op("v", lambda e, j=j, w=w: e.tensor_copy(out=dst[:, j:j + w], in_=psb[:, 0:w]),
              reads=[psb], writes=[(dst, j)])
    return dst


def ln_stats(pb, c, sx_ap, w, pref, reads):
    if pref not in c:
        c[pref] = dict(
            sum=pb.sb(pref + "_sum", [128, 1], F32),
            ssq=pb.sb(pref + "_ssq", [128, 1], F32),
            sqs=pb.sb(pref + "_sqs", [128, w], F32),
            mean=pb.sb(pref + "_mean", [128, 1], F32),
            m2=pb.sb(pref + "_m2", [128, 1], F32),
            var=pb.sb(pref + "_var", [128, 1], F32),
            lv=pb.sb(pref + "_lv", [128, 1], F32),
            rstd=pb.sb(pref + "_rstd", [128, 1], F32),
            nmr=pb.sb(pref + "_nmr", [128, 1], F32),
        )
    t = c[pref]
    pb.op("v", lambda e: e.reduce_sum(out=t["sum"][:], in_=sx_ap, axis=AX.X),
          reads=reads, writes=[t["sum"]])
    pb.op("s", lambda e: e.activation(out=t["sqs"][:, 0:w], in_=sx_ap, func=AF.Square,
                                      accum_out=t["ssq"][:]),
          reads=reads, writes=[t["sqs"], t["ssq"]])
    pb.op("v", lambda e: e.tensor_scalar_mul(out=t["mean"][:], in0=t["sum"][:], scalar1=1.0 / w),
          reads=[t["sum"]], writes=[t["mean"]])
    pb.op("v", lambda e: e.tensor_tensor(out=t["m2"][:], in0=t["mean"][:], in1=t["mean"][:],
                                         op=ALU.mult),
          reads=[t["mean"]], writes=[t["m2"]])
    pb.op("v", lambda e: e.scalar_tensor_tensor(out=t["var"][:], in0=t["ssq"][:],
                                                scalar=1.0 / w, in1=t["m2"][:],
                                                op0=ALU.mult, op1=ALU.subtract),
          reads=[t["ssq"], t["m2"]], writes=[t["var"]])
    beps = pb.cpp(EPS)
    pb.op("s", lambda e: e.activation(out=t["lv"][:], in_=t["var"][:], func=AF.Ln, bias=beps),
          reads=[t["var"], beps], writes=[t["lv"]])
    pb.op("s", lambda e: e.activation(out=t["rstd"][:], in_=t["lv"][:], func=AF.Exp, scale=-0.5),
          reads=[t["lv"]], writes=[t["rstd"]])
    pb.op("v", lambda e: e.scalar_tensor_tensor(out=t["nmr"][:], in0=t["mean"][:], scalar=-1.0,
                                                in1=t["rstd"][:], op0=ALU.mult, op1=ALU.mult),
          reads=[t["mean"], t["rstd"]], writes=[t["nmr"]])
    return t


def quantize_tile(pb, c, sx_ap, w, out_bf_ap, pref, reads, out_writes):
    nb = w // 32
    if pref not in c:
        c[pref] = dict(
            am=pb.sb(pref + "_am", [128, nb, 1], F32),
            ef=pb.sb(pref + "_ef", [128, nb, 1], F32),
            ei=pb.sb(pref + "_ei", [128, nb, 1], I32),
            el=pb.sb(pref + "_el", [128, nb, 1], F32),
            inv=pb.sb(pref + "_inv", [128, nb, 1], F32),
            scl=pb.sb(pref + "_scl", [128, nb, 1], F32),
            y=pb.sb(pref + "_y", [128, w], F32),
            r=pb.sb(pref + "_r", [128, w], F32),
        )
    t = c[pref]
    v3 = sx_ap.rearrange("p (b w) -> p b w", w=32)
    pb.op("v", lambda e: e.tensor_reduce(out=t["am"][:, 0:nb], in_=v3, axis=AX.X, op=ALU.max,
                                         apply_absolute_value=True),
          reads=reads, writes=[t["am"]])
    pb.op("v", lambda e: e.tensor_scalar_max(out=t["am"][:, 0:nb], in0=t["am"][:, 0:nb],
                                             scalar1=float(2.0 ** -63)),
          reads=[t["am"]], writes=[t["am"]])
    pb.op("s", lambda e: e.activation(out=t["el"][:, 0:nb], in_=t["am"][:, 0:nb], func=AF.Ln,
                                      scale=1.0),
          reads=[t["am"]], writes=[t["el"]])
    pb.op("v", lambda e: e.tensor_scalar_mul(out=t["el"][:, 0:nb], in0=t["el"][:, 0:nb],
                                             scalar1=float(1.0 / LN2)),
          reads=[t["el"]], writes=[t["el"]])
    pb.op("v", lambda e: e.tensor_copy(out=t["ei"][:, 0:nb], in_=t["el"][:, 0:nb]),
          reads=[t["el"]], writes=[t["ei"]])
    pb.op("v", lambda e: e.tensor_copy(out=t["ef"][:, 0:nb], in_=t["ei"][:, 0:nb]),
          reads=[t["ei"]], writes=[t["ef"]])
    pb.op("v", lambda e: e.tensor_tensor(out=t["el"][:, 0:nb], in0=t["el"][:, 0:nb],
                                         in1=t["ef"][:, 0:nb], op=ALU.is_lt),
          reads=[t["el"], t["ef"]], writes=[t["el"]])
    pb.op("v", lambda e: e.tensor_tensor(out=t["ef"][:, 0:nb], in0=t["ef"][:, 0:nb],
                                         in1=t["el"][:, 0:nb], op=ALU.subtract),
          reads=[t["ef"], t["el"]], writes=[t["ef"]])
    b3p = pb.cpp(3 * LN2)
    b3n = pb.cpp(-3 * LN2)
    pb.op("s", lambda e: e.activation(out=t["inv"][:, 0:nb], in_=t["ef"][:, 0:nb], func=AF.Exp,
                                      scale=-LN2, bias=b3p),
          reads=[t["ef"], b3p], writes=[t["inv"]])
    pb.op("s", lambda e: e.activation(out=t["scl"][:, 0:nb], in_=t["ef"][:, 0:nb], func=AF.Exp,
                                      scale=LN2, bias=b3n),
          reads=[t["ef"], b3n], writes=[t["scl"]])
    pb.op("v", lambda e: e.tensor_tensor(out=t["y"][:, 0:w].rearrange("p (b w) -> p b w", w=32),
                                         in0=v3, in1=t["inv"][:, 0:nb].broadcast_to([128, nb, 32]),
                                         op=ALU.mult),
          reads=reads + [t["inv"]], writes=[t["y"]])
    MAGIC = float(3 << 22)
    pb.op("v", lambda e: e.tensor_scalar_add(out=t["r"][:, 0:w], in0=t["y"][:, 0:w],
                                             scalar1=MAGIC),
          reads=[t["y"]], writes=[t["r"]])
    pb.op("v", lambda e: e.tensor_scalar_add(out=t["r"][:, 0:w], in0=t["r"][:, 0:w],
                                             scalar1=-MAGIC),
          reads=[t["r"]], writes=[t["r"]])
    pb.op("v", lambda e: e.tensor_tensor(out=out_bf_ap.rearrange("p (b w) -> p b w", w=32),
                                         in0=t["r"][:, 0:w].rearrange("p (b w) -> p b w", w=32),
                                         in1=t["scl"][:, 0:nb].broadcast_to([128, nb, 32]),
                                         op=ALU.mult),
          reads=[t["r"], t["scl"]], writes=out_writes)


def emit_premix(pb, c):
    io = c["io"]
    with pb.scope():
        sx = [pb.sb(f"pm_sx{i}", [128, 1024], F32) for i in range(2)]
        sxn = [pb.sb(f"pm_sxn{i}", [128, 1024], F32) for i in range(2)]
        sxnb = [pb.sb(f"pm_sxnb{i}", [128, 1024], BF16) for i in range(2)]
        sqb = [pb.sb(f"pm_sqb{i}", [128, 1024], BF16) for i in range(2)]

        def pre_tile(i):
            b = i % 2
            sq = i // 16
            pb.dma("q", sx[b][:], io["x"][i * 128:(i + 1) * 128, :], group=f"pm_x{b}")
            t = ln_stats(pb, c, sx[b][:], D, f"pm_ln{b}", [sx[b]])
            pb.op("s", lambda e, b=b, t=t: e.activation(out=sxn[b][:], in_=sx[b][:],
                                                        func=AF.Identity,
                                                        scale=t["rstd"][:], bias=t["nmr"][:]),
                  reads=[sx[b], t["rstd"], t["nmr"]], writes=[sxn[b]])
            pb.op("s", lambda e, b=b: e.activation(out=sxnb[b][:], in_=sxn[b][:], func=AF.Copy),
                  reads=[sxn[b]], writes=[sxnb[b]])
            pb.dma("p", io["xn_f32"][i * 128:(i + 1) * 128, :], sxn[b][:], group="xn_f32",
                   reads=[sxn[b]], writes=[(io["xn_f32"], sq)])
            pb.dma("p", io["xn_bf"][i * 128:(i + 1) * 128, :], sxnb[b][:], group="xn_bf",
                   reads=[sxnb[b]], writes=[(io["xn_bf"], sq)])
            quantize_tile(pb, c, sxn[b][:], D, sqb[b][:], f"pm_q{b}", [sxn[b]], [sqb[b]])
            pb.dma("p", io["xq_bf"][i * 128:(i + 1) * 128, :], sqb[b][:], group="xq_bf",
                   reads=[sqb[b]], writes=[(io["xq_bf"], sq)])

        def wfc_tile(j):
            b = j % 2
            pb.dma("q", sx[b][:], io["wfc"][j * 128:(j + 1) * 128, :], group=f"pm_x{b}")
            quantize_tile(pb, c, sx[b][:], D, sqb[b][:], f"pm_q{b}", [sx[b]], [sqb[b]])
            pb.dma("p", io["wqfc_bf"][j * 128:(j + 1) * 128, :], sqb[b][:], group="wqfc_bf")

        def wproj_tile(j):
            r, qq = j // 4, j % 4
            b = j % 2
            pb.dma("q", sx[b][:], io["wproj"][r * 128:(r + 1) * 128,
                                              qq * 1024:(qq + 1) * 1024], group=f"pm_x{b}")
            quantize_tile(pb, c, sx[b][:], D, sqb[b][:], f"pm_q{b}", [sx[b]], [sqb[b]])
            pb.dma("p", io["wqproj_bf"][r * 128:(r + 1) * 128, qq * 1024:(qq + 1) * 1024],
                   sqb[b][:], group="wqproj_bf")

        psA = pb.ps("mx_psA", [128, T], F32)
        psB = pb.ps("mx_psB", [128, T], F32)
        c["_psb"] = psA
        gamma_b = bcast_row(pb, c, "gamma_b", io["gamma_r"][:], D, F32)
        beta_b = bcast_row(pb, c, "beta_b", io["beta_r"][:], D, F32)
        sbz = pb.sb("mx_bz", [128, DC], F32)
        sbh = pb.sb("mx_bh", [128, DC], F32)
        snbz = pb.sb("mx_nbz", [128, DC], F32)
        sbh05 = pb.sb("mx_bh05", [128, DC], F32)
        pb.dma("q", sbz[:], io["bz_t"])
        pb.dma("q", sbh[:], io["bh_t"])
        pb.op("v", lambda e: e.tensor_scalar_mul(out=snbz[:], in0=sbz[:], scalar1=-1.0),
              reads=[sbz], writes=[snbz])
        pb.op("v", lambda e: e.tensor_scalar_add(out=sbh05[:], in0=sbh[:], scalar1=0.5),
              reads=[sbh], writes=[sbh05])
        sid = pb.sb("mx_id", [128, 128], F32)
        pb.dma("q", sid[:], io["ident_f"])
        xnfm = pb.sb("mx_xnfm", [128, DC, T], BF16)
        wzs = [pb.sb(f"mx_wz{i}", [128, DC, 128], BF16) for i in range(2)]
        whs = [pb.sb(f"mx_wh{i}", [128, DC, 128], BF16) for i in range(2)]
        sc = pb.sb("mx_c", [128, T], F32)
        sz = pb.sb("mx_z", [128, T], F32)
        sr = pb.sb("mx_r", [128, T], F32)
        ss = pb.sb("mx_s", [128, T], F32)
        sh = pb.sb("mx_h", [128, T], F32)
        hstg = [pb.sb(f"mx_hs{i}", [128, 128], F32) for i in range(4)]
        hld = [pb.sb(f"mx_hl{i}", [128, 1024], F32) for i in range(2)]
        sxn_r = [pb.sb(f"mx_xn{i}", [128, 1024], F32) for i in range(2)]
        n1 = [pb.sb(f"mx_n1{i}", [128, 1024], F32) for i in range(2)]
        wzr = io["wz"].rearrange("(kc p) h -> p kc h", p=128)
        whr = io["wh"].rearrange("(kc p) h -> p kc h", p=128)

        def hln_tile(s, j):
            jb = j % 2
            i = s * 16 + j
            pb.dma("q", hld[jb][:], io["h_dram"][i * 128:(i + 1) * 128, :],
                   group=f"mx_hld{jb}", reads=[(io["h_dram"], f"{s}.{j}")])
            pb.dma("q", sxn_r[jb][:], io["xn_f32"][i * 128:(i + 1) * 128, :],
                   group=f"mx_sxn{jb}", reads=[(io["xn_f32"], s)])
            t = ln_stats(pb, c, hld[jb][:], D, f"mx_ln{jb}", [hld[jb]])
            pb.op("s", lambda e, jb=jb, t=t: e.activation(out=n1[jb][:], in_=hld[jb][:],
                                                          func=AF.Identity,
                                                          scale=t["rstd"][:], bias=t["nmr"][:]),
                  reads=[hld[jb], t["rstd"], t["nmr"]], writes=[n1[jb]])
            pb.op("v", lambda e, jb=jb: e.tensor_tensor(out=n1[jb][:], in0=n1[jb][:],
                                                        in1=gamma_b[:], op=ALU.mult),
                  reads=[n1[jb], gamma_b], writes=[n1[jb]])
            pb.op("v", lambda e, jb=jb: e.tensor_tensor(out=n1[jb][:], in0=n1[jb][:],
                                                        in1=beta_b[:], op=ALU.add),
                  reads=[n1[jb], beta_b], writes=[n1[jb]])
            pb.op("v", lambda e, jb=jb: e.tensor_tensor(out=n1[jb][:], in0=n1[jb][:],
                                                        in1=sxn_r[jb][:], op=ALU.add),
                  reads=[n1[jb], sxn_r[jb]], writes=[n1[jb]])
            pb.dma("p", io["s0_f32"][i * 128:(i + 1) * 128, :], n1[jb][:],
                   group=f"s0_f32{jb}")

        def mixer_seq(s, filler, spill):
            for kc in range(DC):
                pb.dmat("q", xnfm[:, kc, :],
                        io["xn_bf"][s * T:(s + 1) * T, kc * 128:(kc + 1) * 128],
                        group="mx_xnfm", reads=[(io["xn_bf"], s)],
                        writes=[(xnfm, kc)])
            for m in range(DC):
                b = m % 2
                pb.dma("p", wzs[b][:], wzr[:, :, m * 128:(m + 1) * 128], group=f"mx_wz{b}")
                pb.dma("p", whs[b][:], whr[:, :, m * 128:(m + 1) * 128], group=f"mx_wh{b}")
                for kc in range(DC):
                    for n in range(4):
                        pb.op("t", lambda e, kc=kc, n=n, b=b: e.matmul(
                            psA[:, n * 512:(n + 1) * 512], wzs[b][:, kc, :],
                            xnfm[:, kc, n * 512:(n + 1) * 512],
                            start=(kc == 0), stop=(kc == DC - 1)),
                            reads=[wzs[b], (xnfm, kc)], writes=[(psA, n)])
                emit_spill(spill)
                for kc in range(DC):
                    for n in range(4):
                        pb.op("t", lambda e, kc=kc, n=n, b=b: e.matmul(
                            psB[:, n * 512:(n + 1) * 512], whs[b][:, kc, :],
                            xnfm[:, kc, n * 512:(n + 1) * 512],
                            start=(kc == 0), stop=(kc == DC - 1)),
                            reads=[whs[b], (xnfm, kc)], writes=[(psB, n)])
                pA = [(psA, n) for n in range(4)]
                pB = [(psB, n) for n in range(4)]
                pb.op("s", lambda e, m=m: e.activation(out=sc[:], in_=psA[:], func=AF.Sigmoid,
                                                       scale=-1.0, bias=snbz[:, m:m + 1]),
                      reads=pA + [snbz], writes=[sc])
                pb.op("s", lambda e, m=m: e.activation(out=sz[:], in_=psA[:], func=AF.Sigmoid,
                                                       bias=sbz[:, m:m + 1]),
                      reads=pA + [sbz], writes=[sz])
                pb.op("s", lambda e, m=m: e.activation(out=sr[:], in_=psB[:], func=AF.Identity,
                                                       bias=sbh05[:, m:m + 1]),
                      reads=pB + [sbh05], writes=[sr])
                pb.op("s", lambda e, m=m: e.activation(out=ss[:], in_=psB[:], func=AF.Sigmoid,
                                                       bias=sbh[:, m:m + 1]),
                      reads=pB + [sbh], writes=[ss])
                pb.op("v", lambda e: e.tensor_max(out=ss[:], in0=ss[:], in1=sr[:]),
                      reads=[ss, sr], writes=[ss])
                pb.op("v", lambda e: e.tensor_tensor(out=sz[:], in0=sz[:], in1=ss[:],
                                                     op=ALU.mult),
                      reads=[sz, ss], writes=[sz])
                pb.op("v", lambda e: e.tensor_tensor_scan(out=sh[:], data0=sc[:], data1=sz[:],
                                                          initial=0.0, op0=ALU.mult,
                                                          op1=ALU.add),
                      reads=[sc, sz], writes=[sh])
                for it in filler[m::DC]:
                    it()
                spill[0] = (s, m)

        def emit_spill(spill):
            if spill[0] is None:
                return
            s, m = spill[0]
            spill[0] = None
            for j in range(16):
                jb = j % 4
                pb.op("t", lambda e, j=j: e.transpose(psB[:, j * 128:(j + 1) * 128],
                                                      sh[:, j * 128:(j + 1) * 128], sid[:]),
                      reads=[sh, sid], writes=[(psB, j // 4)])
                pb.op("v", lambda e, j=j, jb=jb: e.tensor_copy(
                    out=hstg[jb][:], in_=psB[:, j * 128:(j + 1) * 128]),
                    reads=[(psB, j // 4)], writes=[hstg[jb]])
                pb.dma("p", io["h_dram"][(s * 16 + j) * 128:(s * 16 + j + 1) * 128,
                                         m * 128:(m + 1) * 128],
                       hstg[jb][:], group=f"h_dram{jb}",
                       reads=[hstg[jb]], writes=[(io["h_dram"], f"{s}.{j}")])

        spill = [None]
        pre0 = [(lambda i=i: pre_tile(i)) for i in range(16)]
        for it in pre0:
            it()
        fill0 = [(lambda i=i: pre_tile(i)) for i in range(16, 32)] \
            + [(lambda j=j: wproj_tile(j)) for j in range(32)]
        mixer_seq(0, fill0, spill)
        fill1 = [(lambda j=j: wfc_tile(j)) for j in range(32)] \
            + [(lambda s=0, j=j: hln_tile(s, j)) for j in range(16)]
        mixer_seq(1, fill1, spill)
        emit_spill(spill)
        for j in range(16):
            hln_tile(1, j)


SCRATCH_SPECS = {
    "x": ([NTOK, D], F32),
    "xn_f32": ([NTOK, D], F32), "xn_bf": ([NTOK, D], BF16),
    "xq_bf": ([NTOK, D], BF16),
    "wqfc_bf": ([HID, D], BF16), "wqproj_bf": ([D, HID], BF16),
    "s0_f32": ([NTOK, D], F32),
    "x1_f32": ([NTOK, D], F32), "xn2_bf": ([NTOK, D], BF16),
    "s2_f32": ([NTOK, D], F32),
    "out": ([NTOK, D], F32),
    "h_dram": ([NTOK, D], F32),
    "w1_bf": ([HC, DC, 128, 128], BF16), "w2_bf": ([HC, DC, 128, 128], BF16),
    "h1_bf": ([HC, HC, 128, 128], BF16), "h2_bf": ([HC, HC, 128, 128], BF16),
    "w3_bf": ([HC, 128, D], BF16),
    "hq_bf": ([NTOK, HID], BF16),
    "gx_dram": ([HC, 128, T], BF16),
}

LAUNCH_DEFS = {}


def build_launch(name, extra_core_id=None):
    ins, outs, internals, emit_fn = LAUNCH_DEFS[name]
    nc = bass.Bass("TRN2", target_bir_lowering=False)
    pb = PB(nc)
    io = {}
    for nm in ins:
        shape, dt = SHARED_SPECS.get(nm) or SCRATCH_SPECS[nm]
        io[nm] = pb.dram(nm, shape, dt, kind="ExternalInput")
    for nm in outs:
        shape, dt = SCRATCH_SPECS[nm]
        io[nm] = pb.dram(nm, shape, dt, kind="ExternalOutput")
    for nm in internals:
        shape, dt = SCRATCH_SPECS[nm]
        io[nm] = pb.dram(nm, shape, dt, kind="Internal")
    c = {"io": io}
    for v in (EPS, 3 * LN2, -3 * LN2, 1.0, 0.0):
        pb.cpp(v)
    ones = pb._stack.enter_context(pb.nc.sbuf_tensor("ones1_f32", [1, 128], F32))
    pb.op("v", lambda e: e.memset(ones[:], 1.0), writes=[ones])
    c["_ones1_f32"] = ones
    emit_fn(pb, c)
    pb.finish()
    return nc, pb




def build_wconv_pairs(io):
    pairs = []
    for nm, rows, cols in (("w1t", 256, 16384), ("w2t", 256, 16384),
                           ("h1t", 1024, 16384), ("h2t", 1024, 16384),
                           ("w3t", 4096, 1024)):
        srcf = io[nm]
        dstf = io[{"w1t": "w1_bf", "w2t": "w2_bf", "h1t": "h1_bf",
                   "h2t": "h2_bf", "w3t": "w3_bf"}[nm]]
        if nm == "w3t":
            s2d = srcf.rearrange("kc k o -> (kc k) o")
            d2d = dstf.rearrange("kc k o -> (kc k) o")
            CT = 1024
        else:
            s2d = srcf.rearrange("m kc k o -> (m kc) (k o)")
            d2d = dstf.rearrange("m kc k o -> (m kc) (k o)")
            CT = 2048
        for r in range(rows // 128):
            for c0 in range(0, cols, CT):
                pairs.append((s2d[r * 128:(r + 1) * 128, c0:c0 + CT],
                              d2d[r * 128:(r + 1) * 128, c0:c0 + CT], CT))
    return pairs


def emit_ffn0(pb, c):
    io = c["io"]
    wconv_pairs = build_wconv_pairs(io)
    if c.get("fused"):
        c["wconv_rest_list"] = wconv_pairs[64:]
        wconv_pairs = wconv_pairs[:64]
    wci = [0]

    def emit_wconv(k):
        for _ in range(k):
            if wci[0] >= len(wconv_pairs):
                return
            sap, dap, ct = wconv_pairs[wci[0]]
            b = wci[0] % 2
            wci[0] += 1
            cv = c["_wcv"][b]
            cvb = c["_wcvb"][b]
            pb.dma("q", cv[:, 0:ct], sap, group=f"wcv{b}")
            pb.op("s", lambda e, cv=cv, cvb=cvb, ct=ct: e.activation(
                out=cvb[:, 0:ct], in_=cv[:, 0:ct], func=AF.Copy),
                reads=[cv], writes=[cvb])
            pb.dma("s", dap, cvb[:, 0:ct], group=f"wcvo{b}", reads=[cvb],
                   writes=[dap])
    with pb.scope():
        c["_wcv"] = [pb.sb(f"f0_wcv{i}", [128, 2048], F32) for i in range(2)]
        c["_wcvb"] = [pb.sb(f"f0_wcvb{i}", [128, 2048], BF16) for i in range(2)]
        psF = [pb.ps(f"f0_psF{i}", [128, T], F32) for i in range(2)]
        c["_psb"] = psF[0]
        bfc_b = bcast_row(pb, c, "bfc_b", io["bfc_r"][:], HID, BF16)
        wqfcT = pb.sb("f0_wfcT", [128, DC, HID], BF16)
        for kc in range(DC):
            pb.dmat("q", wqfcT[:, kc, :], io["wqfc_bf"][:, kc * 128:(kc + 1) * 128],
                    group="f0_wfcT")
        xqfm = pb.sb("f0_xqfm", [128, DC, 1024], BF16)
        sth = [pb.sb(f"f0_sth{i}", [128, T], F32) for i in range(2)]
        shq = [pb.sb(f"f0_shq{i}", [128, T], BF16) for i in range(2)]
        for grp in range(4):
            for kc in range(DC):
                pb.dmat("q", xqfm[:, kc, :],
                        io["xq_bf"][grp * 1024:(grp + 1) * 1024, kc * 128:(kc + 1) * 128],
                        group="f0_xqfm")
            for tcl in range(8):
                tc = grp * 8 + tcl
                for hh in range(2):
                    b = (tcl * 2 + hh) % 2
                    for kc in range(DC):
                        for n in range(4):
                            pb.op("t", lambda e, kc=kc, n=n, hh=hh, tcl=tcl, b=b: e.matmul(
                                psF[b][:, n * 512:(n + 1) * 512],
                                xqfm[:, kc, tcl * 128:(tcl + 1) * 128],
                                wqfcT[:, kc, hh * 2048 + n * 512:hh * 2048 + (n + 1) * 512],
                                start=(kc == 0), stop=(kc == DC - 1)),
                                reads=[(xqfm, kc), wqfcT], writes=[(psF[b], n)])
                    pp = [(psF[b], n) for n in range(4)]
                    pb.op("v", lambda e, b=b, hh=hh: e.tensor_tensor(
                        out=sth[b][:], in0=psF[b][:], in1=bfc_b[:, hh * 2048:(hh + 1) * 2048],
                        op=ALU.add),
                        reads=pp + [bfc_b], writes=[sth[b]])
                    pb.op("s", lambda e, b=b: e.activation(out=sth[b][:], in_=sth[b][:],
                                                           func=AF.Gelu),
                          reads=[sth[b]], writes=[sth[b]])
                    quantize_tile(pb, c, sth[b][:], T, shq[b][:], f"f0_q{b}",
                                  [sth[b]], [shq[b]])
                    pb.dma("p", io["hq_bf"][tc * 128:(tc + 1) * 128,
                                            hh * 2048:(hh + 1) * 2048],
                           shq[b][:], group=f"hq_bf{b}")
                    emit_wconv(2)
    with pb.scope():
        c["_wcv"] = [pb.sb(f"f0p_wcv{i}", [128, 2048], F32) for i in range(2)]
        c["_wcvb"] = [pb.sb(f"f0p_wcvb{i}", [128, 2048], BF16) for i in range(2)]
        psP = [pb.ps(f"f0_psP{i}", [128, D], F32) for i in range(2)]
        c["_psb"] = psP[0]
        bproj_b = bcast_row(pb, c, "bproj_b", io["bproj_r"][:], D, BF16)
        wqpT = pb.sb("f0_wpT", [128, HC, D], BF16)
        for kc in range(HC):
            pb.dmat("q", wqpT[:, kc, :], io["wqproj_bf"][:, kc * 128:(kc + 1) * 128],
                    group="f0_wpT")
        hqfm = pb.sb("f0_hqfm", [128, HC, 1024], BF16)
        sgl = [pb.sb(f"f0_sgl{i}", [128, D], F32) for i in range(2)]
        sx = [pb.sb(f"f0_sx{i}", [128, D], F32) for i in range(2)]
        ss0 = [pb.sb(f"f0_ss0{i}", [128, D], F32) for i in range(2)]
        sxnb = [pb.sb(f"f0_sxnb{i}", [128, D], BF16) for i in range(2)]
        for grp in range(4):
            for kc in range(HC):
                pb.dmat("q", hqfm[:, kc, :],
                        io["hq_bf"][grp * 1024:(grp + 1) * 1024, kc * 128:(kc + 1) * 128],
                        group="f0_hqfm")
            for tcl in range(8):
                tc = grp * 8 + tcl
                b = tcl % 2
                for kc in range(HC):
                    for n in range(2):
                        pb.op("t", lambda e, kc=kc, n=n, tcl=tcl, b=b: e.matmul(
                            psP[b][:, n * 512:(n + 1) * 512],
                            hqfm[:, kc, tcl * 128:(tcl + 1) * 128],
                            wqpT[:, kc, n * 512:(n + 1) * 512],
                            start=(kc == 0), stop=(kc == HC - 1)),
                            reads=[(hqfm, kc), wqpT], writes=[(psP[b], n)])
                pp = [(psP[b], n) for n in range(2)]
                pb.op("v", lambda e, b=b: e.tensor_tensor(out=sgl[b][:], in0=psP[b][:],
                                                          in1=bproj_b[:], op=ALU.add),
                      reads=pp + [bproj_b], writes=[sgl[b]])
                pb.op("s", lambda e, b=b: e.activation(out=sgl[b][:], in_=sgl[b][:],
                                                       func=AF.Silu),
                      reads=[sgl[b]], writes=[sgl[b]])
                pb.dma("q", sx[b][:], io["x"][tc * 128:(tc + 1) * 128, :], group=f"f0_x{b}")
                pb.dma("q", ss0[b][:], io["s0_f32"][tc * 128:(tc + 1) * 128, :],
                       group=f"f0_s0{b}")
                pb.op("v", lambda e, b=b: e.tensor_tensor(out=sgl[b][:], in0=sgl[b][:],
                                                          in1=ss0[b][:], op=ALU.mult),
                      reads=[sgl[b], ss0[b]], writes=[sgl[b]])
                pb.op("v", lambda e, b=b: e.tensor_tensor(out=sx[b][:], in0=sx[b][:],
                                                          in1=sgl[b][:], op=ALU.add),
                      reads=[sx[b], sgl[b]], writes=[sx[b]])
                pb.dma("p", io["x1_f32"][tc * 128:(tc + 1) * 128, :], sx[b][:],
                       group=f"x1_f32{b}")
                t = ln_stats(pb, c, sx[b][:], D, f"f0_ln{b}", [sx[b]])
                pb.op("s", lambda e, b=b, t=t: e.activation(out=sxnb[b][:], in_=sx[b][:],
                                                            func=AF.Identity,
                                                            scale=t["rstd"][:], bias=t["nmr"][:]),
                      reads=[sx[b], t["rstd"], t["nmr"]], writes=[sxnb[b]])
                pb.dma("p", io["xn2_bf"][tc * 128:(tc + 1) * 128, :], sxnb[b][:],
                       group=f"xn2_bf{b}")
                emit_wconv(2)
        emit_wconv(len(wconv_pairs))


LAUNCH_DEFS["premix"] = (
    ["x", "wz", "wh", "bz_t", "bh_t", "gamma_r", "beta_r", "ident_f", "wfc", "wproj"],
    ["xn_f32", "xn_bf", "xq_bf", "wqfc_bf", "wqproj_bf", "s0_f32"],
    ["h_dram"], emit_premix)


LAUNCH_DEFS["ffn0"] = (
    ["xq_bf", "wqfc_bf", "wqproj_bf", "bfc_r", "bproj_r", "s0_f32", "x",
     "w1t", "w2t", "h1t", "h2t", "w3t"],
    ["x1_f32", "xn2_bf", "w1_bf", "w2_bf", "h1_bf", "h2_bf", "w3_bf"],
    ["hq_bf"], emit_ffn0)


def emit_attn(pb, c):
    io = c["io"]
    STRIP_OFF = [0]
    for kc in range(16):
        STRIP_OFF.append(STRIP_OFF[-1] + (T - 128 * kc))
    NPT = STRIP_OFF[-1]

    with pb.scope():
        psB4 = pb.ps("at_psB4", [128, 2048], F32)
        psQ = pb.ps("at_psQ", [128, 1024], F32)
        psX = pb.ps("at_psX", [128, 1024], F32)
        c["_psb"] = psX
        btiles = pb.sb("at_bt", [128, NH * 2, 128], BF16)
        pb.dma("p", btiles[:], io["btiles"])
        srow = pb.sb("at_row", [1, NH * 33], F32)
        pb.dma("q", srow[:], io["rt_row"])
        srow3 = srow[:].rearrange("p (h u) -> p h u", u=33)
        bmaxh = pb.sb("at_bmaxh", [1, NH, 1], F32)
        pb.op("v", lambda e: e.tensor_reduce(out=bmaxh[:], in_=srow3[:, :, 0:17],
                                             axis=AX.X, op=ALU.max),
              reads=[srow], writes=[bmaxh])
        gbm = pb.sb("at_gbm", [1, 1], F32)
        pb.op("v", lambda e: e.tensor_reduce(
            out=gbm[:], in_=bmaxh[:].rearrange("p h u -> p (h u)"),
            axis=AX.X, op=ALU.max),
            reads=[bmaxh], writes=[gbm])
        onescol = pb.sb("at_onec", [128, 1], F32)
        pb.op("v", lambda e: e.memset(onescol[:], 1.0), writes=[onescol])
        bp_b = bcast_row(pb, c, "bp_b", io["bp_r"][:], D, F32)
        ones = c["_ones1_f32"]
        rel0bc = pb.sb("at_r0bc", [128, NH], F32)
        pb.op("t", lambda e: e.matmul(psX[:, 0:NH], ones[:], srow3[0:1, :, 0],
                                      start=True, stop=True),
              reads=[ones, srow], writes=[(psX, 0)])
        pb.op("v", lambda e: e.tensor_copy(out=rel0bc[:], in_=psX[:, 0:NH]),
              reads=[(psX, 0)], writes=[rel0bc])
        for h in range(NH):
            pb.op("v", lambda e, h=h: e.tensor_scalar(
                out=btiles[:, 2 * h:2 * h + 2, :], in0=btiles[:, 2 * h:2 * h + 2, :],
                scalar1=rel0bc[:, h:h + 1], scalar2=8.0,
                op0=ALU.subtract, op1=ALU.mult),
                reads=[(btiles, h), rel0bc], writes=[(btiles, h)])
        nrm = pb.sb("at_nrm", [1, 4], F32)
        u3 = pb.sb("at_u3", [1, 2], F32)
        rel0p = pb.sb("at_rel0", [1, 2], F32)
        upair = pb.sb("at_up", [1, 4], F32)
        mxq = pb.sb("at_mxq", [128, 1], F32)
        mxk = pb.sb("at_mxk", [128, 1], F32)
        rmt = pb.sb("at_rmt", [128, 1], F32)
        mx2 = pb.sb("at_mx2", [128, 1], F32)
        bU = [pb.sb(f"at_bU{i}", [128, 2], F32) for i in range(2)]
        linv = pb.sb("at_linv", [1, 2048], F32)
        linv2 = pb.sb("at_linv2", [1, 2048], F32)
        ss2 = [pb.sb("at_s2_0", [128, D], F32)] * 2
        vaug = pb.sb("at_v", [128, 16, NH * 65 + 63], BF16)
        o_fm = pb.sb("at_o", [128, DC, T], BF16)
        wqr3 = io["wqr"].rearrange("(kc p) o -> p kc o", p=128)
        wkr3 = io["wkr"].rearrange("(kc p) o -> p kc o", p=128)

        for s in range(NSEQ):
            with pb.scope():
                xn2fm = pb.sb(f"at_xn2{s}", [128, DC, T], BF16)
                for kc in range(DC):
                    pb.dmat("q", xn2fm[:, kc, :],
                            io["xn2_bf"][s * T:(s + 1) * T, kc * 128:(kc + 1) * 128],
                            group="at_xn2")
                with pb.scope():
                    wv_b = pb.sb(f"at_wv{s}", [128, DC, D], BF16)
                    pb.dma("p", wv_b[:],
                           io["wvr"].rearrange("(kc p) o -> p kc o", p=128),
                           group="at_wv")
                    pb.op("v", lambda e: e.memset(
                        vaug[:, :, 0:NH * 65].rearrange("p t (h w) -> p t h w", w=65)[:, :, :, 64:65], 1.0),
                        writes=[(vaug, "ones")])
                    pb.op("v", lambda e: e.memset(vaug[:, :, NH * 65:], 0.0),
                          writes=[(vaug, "pad")])
                    for tc in range(16):
                        for kc in range(DC):
                            for n in range(2):
                                pb.op("t", lambda e, tc=tc, kc=kc, n=n: e.matmul(
                                    psX[:, n * 512:(n + 1) * 512],
                                    xn2fm[:, kc, tc * 128:(tc + 1) * 128],
                                    wv_b[:, kc, n * 512:(n + 1) * 512],
                                    start=(kc == 0), stop=(kc == DC - 1)),
                                    reads=[(xn2fm, kc), wv_b], writes=[(psX, n)])
                        pb.op("s", lambda e, tc=tc: e.activation(
                            out=vaug[:, tc, 0:NH * 65].rearrange("p (h w) -> p h w", w=65)[:, :, 0:64],
                            in_=psX[:, 0:1024], func=AF.Copy),
                            reads=[(psX, 0), (psX, 1)], writes=[(vaug, tc)])
                pT = pb.sb(f"at_pT{s}", [128, NPT], BF16)
                qh2 = [pb.sb(f"at_q{s}{i}", [128, T], BF16) for i in range(2)]
                kh2 = [pb.sb(f"at_k{s}{i}", [128, T], BF16) for i in range(2)]
                wq2 = [pb.sb(f"at_wq{s}{i}", [128, DC, 128], BF16) for i in range(2)]
                wk2 = [pb.sb(f"at_wk{s}{i}", [128, DC, 128], BF16) for i in range(2)]
                wcv2 = pb.sb(f"at_wcv{s}", [128, 2048], F32)
                wcvb2 = pb.sb(f"at_wcvb{s}", [128, 2048], BF16)
                rest = c.get("wconv_rest_list", [])

                def emit_wconv2(k):
                    for _ in range(k):
                        if not rest:
                            return
                        sap, dap, ct = rest.pop(0)
                        pb.dma("s", wcv2[:, 0:ct], sap, group="at_wcv")
                        pb.op("v", lambda e, ct=ct: e.tensor_copy(
                            out=wcvb2[:, 0:ct], in_=wcv2[:, 0:ct]),
                            reads=[wcv2], writes=[wcvb2])
                        pb.dma("p", dap, wcvb2[:, 0:ct], group="at_wcvo",
                               reads=[wcvb2], writes=[dap])
                def pair_setup(hc2):
                    b = hc2 % 2
                    pb.dma("p", wq2[b][:], wqr3[:, :, hc2 * 128:(hc2 + 1) * 128],
                           group=f"at_wq{b}")
                    pb.dma("p", wk2[b][:], wkr3[:, :, hc2 * 128:(hc2 + 1) * 128],
                           group=f"at_wk{b}")
                    qbank = lambda n: (psQ[:, (n % 2) * 512:(n % 2 + 1) * 512], (psQ, n % 2)) \
                        if n < 2 else (psX[:, (n % 2) * 512:(n % 2 + 1) * 512], (psX, n % 2))
                    for (w_b, dst), mx in (((wq2[b], qh2[b]), mxq), ((wk2[b], kh2[b]), mxk)):
                        for kc in range(DC):
                            for n in range(4):
                                ap, tag = qbank(n)
                                pb.op("t", lambda e, w_b=w_b, kc=kc, n=n, ap=ap: e.matmul(
                                    ap, w_b[:, kc, :],
                                    xn2fm[:, kc, n * 512:(n + 1) * 512],
                                    start=(kc == 0), stop=(kc == DC - 1)),
                                    reads=[w_b, (xn2fm, kc)], writes=[tag])
                        for n in range(4):
                            ap, tag = qbank(n)
                            pb.op("s", lambda e, dst=dst, n=n, ap=ap: e.activation(
                                out=dst[:, n * 512:(n + 1) * 512], in_=ap, func=AF.Copy),
                                reads=[tag], writes=[(dst, n)])
                            if n == 0:
                                pb.op("v", lambda e, dst=dst, n=n, mx=mx: e.tensor_reduce(
                                    out=mx[:], in_=dst[:, n * 512:(n + 1) * 512],
                                    axis=AX.X, op=ALU.max, apply_absolute_value=True),
                                    reads=[(dst, n)], writes=[mx])
                            else:
                                pb.op("v", lambda e, dst=dst, n=n: e.tensor_reduce(
                                    out=rmt[:], in_=dst[:, n * 512:(n + 1) * 512],
                                    axis=AX.X, op=ALU.max, apply_absolute_value=True),
                                    reads=[(dst, n)], writes=[rmt])
                                pb.op("v", lambda e, mx=mx: e.tensor_max(
                                    out=mx[:], in0=mx[:], in1=rmt[:]),
                                    reads=[mx, rmt], writes=[mx])
                    for qi, mx in enumerate((mxq, mxk)):
                        pb.op("v", lambda e, mx=mx: e.tensor_tensor(out=mx2[:], in0=mx[:],
                                                                    in1=mx[:], op=ALU.mult),
                              reads=[mx], writes=[mx2])
                        for hh in range(2):
                            pb.op("t", lambda e, hh=hh: e.matmul(
                                psX[0:1, 0:1], onescol[hh * 64:(hh + 1) * 64, :],
                                mx2[hh * 64:(hh + 1) * 64, :], start=True, stop=True),
                                reads=[onescol, mx2], writes=[(psX, 0)])
                            pb.op("v", lambda e, qi=qi, hh=hh: e.tensor_copy(
                                out=nrm[0:1, qi * 2 + hh:qi * 2 + hh + 1], in_=psX[0:1, 0:1]),
                                reads=[(psX, 0)], writes=[(nrm, qi * 2 + hh)])
                    nr = [(nrm, i) for i in range(4)]
                    pb.op("v", lambda e: e.tensor_tensor(out=u3[:], in0=nrm[0:1, 0:2],
                                                         in1=nrm[0:1, 2:4], op=ALU.mult),
                          reads=nr, writes=[u3])
                    pb.op("v", lambda e: e.tensor_scalar_max(out=u3[:], in0=u3[:],
                                                             scalar1=1e-30),
                          reads=[u3], writes=[u3])
                    pb.op("s", lambda e: e.activation(out=u3[:], in_=u3[:], func=AF.Ln),
                          reads=[u3], writes=[u3])
                    pb.op("s", lambda e: e.activation(out=u3[:], in_=u3[:], func=AF.Exp,
                                                      scale=0.5),
                          reads=[u3], writes=[u3])
                    pb.op("v", lambda e: e.scalar_tensor_tensor(
                        out=u3[:], in0=u3[:], scalar=0.1275,
                        in1=gbm[0:1, 0:1].broadcast_to([1, 2]), op0=ALU.mult, op1=ALU.add),
                        reads=[u3, gbm], writes=[u3])
                    pb.op("v", lambda e, hc2=hc2: e.tensor_copy(
                        out=rel0p[:], in_=srow3[0:1, 2 * hc2:2 * hc2 + 2, 0]),
                        reads=[srow], writes=[rel0p])
                    pb.op("v", lambda e: e.tensor_tensor(
                        out=upair[0:1, 0:2], in0=rel0p[:], in1=u3[:], op=ALU.subtract),
                        reads=[rel0p, u3], writes=[(upair, 0)])
                    pb.op("t", lambda e: e.matmul(psX[:, 0:2], ones[:], upair[0:1, 0:2],
                                                  start=True, stop=True),
                          reads=[ones, (upair, 0)], writes=[(psX, 0)])
                    pb.op("v", lambda e, b=b: e.tensor_copy(out=bU[b][:], in_=psX[:, 0:2]),
                          reads=[(psX, 0)], writes=[bU[b]])

                def do_head(hc2, hh):
                    b = hc2 % 2
                    h = hc2 * 2 + hh
                    qh_ap = qh2[b][hh * 64:(hh + 1) * 64, :]
                    kh_ap = kh2[b][hh * 64:(hh + 1) * 64, :]
                    qreads = [(qh2[b], n) for n in range(4)]
                    kreads = [(kh2[b], n) for n in range(4)]
                    ci = 0
                    for kc in range(16):
                        N = T - 128 * kc
                        offp = STRIP_OFF[kc]
                        for r0 in range(0, N, 1024):
                            reg = ci % 2
                            ci += 1
                            w = min(1024, N - r0)
                            for n0 in range(0, w, 512):
                                wn = min(512, w - n0)
                                pb.op("t", lambda e, kc=kc, r0=r0, n0=n0, wn=wn, reg=reg,
                                      qh_ap=qh_ap, kh_ap=kh_ap: e.matmul(
                                    psB4[:, reg * 1024 + n0:reg * 1024 + n0 + wn],
                                    kh_ap[:, kc * 128:(kc + 1) * 128],
                                    qh_ap[:, kc * 128 + r0 + n0:kc * 128 + r0 + n0 + wn],
                                    start=True, stop=True),
                                    reads=qreads + kreads,
                                    writes=[(psB4, f"b{reg * 2 + n0 // 512}")])
                            sreads = [(psB4, f"b{reg * 2 + n0 // 512}") for n0 in range(0, w, 512)]
                            if r0 == 0:
                                nw = 128 if kc == 15 else 256
                                pb.op("v", lambda e, reg=reg, h=h, nw=nw: e.tensor_tensor(
                                    out=psB4[:, reg * 1024:reg * 1024 + nw],
                                    in0=psB4[:, reg * 1024:reg * 1024 + nw],
                                    in1=btiles[:, 2 * h:2 * h + 2, :].rearrange(
                                        "p b j -> p (b j)")[:, 0:nw], op=ALU.add),
                                    reads=sreads[0:1] + [(btiles, h)],
                                    writes=[(psB4, f"b{reg * 2}")])
                            pb.op("s", lambda e, offp=offp, r0=r0, w=w, reg=reg, b=b,
                                  hh=hh: e.activation(
                                out=pT[:, offp + r0:offp + r0 + w],
                                in_=psB4[:, reg * 1024:reg * 1024 + w], func=AF.Exp,
                                scale=SCALE_ATT, bias=bU[b][:, hh:hh + 1]),
                                reads=sreads + [bU[b]], writes=[(pT, kc)])
                    for kc in range(16):
                        for bk in range(max(kc // 4, 0), 4):
                            b0 = bk * 512
                            b1 = b0 + 512
                            s0 = max(b0, kc * 128)
                            last_kc = min(15, (b1 - 1) // 128)
                            pb.op("t", lambda e, kc=kc, s0=s0, b1=b1, h=h,
                                  last_kc=last_kc: e.matmul(
                                psB4[:, s0:b1],
                                vaug[:, kc, h * 65:h * 65 + 128],
                                pT[:, STRIP_OFF[kc] + s0 - kc * 128:
                                    STRIP_OFF[kc] + b1 - kc * 128],
                                start=(kc == 0), stop=(kc == last_kc)),
                                reads=[(vaug, kc), (vaug, "ones"), (vaug, "pad"), (pT, kc)],
                                writes=[(psB4, f"b{bk}")])
                    avr = [(psB4, f"b{bk}") for bk in range(4)]
                    pb.op("s", lambda e, hh=hh, hc2=hc2: e.activation(
                        out=o_fm[hh * 64:(hh + 1) * 64, hc2, :],
                        in_=psB4[0:64, :], func=AF.Copy),
                        reads=avr, writes=[(o_fm, f"{hh}.{hc2}")])
                    pb.op("v", lambda e: e.tensor_scalar_max(
                        out=linv[:], in0=psB4[64:65, :], scalar1=1e-30),
                        reads=avr, writes=[(linv, 0), (linv, 1)])
                    pb.op("v", lambda e: e.reciprocal(out=linv2[:], in_=linv[:]),
                          reads=[(linv, 0), (linv, 1)], writes=[linv2])
                    for n4 in range(4):
                        pb.op("t", lambda e, n4=n4: e.matmul(
                            psX[0:64, (n4 % 2) * 512:(n4 % 2 + 1) * 512], ones[0:1, 0:64],
                            linv2[0:1, n4 * 512:(n4 + 1) * 512],
                            start=True, stop=True),
                            reads=[ones, linv2], writes=[(psX, n4 % 2)])
                        pb.op("v", lambda e, hh=hh, hc2=hc2, n4=n4: e.tensor_tensor(
                            out=o_fm[hh * 64:(hh + 1) * 64, hc2,
                                     n4 * 512:(n4 + 1) * 512],
                            in0=o_fm[hh * 64:(hh + 1) * 64, hc2,
                                     n4 * 512:(n4 + 1) * 512],
                            in1=psX[0:64, (n4 % 2) * 512:(n4 % 2 + 1) * 512], op=ALU.mult),
                            reads=[(psX, n4 % 2), (o_fm, f"{hh}.{hc2}")],
                            writes=[(o_fm, f"{hh}.{hc2}")])

                pair_setup(0)
                for hc2 in range(8):
                    do_head(hc2, 0)
                    emit_wconv2(2)
                    if hc2 + 1 < 8:
                        pair_setup(hc2 + 1)
                    do_head(hc2, 1)
                    emit_wconv2(2)
                if s == NSEQ - 1:
                    emit_wconv2(len(rest))
            with pb.scope():
                wp_b = pb.sb(f"at_wp{s}", [128, DC, D], BF16)
                pb.dma("p", wp_b[:], io["wp"].rearrange("(kc p) o -> p kc o", p=128),
                       group="at_wp")
                for tc in range(16):
                    b = tc % 2
                    for kc in range(DC):
                        for n in range(2):
                            pb.op("t", lambda e, tc=tc, kc=kc, n=n: e.matmul(
                                psX[:, n * 512:(n + 1) * 512],
                                o_fm[:, kc, tc * 128:(tc + 1) * 128],
                                wp_b[:, kc, n * 512:(n + 1) * 512],
                                start=(kc == 0), stop=(kc == DC - 1)),
                                reads=[(o_fm, f"0.{kc}"), (o_fm, f"1.{kc}"), wp_b],
                                writes=[(psX, n)])
                    pb.op("v", lambda e, b=b: e.tensor_tensor(
                        out=ss2[b][:], in0=psX[:, 0:1024], in1=bp_b[:], op=ALU.add),
                        reads=[(psX, 0), (psX, 1), bp_b], writes=[ss2[b]])
                    pb.dma("p", io["s2_f32"][(s * 16 + tc) * 128:(s * 16 + tc + 1) * 128, :],
                           ss2[b][:], group=f"s2_f32{b}")


LAUNCH_DEFS["attn"] = (
    ["xn2_bf", "wqr", "wkr", "wvr", "wp", "bp_r", "rt_row", "btiles"],
    ["s2_f32"], [], emit_attn)


def emit_ffn1(pb, c):
    io = c["io"]
    with pb.scope():
        scl = {}
        for nm in ("s1c", "s2c", "sh1c", "sh2c"):
            scl[nm] = pb.sb("f1_" + nm, [128, HC], F32)
            pb.dma("q", scl[nm][:], io[nm], group="f1_" + nm)
        for s in range(NSEQ):
            with pb.scope():
                h_fm = pb.sb(f"f1_h{s}", [128, HC, T], BF16)
                with pb.scope():
                    psT1 = pb.ps(f"f1_psT1{s}", [128, T], F32)
                    psT2 = pb.ps(f"f1_psT2{s}", [128, T], F32)
                    xn2 = pb.sb(f"f1_xn2{s}", [128, DC, T], BF16)
                    w1s = [pb.sb(f"f1_w1s{s}{i}", [128, DC, 128], BF16) for i in range(2)]
                    w2s = [pb.sb(f"f1_w2s{s}{i}", [128, DC, 128], BF16) for i in range(2)]
                    sa = pb.sb(f"f1_sa{s}", [128, T], F32)
                    w1r = io["w1_bf"].rearrange("m kc k o -> m k kc o")
                    w2r = io["w2_bf"].rearrange("m kc k o -> m k kc o")
                    for kc in range(DC):
                        pb.dmat("q", xn2[:, kc, :],
                                io["xn2_bf"][s * T:(s + 1) * T,
                                             kc * 128:(kc + 1) * 128],
                                group="f1_xn2")
                    for m in range(HC):
                        b = m % 2
                        pb.dma("q", w1s[b][:], w1r[m], group=f"f1_w1{b}")
                        pb.dma("q", w2s[b][:], w2r[m], group=f"f1_w2{b}")
                        for kc in range(DC):
                            for n in range(4):
                                pb.op("t", lambda e, kc=kc, n=n, b=b: e.matmul(
                                    psT1[:, n * 512:(n + 1) * 512], w1s[b][:, kc, :],
                                    xn2[:, kc, n * 512:(n + 1) * 512],
                                    start=(kc == 0), stop=(kc == DC - 1)),
                                    reads=[w1s[b], (xn2, kc)], writes=[(psT1, n)])
                        for kc in range(DC):
                            for n in range(4):
                                pb.op("t", lambda e, kc=kc, n=n, b=b: e.matmul(
                                    psT2[:, n * 512:(n + 1) * 512], w2s[b][:, kc, :],
                                    xn2[:, kc, n * 512:(n + 1) * 512],
                                    start=(kc == 0), stop=(kc == DC - 1)),
                                    reads=[w2s[b], (xn2, kc)], writes=[(psT2, n)])
                        p1 = [(psT1, n) for n in range(4)]
                        p2 = [(psT2, n) for n in range(4)]
                        pb.op("s", lambda e, m=m: e.activation(
                            out=sa[:], in_=psT1[:], func=AF.Silu,
                            scale=scl["s1c"][:, m:m + 1]),
                            reads=p1 + [scl["s1c"]], writes=[sa])
                        pb.op("v", lambda e, m=m: e.scalar_tensor_tensor(
                            out=h_fm[:, m, :], in0=psT2[:],
                            scalar=scl["s2c"][:, m:m + 1], in1=sa[:],
                            op0=ALU.mult, op1=ALU.mult),
                            reads=p2 + [scl["s2c"], sa], writes=[(h_fm, m)])
                psU = [pb.ps(f"f1_psU{s}{i}", [128, T], F32) for i in range(2)]
                for ui, (wt, sc) in enumerate((("h1_bf", "sh1c"), ("h2_bf", "sh2c"))):
                    stU = [pb.sb(f"f1_u{ui}s{s}{i}", [128, HC, 128], BF16) for i in range(3)]
                    gout = [pb.sb(f"f1_g{ui}o{s}{i}", [128, T], BF16) for i in range(2)]
                    wr = io[wt].rearrange("m kc k o -> m k kc o")
                    for m in range(HC):
                        b = m % 3
                        bp2 = m % 2
                        pb.dma("q", stU[b][:], wr[m], group=f"f1_u{ui}w{b}")
                        for kc in range(HC):
                            for n in range(4):
                                pb.op("t", lambda e, kc=kc, n=n, b=b, bp2=bp2: e.matmul(
                                    psU[bp2][:, n * 512:(n + 1) * 512], stU[b][:, kc, :],
                                    h_fm[:, kc, n * 512:(n + 1) * 512],
                                    start=(kc == 0), stop=(kc == HC - 1)),
                                    reads=[stU[b], h_fm], writes=[(psU[bp2], n)])
                        pu = [(psU[bp2], n) for n in range(4)]
                        pb.op("s", lambda e, m=m, bp2=bp2, sc=sc: e.activation(
                            out=gout[bp2][:], in_=psU[bp2][:], func=AF.Gelu,
                            scale=scl[sc][:, m:m + 1]),
                            reads=pu + [scl[sc]], writes=[gout[bp2]])
                        pb.dma("p", io["gx_dram"][m], gout[bp2][:], group=f"f1_gx{bp2}")
                    for m in range(HC):
                        pb.dma("q", h_fm[:, m, :], io["gx_dram"][m],
                               group="f1_hre", reads=[("gx_dram", m)],
                               writes=[(h_fm, m)])
                    if ui == 0:
                        continue
            with pb.scope():
                w3sb = pb.sb(f"f1_w3{s}", [128, HC, D], BF16)
                w3r = io["w3_bf"].rearrange("(kc) k o -> k kc o")
                for q in range(4):
                    pb.dma("q", w3sb[:, q * 8:(q + 1) * 8, :], w3r[:, q * 8:(q + 1) * 8, :],
                           group="f1_w3")
                psW = [pb.ps(f"f1_psW{s}{i}", [128, 1024], F32) for i in range(2)]
                c["_psb"] = psW[0]
                s3b = bcast_row(pb, c, f"s3b{s}", io["s3_r"][:], D, F32)
                for kc in range(HC):
                    pb.op("v", lambda e, kc=kc: e.tensor_tensor(
                        out=w3sb[:, kc, :], in0=w3sb[:, kc, :], in1=s3b[:], op=ALU.mult),
                        reads=[(w3sb, kc), s3b], writes=[(w3sb, kc)])
                gl = [pb.sb(f"f1_gl{s}{i}", [128, HC, 128], BF16) for i in range(2)]
                sgf = [pb.sb(f"f1_sgf{s}{i}", [128, D], F32) for i in range(2)]
                sx1 = [pb.sb(f"f1_sx1{s}{i}", [128, D], F32) for i in range(2)]
                ss2 = [pb.sb(f"f1_ss2{s}{i}", [128, D], F32) for i in range(2)]
                gxr = io["gx_dram"]
                for tc in range(16):
                    b = tc % 2
                    pb.dma("q", gl[b][:], gxr[:, :, tc * 128:(tc + 1) * 128].rearrange(
                        "kc k t -> k kc t"), group=f"f1_gl{b}", reads=[("gx_dram", "")])
                    for kc in range(HC):
                        for n in range(2):
                            pb.op("t", lambda e, kc=kc, n=n, b=b: e.matmul(
                                psW[b][:, n * 512:(n + 1) * 512], gl[b][:, kc, :],
                                w3sb[:, kc, n * 512:(n + 1) * 512],
                                start=(kc == 0), stop=(kc == HC - 1)),
                                reads=[gl[b], (w3sb, kc)], writes=[(psW[b], n)])
                    pw = [(psW[b], n) for n in range(2)]
                    pb.op("s", lambda e, b=b: e.activation(out=sgf[b][:], in_=psW[b][:],
                                                           func=AF.Silu),
                          reads=pw, writes=[sgf[b]])
                    i = s * 16 + tc
                    pb.dma("q", sx1[b][:], io["x1_f32"][i * 128:(i + 1) * 128, :],
                           group=f"f1_x1{b}")
                    pb.dma("q", ss2[b][:], io["s2_f32"][i * 128:(i + 1) * 128, :],
                           group=f"f1_s2{b}")
                    pb.op("v", lambda e, b=b: e.tensor_tensor(
                        out=sgf[b][:], in0=sgf[b][:], in1=ss2[b][:], op=ALU.mult),
                        reads=[sgf[b], ss2[b]], writes=[sgf[b]])
                    pb.op("v", lambda e, b=b: e.tensor_tensor(
                        out=sx1[b][:], in0=sx1[b][:], in1=sgf[b][:], op=ALU.add),
                        reads=[sx1[b], sgf[b]], writes=[sx1[b]])
                    pb.dma("p", io["out"][i * 128:(i + 1) * 128, :], sx1[b][:],
                           group=f"out{b}")


LAUNCH_DEFS["ffn1"] = (
    ["xn2_bf", "w1_bf", "w2_bf", "h1_bf", "h2_bf", "w3_bf", "s1c", "s2c",
     "sh1c", "sh2c", "s3_r", "x1_f32", "s2_f32"],
    ["out"], ["gx_dram"], emit_ffn1)


from concourse.bass_utils import run_bass_kernel_spmd

N_CORES = 8
LAUNCH_ORDER = ["premix", "ffn0", "attn", "ffn1"]


def build_fused():
    nc = bass.Bass("TRN2", target_bir_lowering=False)
    pb = PB(nc)
    io = {}
    for nm, (shape, dt) in SHARED_SPECS.items():
        io[nm] = pb.dram(nm, shape, dt, kind="ExternalInput")
    io["x"] = pb.dram("x", list(SCRATCH_SPECS["x"][0]), F32, kind="ExternalInput")
    io["out"] = pb.dram("out", list(SCRATCH_SPECS["out"][0]), F32,
                        kind="ExternalOutput")
    for nm, (shape, dt) in SCRATCH_SPECS.items():
        if nm in ("x", "out"):
            continue
        io[nm] = pb.dram(nm, shape, dt, kind="Internal")
    c = {"io": io, "fused": True}
    for v in (EPS, 3 * LN2, -3 * LN2, 1.0, 0.0):
        pb.cpp(v)
    ones = pb._stack.enter_context(pb.nc.sbuf_tensor("ones1_f32", [1, 128], F32))
    pb.op("v", lambda e: e.memset(ones[:], 1.0), writes=[ones])
    c["_ones1_f32"] = ones
    for name in LAUNCH_ORDER:
        LAUNCH_DEFS[name][3](pb, c)
    pb.finish()
    return nc, pb


def kernel(**inputs):
    shared = host_prep_shared(inputs)
    cores = [host_prep_core(inputs, ci) for ci in range(N_CORES)]
    nc, pb = build_fused()
    in_maps = [dict(shared, **cores[ci]) for ci in range(N_CORES)]
    res = run_bass_kernel_spmd(nc, in_maps, core_ids=list(range(N_CORES)))
    out = np.concatenate(
        [np.asarray(res.results[ci]["out"], np.float32).reshape(2, T, D)
         for ci in range(N_CORES)], axis=0)
    pb.close()
    return out
